# version 10
# speedup vs baseline: 1.4882x; 1.1420x over previous
import numpy as np
import ml_dtypes

import concourse.bass as bass
import concourse.mybir as mybir
from concourse.bass_utils import run_bass_kernel_spmd

F32 = mybir.dt.float32
BF16 = mybir.dt.bfloat16
I32 = mybir.dt.int32
AF = mybir.ActivationFunctionType
ALU = mybir.AluOpType
AX = mybir.AxisListType.X

NC = 8
B = 2
S = 8192
D = 1024
NH = 16
HD = 64
TLOC = (B * S) // NC
NT = TLOC // 128
BF = np.dtype(ml_dtypes.bfloat16)



def host_ident128():
    return np.eye(128, dtype=BF)


def host_band_masks():
    m = np.zeros((4, 128, 512), dtype=np.float32)
    for r in range(4):
        for k in range(128):
            m[r, k, 128 * r + k:] = 1.0
    return np.ascontiguousarray(m.astype(BF))


def host_rope_tables(positions, d=32, theta=10000.0):
    inv = theta ** (-(np.arange(0, d, 2, dtype=np.float64)) / d)
    ang = positions[:, None].astype(np.float64) * inv[None, :]
    c = np.cos(ang).T
    s = np.sin(ang).T
    T = positions.shape[0]
    cos = np.empty((d, T), np.float64)
    sin = np.empty((d, T), np.float64)
    cos[0::2] = c
    cos[1::2] = c
    sin[0::2] = -s
    sin[1::2] = s
    return np.ascontiguousarray(cos.astype(np.float32)), np.ascontiguousarray(sin.astype(np.float32))


def swap_cols_pairwise(w):
    w2 = np.empty_like(w)
    w2[:, 0::2] = w[:, 1::2]
    w2[:, 1::2] = w[:, 0::2]
    return w2


def f32c(a):
    return np.ascontiguousarray(a, dtype=np.float32)




class Sem:
    def __init__(self, handle):
        self.h = handle
        self.n = 0

    def inc(self, instr, k=1):
        instr.then_inc(self.h, k)
        self.n += k
        return instr


class Prog:
    def __init__(self, name):
        self.nc = bass.Bass("TRN2", target_bir_lowering=False)
        self.name = name
        self.ctx = []
        self._sems = {}
        self._uid = 0

    def uid(self):
        self._uid += 1
        return self._uid

    def dram_in(self, name, shape, dtype=F32):
        return self.nc.dram_tensor(name, list(shape), dtype, kind="ExternalInput")

    def dram_out(self, name, shape, dtype=F32):
        return self.nc.dram_tensor(name, list(shape), dtype, kind="ExternalOutput")

    def sbuf(self, name, shape, dtype):
        h = self.nc.sbuf_tensor(name, list(shape), dtype)
        t = h.__enter__()
        self.ctx.append((name, h))
        return t

    def make_arena(self, nbytes):
        self._arena = self.sbuf(f"arena{self.uid()}", [128, nbytes], mybir.dt.uint8)
        self._arena_addr = self.nc.lookup_mloc(self._arena).addr
        return self._arena

    def sbuf_at(self, name, shape, dtype, off):
        return self.nc.alloc_sbuf_tensor_at(name, list(shape), dtype,
                                            offset=self._arena_addr + off)

    def psum(self, name, shape, dtype=F32):
        h = self.nc.psum_tensor(name, list(shape), dtype)
        t = h.__enter__()
        self.ctx.append((name, h))
        return t

    def constcol(self, val, rows=128):
        key = (float(val), rows)
        if not hasattr(self, "_consts"):
            self._consts = {}
        if key not in self._consts:
            t = self.sbuf(f"const{self.uid()}", [rows, 1], F32)
            with self.nc.Block() as block:
                @block.vector
                def _(v):
                    v.memset(t[:, :], float(val))
            self._consts[key] = t
        return self._consts[key][:, :]

    def sem(self, name):
        if name not in self._sems:
            h = self.nc.semaphore(name)
            s = h.__enter__()
            self.ctx.append((name, h))
            self._sems[name] = Sem(s)
        return self._sems[name]


def mm_grid(p: Prog, jobs, psum_banks, wait_sems=(), tag=None, nstage=0,
            stage_free_extra=0, gp_loads=()):
    mm = p.sem("mm_shared")
    fr = p.sem("fr_shared")
    st = p.sem("st_shared")
    nbank = len(psum_banks)
    mm_base, fr_base, st_base = mm.n, fr.n, st.n
    has_v = any(j["engine"] == "vector" for j in jobs)
    has_s = any(j["engine"] == "scalar" for j in jobs)
    assert not (has_v and has_s), "mm_grid: use one evac engine per grid"
    dma_jobs = [i for i, j in enumerate(jobs) if "dma" in j]

    with p.nc.Block() as block:
        @block.tensor
        def _(tensor):
            for s, cnt in wait_sems:
                tensor.wait_ge(s.h, cnt)
            for i, job in enumerate(jobs):
                if "wait" in job:
                    ws, wc = job["wait"]
                    tensor.wait_ge(ws.h, wc)
                if i >= nbank:
                    tensor.wait_ge(fr.h, fr_base + i - nbank + 1)
                pa = psum_banks[i % nbank]
                rows = job.get("rows", 128)
                pb = job.get("pbase", 0)
                w = job["out_w"]
                ents = job["mms"]
                n_m = sum(1 for e in ents if e[0] == "m")
                mi = 0
                last = None
                for e in ents:
                    if e[0] == "m":
                        _, lhsT, rhs = e
                        last = tensor.matmul(pa[pb:pb + rows, 0:w], lhsT, rhs,
                                             start=(mi == 0), stop=(mi == n_m - 1))
                        mi += 1
                    else:
                        _, src, ident = e
                        last = tensor.transpose(pa[pb:pb + rows, 0:w], src, ident)
                mm.inc(last, 1)

        def section(eng):
            ndma = 0
            for i, job in enumerate(jobs):
                eng.wait_ge(mm.h, mm_base + i + 1)
                if nstage and "dma" in job:
                    k = dma_jobs.index(i)
                    if k >= nstage - stage_free_extra:
                        eng.wait_ge(st.h, st_base + 16 * (k - nstage + 1 + stage_free_extra))
                pa = psum_banks[i % nbank]
                pb = job.get("pbase", 0)
                last = job["evac"](eng, pa[pb:pb + job.get("rows", 128), 0:job["out_w"]])
                fr.inc(last, 1)

        if has_v:
            @block.vector
            def _(vector):
                section(vector)
        if has_s:
            @block.scalar
            def _(scalar):
                section(scalar)

        if dma_jobs:
            @block.sync
            def _(sync):
                for k, i in enumerate(dma_jobs):
                    sync.wait_ge(fr.h, fr_base + i + 1)
                    dram_ap, stage_ap = jobs[i]["dma"]
                    st.inc(sync.dma_start(out=dram_ap, in_=stage_ap), 16)
                sync.wait_ge(st.h, st.n)

        if gp_loads:
            dg = p.sem("dma_gp")

            @block.gpsimd
            def _(g):
                for cnt, prs in gp_loads:
                    if cnt:
                        g.wait_ge(fr.h, cnt)
                    for dst, src in prs:
                        dg.inc(g.dma_start(out=dst, in_=src), 16)


def dma_block(p: Prog, pairs, queue="sync", tag=None, wait=True):
    s = p.sem("dma_shared" if queue == "sync" else "dma_gp")

    with p.nc.Block() as block:
        def body(e):
            for dst, src in pairs:
                s.inc(e.dma_start(out=dst, in_=src), 16)
            if wait:
                e.wait_ge(s.h, s.n)
        if queue == "sync":
            @block.sync
            def _(e):
                body(e)
        else:
            @block.gpsimd
            def _(e):
                body(e)
    return s


def chunkT(buf, c, t0, w, ntok=TLOC):
    return buf[:, c * ntok + t0: c * ntok + t0 + w]


def emit_to_xT(p: Prog, xb_sb, xT_sb, ident, psA, psB, n_tiles=NT, cols=D,
               ntok=None):
    banks = [psA] if psA is psB else [psA, psB]
    """PE-transpose token-major bf16 tiles into chunk-major feature-major buf."""
    if ntok is None:
        ntok = n_tiles * 128
    jobs = []
    nch = cols // 128
    for c in range(nch):
        for t in range(n_tiles):
            def mk(c=c, t=t):
                def evac(eng, pa):
                    return eng.tensor_copy(
                        xT_sb[:, c * ntok + t * 128: c * ntok + (t + 1) * 128], pa)
                return evac
            jobs.append(dict(
                mms=[("t", xb_sb[:, t * cols + c * 128: t * cols + (c + 1) * 128],
                      ident[:, :])],
                out_w=128, engine="vector", evac=mk()))
    mm_grid(p, jobs, banks, tag=f"xT{p.uid()}")


def emit_ln(p: Prog, x_sb, n_tiles=NT, cols=D, eps=1e-5):
    u = p.uid()
    st = p.sbuf(f"lnst{u}", [128, 12], F32)
    mv = p.sbuf(f"lnmv{u}", [128, 8], F32)
    t4 = p.sbuf(f"lnt4{u}", [128, 4], F32)
    epsb = p.constcol(eps)
    sv = p.sem("lnsv")
    ss = p.sem("lnss")
    fl = p.sem("lnfl")
    sv_b, ss_b = sv.n, ss.n
    assert cols == 1024
    with p.nc.Block() as block:
        @block.vector
        def _(v):
            def flush(ins):
                fl.inc(ins, 1)
                v.wait_ge(fl.h, fl.n)
            for t in range(n_tiles):
                xt = x_sb[:, t * cols:(t + 1) * cols]
                v.bn_stats(st[:, 0:6], xt[:, 0:512])
                flush(v.bn_stats(st[:, 6:12], xt[:, 512:1024]))
                means = bass.AP(st, 1, [[12, 128], [3, 4]])
                m2s = bass.AP(st, 2, [[12, 128], [3, 4]])
                v.reduce_sum(mv[:, 0:1], means, axis=AX)
                v.reduce_sum(mv[:, 1:2], m2s, axis=AX)
                flush(v.tensor_mul(t4[:, 0:4], means, means))
                v.tensor_scalar_mul(mv[:, 4:5], mv[:, 0:1], 0.25)
                v.tensor_scalar_mul(mv[:, 5:6], mv[:, 0:1], -0.25)
                flush(v.reduce_sum(mv[:, 2:3], t4[:, 0:4], axis=AX))
                v.tensor_mul(t4[:, 0:1], mv[:, 4:5], mv[:, 4:5])
                flush(v.scalar_tensor_tensor(mv[:, 3:4], mv[:, 2:3], 256.0,
                                             mv[:, 1:2], ALU.mult, ALU.add))
                sv.inc(v.scalar_tensor_tensor(mv[:, 6:7], mv[:, 3:4], 1.0 / 1024.0,
                                              t4[:, 0:1], ALU.mult, ALU.subtract), 1)
                v.wait_ge(ss.h, ss_b + t + 1)
                v.tensor_scalar(xt, xt, mv[:, 5:6], mv[:, 7:8], ALU.add, ALU.mult)
        @block.scalar
        def _(sc):
            for t in range(n_tiles):
                sc.wait_ge(sv.h, sv_b + t + 1)
                sc.activation(mv[:, 6:7], mv[:, 6:7], AF.Ln, bias=epsb)
                ss.inc(sc.activation(mv[:, 7:8], mv[:, 6:7], AF.Exp, scale=-0.5), 1)


def load_w_bf16(p: Prog, pairs):
    dma_block(p, pairs, queue="gpsimd")




def build_T0A():
    p = Prog("T0A")
    nc = p.nc
    x_in = p.dram_in("x", [TLOC, D]).ap()
    wdq = p.dram_in("wdq", [D, 256]).ap()
    wdkv = p.dram_in("wdkv", [D, 256]).ap()
    wkr = p.dram_in("wkr", [D, 32]).ap()
    wkrs = p.dram_in("wkrs", [D, 32]).ap()
    wuq = p.dram_in("wuq", [256, D]).ap()
    wqr = p.dram_in("wqr", [256, 512]).ap()
    wqrs = p.dram_in("wqrs", [256, 512]).ap()
    wuk = p.dram_in("wuk", [256, D]).ap()
    wuv = p.dram_in("wuv", [256, D]).ap()
    identi = p.dram_in("ident", [128, 128], BF16).ap()
    ropec = p.dram_in("ropec", [128, TLOC]).ap()
    ropes = p.dram_in("ropes", [128, TLOC]).ap()

    o_qc = p.dram_out("qc", [D, TLOC], BF16).ap()
    o_qr = p.dram_out("qrr", [512, TLOC], BF16).ap()
    o_kc = p.dram_out("kc", [D, TLOC], BF16).ap()
    o_kr = p.dram_out("krr", [32, TLOC], BF16).ap()
    o_v = p.dram_out("v", [D, TLOC], BF16).ap()

    xb = p.sbuf("xb", [128, NT * D], BF16)
    xT = p.sbuf("xT", [128, 8 * TLOC], BF16)
    cq = p.sbuf("cq", [128, 2 * TLOC], BF16)
    ckv = p.sbuf("ckv", [128, 2 * TLOC], BF16)
    w1 = p.sbuf("w1", [128, 8 * 256], BF16)
    w2 = p.sbuf("w2", [128, 8 * 256], BF16)
    w3 = p.sbuf("w3", [128, 8 * 64], BF16)
    w4 = p.sbuf("w4", [128, 2 * D], BF16)
    w5 = p.sbuf("w5", [128, 2 * 1024], BF16)
    w6 = p.sbuf("w6", [128, 2 * D], BF16)
    w7 = p.sbuf("w7", [128, 2 * D], BF16)
    ident = p.sbuf("id", [128, 128], BF16)
    rc = p.sbuf("rc", [128, TLOC], F32)
    rs = p.sbuf("rs", [128, TLOC], F32)
    stg = p.sbuf("stg", [128, 6 * 512], BF16)
    tq = p.sbuf("tq", [128, 2 * 512], BF16)
    tk = p.sbuf("tk", [32, 3 * 512], BF16)

    psums = [p.psum(f"ps{i}", [128, 512]) for i in range(4)]
    psb = [p.psum(f"psb{i}", [128, 512], BF16) for i in range(2)]

    xsrc = x_in.rearrange("(t p) c -> p t c", p=128)
    xdst = xb.ap().rearrange("p (t c) -> p t c", t=NT)
    dma_block(p, [(ident[:, :], identi), (rc[:, :], ropec), (rs[:, :], ropes)])
    dma_block(p, [(xdst, xsrc)], queue="gpsimd")
    wpairs = []
    for c in range(8):
        wpairs.append((w1[:, c * 256:(c + 1) * 256], wdq[c * 128:(c + 1) * 128, :]))
        wpairs.append((w2[:, c * 256:(c + 1) * 256], wdkv[c * 128:(c + 1) * 128, :]))
        wpairs.append((w3[:, c * 64:c * 64 + 32], wkr[c * 128:(c + 1) * 128, :]))
        wpairs.append((w3[:, c * 64 + 32:c * 64 + 64], wkrs[c * 128:(c + 1) * 128, :]))
    for c in range(2):
        wpairs.append((w4[:, c * D:(c + 1) * D], wuq[c * 128:(c + 1) * 128, :]))
        wpairs.append((w5[:, c * 1024:c * 1024 + 512], wqr[c * 128:(c + 1) * 128, :]))
        wpairs.append((w5[:, c * 1024 + 512:(c + 1) * 1024], wqrs[c * 128:(c + 1) * 128, :]))
        wpairs.append((w6[:, c * D:(c + 1) * D], wuk[c * 128:(c + 1) * 128, :]))
        wpairs.append((w7[:, c * D:(c + 1) * D], wuv[c * 128:(c + 1) * 128, :]))
    load_w_bf16(p, wpairs)

    emit_to_xT(p, xb, xT, ident, psb[0], psb[1])

    jobs = []
    for dst, wsb in ((cq, w1), (ckv, w2)):
        for f in range(2):
            for m in range(4):
                def mk(dst=dst, f=f, m=m):
                    def evac(eng, pa):
                        return eng.tensor_copy(chunkT(dst, f, m * 512, 512), pa)
                    return evac
                jobs.append(dict(
                    mms=[("m", wsb[:, c * 256 + f * 128:(c * 256) + (f + 1) * 128],
                          chunkT(xT, c, m * 512, 512)) for c in range(8)],
                    out_w=512, engine="vector", evac=mk()))
    mm_grid(p, jobs, psums, tag="lat")

    krtmp = p.sbuf("krtmp", [32, 4 * 512], BF16)
    jobs = []
    for m in range(4):
        def mkA(m=m):
            def evac(eng, pa):
                return eng.tensor_copy(krtmp[:, m * 512:(m + 1) * 512], pa)
            return evac
        jobs.append(dict(
            mms=[("m", w3[:, c * 64:c * 64 + 32], chunkT(xT, c, m * 512, 512))
                 for c in range(8)],
            out_w=512, rows=32, engine="vector", evac=mkA()))
    mm_grid(p, jobs, psums[:2], tag="kra")
    jobs = []
    for m in range(4):
        def mkB(m=m):
            def evac(eng, pa):
                eng.tensor_tensor(tk[:, 0:512], pa, rs[0:32, m * 512:(m + 1) * 512], ALU.mult)
                eng.tensor_mul(tk[:, 512:1024], krtmp[:, m * 512:(m + 1) * 512],
                               rc[0:32, m * 512:(m + 1) * 512])
                return eng.tensor_add(tk[:, 1024:1536], tk[:, 0:512], tk[:, 512:1024])
            return evac
        jobs.append(dict(
            mms=[("m", w3[:, c * 64 + 32:c * 64 + 64], chunkT(xT, c, m * 512, 512))
                 for c in range(8)],
            out_w=512, rows=32, engine="vector", evac=mkB(),
            dma=(o_kr[:, m * 512:(m + 1) * 512], tk[:, 1024:1536])))
    mm_grid(p, jobs, psums[:2], tag="krb", nstage=1)

    jobs = []
    ji = 0
    for dst, wsb, srcb in ((o_qc, w4, cq), (o_kc, w6, ckv), (o_v, w7, ckv)):
        for f in range(8):
            for m in range(4):
                sl = ji % 6
                ji += 1
                def mk(sl=sl):
                    def evac(eng, pa):
                        return eng.tensor_copy(stg[:, sl * 512:(sl + 1) * 512], pa)
                    return evac
                jobs.append(dict(
                    mms=[("m", wsb[:, c * D + f * 128:c * D + (f + 1) * 128],
                          chunkT(srcb, c, m * 512, 512)) for c in range(2)],
                    out_w=512, engine="vector", evac=mk(),
                    dma=(dst[f * 128:(f + 1) * 128, m * 512:(m + 1) * 512],
                         stg[:, sl * 512:(sl + 1) * 512])))
    mm_grid(p, jobs, psums, tag="kv", nstage=6)

    qrtmp = p.sbuf("qrtmp", [128, 4 * 512], BF16)
    for ch in range(4):
        jobs = []
        for m in range(4):
            def mkA(ch=ch, m=m):
                def evac(eng, pa):
                    return eng.tensor_copy(qrtmp[:, m * 512:(m + 1) * 512], pa)
                return evac
            jobs.append(dict(
                mms=[("m", w5[:, c * 1024 + ch * 128:c * 1024 + (ch + 1) * 128],
                      chunkT(cq, c, m * 512, 512)) for c in range(2)],
                out_w=512, engine="vector", evac=mkA()))
        mm_grid(p, jobs, psums[:2], tag=f"qra{ch}")
        jobs = []
        for m in range(4):
            def mkB(ch=ch, m=m):
                def evac(eng, pa):
                    eng.tensor_tensor(tq[:, 0:512], pa,
                                      rs[:, m * 512:(m + 1) * 512], ALU.mult)
                    eng.tensor_tensor(tq[:, 512:1024], qrtmp[:, m * 512:(m + 1) * 512],
                                      rc[:, m * 512:(m + 1) * 512], ALU.mult)
                    return eng.tensor_add(tq[:, 0:512], tq[:, 0:512], tq[:, 512:1024])
                return evac
            jobs.append(dict(
                mms=[("m", w5[:, c * 1024 + 512 + ch * 128:c * 1024 + 512 + (ch + 1) * 128],
                      chunkT(cq, c, m * 512, 512)) for c in range(2)],
                out_w=512, engine="vector", evac=mkB(),
                dma=(o_qr[ch * 128:(ch + 1) * 128, m * 512:(m + 1) * 512],
                     tq[:, 0:512])))
        mm_grid(p, jobs, psums[:2], tag=f"qrb{ch}", nstage=1)

    return p



_PROGS = {}


def get_prog(name, builder):
    if name not in _PROGS:
        _PROGS[name] = builder()
    return _PROGS[name]


def run_launch(name, builder, in_maps):
    p = get_prog(name, builder)
    return run_bass_kernel_spmd(p.nc, in_maps, core_ids=list(range(NC)))


def prep_T0A(inputs):
    x = inputs["L0_attn_x"].reshape(B * S, D)
    wdq = f32c(inputs["L0_attn_W_DQ"])
    wdkv = f32c(inputs["L0_attn_W_DKV"])
    wkr = f32c(inputs["L0_attn_W_KR"])
    wkrs = f32c(swap_cols_pairwise(wkr))
    wuq = f32c(inputs["L0_attn_W_UQ"])
    wqr = f32c(inputs["L0_attn_W_QR"])
    wqrs = f32c(swap_cols_pairwise(wqr))
    wuk = f32c(inputs["L0_attn_W_UK"])
    wuv = f32c(inputs["L0_attn_W_UV"])
    ident = host_ident128()
    maps = []
    for c in range(NC):
        t0 = c * TLOC
        pos = (np.arange(t0, t0 + TLOC) % S)
        rc, rs = host_rope_tables(pos)
        rc = np.ascontiguousarray(np.tile(rc, (4, 1)))
        rs = np.ascontiguousarray(np.tile(rs, (4, 1)))
        maps.append(dict(x=f32c(x[t0:t0 + TLOC]), wdq=wdq, wdkv=wdkv, wkr=wkr,
                         wkrs=wkrs, wuq=wuq, wqr=wqr, wqrs=wqrs, wuk=wuk,
                         wuv=wuv, ident=ident, ropec=rc, ropes=rs))
    return maps


def run_launch(name, builder, maps):
    p = get_prog(name, builder)
    return run_bass_kernel_spmd(p.nc, maps, core_ids=list(range(NC)))


def kernel(**inputs):
    st = {}

    res = run_launch("T0A", build_T0A, prep_T0A(inputs))
    for k in ("qc", "qrr", "kc", "krr", "v"):
        st[k] = np.concatenate([res.results[c][k] for c in range(NC)], axis=1)

    res = run_launch("F0", build_F0, prep_F0(st))
    st["ao_units"] = [res.results[c]["ao"] for c in range(NC)]

    res = run_launch("T0B", build_T0B, prep_T0B(inputs, st))
    st["x2"] = [res.results[c]["x2"] for c in range(NC)]
    st["qt"] = [res.results[c]["qt"] for c in range(NC)]
    st["ktm"] = [res.results[c]["ktm"] for c in range(NC)]
    st["vtm"] = [res.results[c]["vtm"] for c in range(NC)]
    st["totals"] = [res.results[c]["totals"] for c in range(NC)]

    res = run_launch("T1", build_T1, prep_T1(inputs, st))
    st["x3"] = [res.results[c]["x3"] for c in range(NC)]
    st["qh"] = np.concatenate([res.results[c]["qh"] for c in range(NC)], axis=1)
    st["kh"] = np.concatenate([res.results[c]["kh"] for c in range(NC)], axis=1)
    st["vh"] = np.concatenate([res.results[c]["vh"] for c in range(NC)], axis=1)
    st["kss"] = np.concatenate([res.results[c]["kss"] for c in range(NC)], axis=1)

    res = run_launch("F1", build_F1, prep_F1(st))
    st["ao1_units"] = [res.results[c]["ao"] for c in range(NC)]

    res = run_launch("T2", build_T2, prep_T2(inputs, st))
    st["x4"] = [res.results[c]["x4"] for c in range(NC)]
    st["q2"] = [res.results[c]["q2"] for c in range(NC)]
    st["k2"] = [res.results[c]["k2"] for c in range(NC)]
    st["v2"] = [res.results[c]["v2"] for c in range(NC)]
    st["qss2"] = [res.results[c]["qss2"] for c in range(NC)]
    st["kss2"] = [res.results[c]["kss2"] for c in range(NC)]

    res = run_launch("T3", build_T3, prep_T3(inputs, st))
    st["x5"] = [res.results[c]["x5"] for c in range(NC)]

    res = run_launch("T4", build_T4, prep_T4(inputs, st))
    st["x6"] = [res.results[c]["x6"] for c in range(NC)]
    st["q3"] = np.concatenate([res.results[c]["q3"] for c in range(NC)], axis=1)
    st["k3"] = np.concatenate([res.results[c]["k3"] for c in range(NC)], axis=1)
    st["v3"] = np.concatenate([res.results[c]["v3"] for c in range(NC)], axis=1)
    st["qss3"] = np.concatenate([res.results[c]["qss3"] for c in range(NC)], axis=1)
    st["kss3"] = np.concatenate([res.results[c]["kss3"] for c in range(NC)], axis=1)

    res = run_launch("F3", build_F3, prep_F3(st))
    st["ao3_units"] = [res.results[c]["ao"] for c in range(NC)]

    res = run_launch("T5", build_T5, prep_T5(inputs, st))
    x8 = np.concatenate([res.results[c]["x8"] for c in range(NC)], axis=0)
    return np.ascontiguousarray(x8.reshape(B, S, D), dtype=np.float32)



TAU0 = float(np.sqrt(np.float64(96.0)))


def emit_flash_unit(p: Prog, u, *, dq, q_sb, k_sb, vhat_sb, msk, ones64,
                    exp_scale, exp_bias, clip, ao_out, pt, ps_s, ps_o, ps_b,
                    stage, C_bc=None):
    sc = p.sem("f_sc")
    ex = p.sem("f_ex")
    pr = p.sem("f_pr")
    pv = p.sem("f_pv")
    orr = p.sem("f_or")
    e1 = p.sem("f_e1")
    bc = p.sem("f_bc")
    ar = p.sem("f_ar")
    st = p.sem("f_st")

    iters = [(m, K) for m in range(16) for K in range(4 * m + 4)]
    n_it = len(iters)
    base = dict(sc=sc.n, ex=ex.n, pr=pr.n, pv=pv.n, orr=orr.n,
                e1=e1.n, bc=bc.n, ar=ar.n, st=st.n)

    def pv_cum(m):
        return base["pv"] + 2 * (m + 1) * (m + 2)

    with p.nc.Block() as block:
        @block.tensor
        def _(tensor):
            def emit_bc(j):
                tensor.wait_ge(e1.h, base["e1"] + j + 1)
                if j >= 1:
                    tensor.wait_ge(ar.h, base["ar"] + j)
                rd = stage["rd"][0:1, 512 * (j % 2):512 * (j % 2 + 1)]
                bc.inc(tensor.matmul(ps_b[0:64, 0:512], ones64[0:1, 0:64], rd,
                                     start=True, stop=True), 1)

            def emit_pv(nit):
                pm, pK = iters[nit]
                tensor.wait_ge(pr.h, base["pr"] + nit + 1)
                if pK == 0 and pm >= 2:
                    tensor.wait_ge(orr.h, base["orr"] + pm - 1)
                pv.inc(tensor.matmul(
                    ps_o[pm % 2][0:65, 0:512],
                    vhat_sb[:, 65 * pK:65 * pK + 65],
                    pt[nit % 4][:, 0:512],
                    start=(pK == 0), stop=(pK == 4 * pm + 3)), 1)

            nit = 0
            for m in range(16):
                if m >= 2:
                    emit_bc(m - 2)
                for K in range(4 * m + 4):
                    if nit >= 2:
                        tensor.wait_ge(ex.h, base["ex"] + nit - 1)
                    sc.inc(tensor.matmul(
                        ps_s[nit % 2][:, 0:512],
                        k_sb[0:dq, 128 * K:128 * (K + 1)],
                        q_sb[0:dq, 512 * m:512 * (m + 1)],
                        start=True, stop=True), 1)
                    if nit >= 1:
                        emit_pv(nit - 1)
                    nit += 1
            emit_pv(n_it - 1)
            emit_bc(14)
            emit_bc(15)

        @block.scalar
        def _(scalar):
            def epilogue(j):
                scalar.wait_ge(pv.h, pv_cum(j))
                rf = stage["rf"][0:1, 0:512]
                rd = stage["rd"][0:1, 512 * (j % 2):512 * (j % 2 + 1)]
                scalar.activation(rf, ps_o[j % 2][64:65, 0:512], AF.Ln)
                if j >= 2:
                    scalar.wait_ge(bc.h, base["bc"] + j - 1)
                e1.inc(scalar.activation(rd, rf, AF.Exp, scale=-1.0), 1)
                if j >= 2:
                    scalar.wait_ge(ar.h, base["ar"] + j - 1)
                nm = stage["num"][:, 512 * (j % 2):512 * (j % 2 + 1)]
                orr.inc(scalar.copy(nm[0:64, 0:512], ps_o[j % 2][0:64, 0:512]), 1)

            nit = 0
            for m in range(16):
                for K in range(4 * m + 4):
                    scalar.wait_ge(sc.h, base["sc"] + nit + 1)
                    if nit >= 4:
                        scalar.wait_ge(pv.h, base["pv"] + nit - 3)
                    bias_ap = exp_bias if C_bc is None else C_bc(m)
                    ex.inc(scalar.activation(pt[nit % 4][:, 0:512],
                                             ps_s[nit % 2][:, 0:512], AF.Exp,
                                             bias=bias_ap, scale=float(exp_scale)), 1)
                    nit += 1
                if m >= 1:
                    epilogue(m - 1)
            epilogue(15)

        @block.vector
        def _(vector):
            def emit_tt(j):
                vector.wait_ge(bc.h, base["bc"] + j + 1)
                vector.wait_ge(orr.h, base["orr"] + j + 1)
                if j >= 2:
                    vector.wait_ge(st.h, base["st"] + 16 * (j - 1))
                ao = stage["ao"][:, 512 * (j % 2):512 * (j % 2 + 1)]
                nm = stage["num"][:, 512 * (j % 2):512 * (j % 2 + 1)]
                ar.inc(vector.tensor_tensor(ao[0:64, 0:512], nm[0:64, 0:512],
                                            ps_b[0:64, 0:512], ALU.mult), 1)

            nit = 0
            for m in range(16):
                for K in range(4 * m + 4):
                    vector.wait_ge(ex.h, base["ex"] + nit + 1)
                    t = pt[nit % 4][:, 0:512]
                    last = None
                    if clip is not None:
                        last = vector.tensor_scalar(t, t, float(clip[1]), float(clip[0]),
                                                    ALU.min, ALU.max)
                    if K >= 4 * m:
                        r = K - 4 * m
                        last = vector.tensor_tensor(t, t, msk[:, 512 * r:512 * (r + 1)],
                                                    ALU.mult)
                    if last is None:
                        last = vector.engine_nop()
                    pr.inc(last, 1)
                    nit += 1
                if m >= 2:
                    emit_tt(m - 2)
            emit_tt(14)
            emit_tt(15)

        @block.sync
        def _(sync):
            for m in range(16):
                sync.wait_ge(ar.h, base["ar"] + m + 1)
                ao = stage["ao"][:, 512 * (m % 2):512 * (m % 2 + 1)]
                st.inc(sync.dma_start(out=ao_out(m), in_=ao[0:64, 0:512]), 16)
            sync.wait_ge(st.h, st.n)


def emit_flash_pair(p: Prog, US, *, msk, ones64, exp_scale, clip, wait_lds=()):
    iters = [(m, K) for m in range(16) for K in range(4 * m + 4)]
    n_it = len(iters)
    for U in US:
        U["base"] = {k: s.n for k, s in U["sems"].items()}

    def B(U, k):
        return U["base"][k]

    def SS(U, k):
        return U["sems"][k]

    def pv_cum(U, m):
        return B(U, "pv") + 2 * (m + 1) * (m + 2)

    with p.nc.Block() as block:
        @block.tensor
        def _(tensor):
            for s, cnt in wait_lds:
                tensor.wait_ge(s.h, cnt)

            def emit_bc(U, j):
                tensor.wait_ge(SS(U, "e1").h, B(U, "e1") + j + 1)
                if j >= 1:
                    tensor.wait_ge(SS(U, "ar").h, B(U, "ar") + j)
                rd = U["rd"][0:1, 512 * (j % 2):512 * (j % 2 + 1)]
                SS(U, "bc").inc(
                    tensor.matmul(U["ps_b"][0:64, 0:512], ones64[0:1, 0:64], rd,
                                  start=True, stop=True), 1)

            def emit_pv(U, nit):
                pm, pK = iters[nit]
                tensor.wait_ge(SS(U, "pr").h, B(U, "pr") + nit + 1)
                if pK == 0 and pm >= 1:
                    tensor.wait_ge(SS(U, "orr").h, B(U, "orr") + pm)
                SS(U, "pv").inc(tensor.matmul(
                    U["ps_o"][0:65, 0:512],
                    U["vhat"][:, 65 * pK:65 * pK + 65],
                    U["pt"][nit % 4][:, 0:512],
                    start=(pK == 0), stop=(pK == 4 * pm + 3)), 1)

            nit = 0
            for m in range(16):
                for U in US:
                    if m >= 2:
                        emit_bc(U, m - 2)
                for K in range(4 * m + 4):
                    for U in US:
                        if nit >= 2:
                            tensor.wait_ge(SS(U, "ex").h, B(U, "ex") + nit - 1)
                        dq = U["dq"]
                        SS(U, "sc").inc(tensor.matmul(
                            U["ps_s"][nit % 2][:, 0:512],
                            U["k"][0:dq, 128 * K:128 * (K + 1)],
                            U["q"][0:dq, 512 * m:512 * (m + 1)],
                            start=True, stop=True), 1)
                        if nit >= 1:
                            emit_pv(U, nit - 1)
                    nit += 1
            for U in US:
                emit_pv(U, n_it - 1)
                emit_bc(U, 14)
                emit_bc(U, 15)

        @block.scalar
        def _(scalar):
            def epilogue(U, j):
                scalar.wait_ge(SS(U, "pv").h, pv_cum(U, j))
                rf = U["rf"][0:1, 0:512]
                rd = U["rd"][0:1, 512 * (j % 2):512 * (j % 2 + 1)]
                scalar.activation(rf, U["ps_o"][64:65, 0:512], AF.Ln)
                if j >= 2:
                    scalar.wait_ge(SS(U, "bc").h, B(U, "bc") + j - 1)
                SS(U, "e1").inc(scalar.activation(rd, rf, AF.Exp, scale=-1.0), 1)
                if j >= 2:
                    scalar.wait_ge(SS(U, "ar").h, B(U, "ar") + j - 1)
                nm = U["num"][:, 512 * (j % 2):512 * (j % 2 + 1)]
                SS(U, "orr").inc(
                    scalar.copy(nm[0:64, 0:512], U["ps_o"][0:64, 0:512]), 1)

            nit = 0
            for m in range(16):
                for K in range(4 * m + 4):
                    for U in US:
                        scalar.wait_ge(SS(U, "sc").h, B(U, "sc") + nit + 1)
                        if nit >= 4:
                            scalar.wait_ge(SS(U, "pv").h, B(U, "pv") + nit - 3)
                        bias_ap = U["bias"](m) if callable(U["bias"]) else U["bias"]
                        SS(U, "ex").inc(scalar.activation(
                            U["pt"][nit % 4][:, 0:512],
                            U["ps_s"][nit % 2][:, 0:512], AF.Exp,
                            bias=bias_ap, scale=float(exp_scale)), 1)
                    if K == 0 and m >= 1:
                        for U in US:
                            epilogue(U, m - 1)
                    nit += 1
            for U in US:
                epilogue(U, 15)

        @block.vector
        def _(vector):
            def emit_tt(U, j):
                vector.wait_ge(SS(U, "bc").h, B(U, "bc") + j + 1)
                vector.wait_ge(SS(U, "orr").h, B(U, "orr") + j + 1)
                if j >= 2:
                    vector.wait_ge(SS(U, "st").h, B(U, "st") + 16 * (j - 1))
                ao = U["ao"][:, 512 * (j % 2):512 * (j % 2 + 1)]
                nm = U["num"][:, 512 * (j % 2):512 * (j % 2 + 1)]
                SS(U, "ar").inc(vector.tensor_tensor(
                    ao[0:64, 0:512], nm[0:64, 0:512],
                    U["ps_b"][0:64, 0:512], ALU.mult), 1)

            nit = 0
            for m in range(16):
                for K in range(4 * m + 4):
                    for U in US:
                        vector.wait_ge(SS(U, "ex").h, B(U, "ex") + nit + 1)
                        t = U["pt"][nit % 4][:, 0:512]
                        last = None
                        if clip is not None:
                            last = vector.tensor_scalar(
                                t, t, float(clip[1]), float(clip[0]),
                                ALU.min, ALU.max)
                        if K >= 4 * m:
                            r = K - 4 * m
                            last = vector.tensor_tensor(
                                t, t, msk[:, 512 * r:512 * (r + 1)], ALU.mult)
                        if last is None:
                            last = vector.engine_nop()
                        SS(U, "pr").inc(last, 1)
                    nit += 1
                if m >= 2:
                    for U in US:
                        emit_tt(U, m - 2)
            for U in US:
                emit_tt(U, 14)
                emit_tt(U, 15)

        @block.sync
        def _(sync):
            for m in range(16):
                for U in US:
                    sync.wait_ge(SS(U, "ar").h, B(U, "ar") + m + 1)
                    ao = U["ao"][:, 512 * (m % 2):512 * (m % 2 + 1)]
                    SS(U, "st").inc(
                        sync.dma_start(out=U["ao_out"](m), in_=ao[0:64, 0:512]), 16)
            for U in US:
                sync.wait_ge(SS(U, "st").h, SS(U, "st").n)


def flash_sems(p, t):
    return {k: p.sem(f"f_{k}{t}")
            for k in ("sc", "ex", "pr", "pv", "orr", "e1", "bc", "ar", "st")}


def build_F0():
    p = Prog("F0")
    dq = 96
    q_in = p.dram_in("q", [2, 2, 96, S], BF16).ap()
    k_in = p.dram_in("k", [2, 2, 96, S], BF16).ap()
    v_in = p.dram_in("v", [2, 2, S, 64], BF16).ap()
    m_in = p.dram_in("masks", [4, 128, 512], BF16).ap()
    ao_o = p.dram_out("ao", [2, 2, 64, S], BF16).ap()

    q_sb = [p.sbuf(f"q{i}", [96, S], BF16) for i in range(4)]
    k_sb = [p.sbuf(f"k{i}", [96, S], BF16) for i in range(4)]
    vh_sb = [p.sbuf(f"vh{i}", [128, 64 * 65], BF16) for i in range(4)]
    msk = p.sbuf("msk", [128, 4 * 512], BF16)
    ones64 = p.sbuf("ones64", [1, 64], BF16)
    pt = [p.sbuf(f"pt{i}", [128, 512], BF16) for i in range(8)]
    stg = [dict(rd=p.sbuf(f"rd{t}", [1, 2 * 512], BF16),
                rf=p.sbuf(f"rf{t}", [1, 512], F32),
                num=p.sbuf(f"num{t}", [64, 2 * 512], BF16),
                ao=p.sbuf(f"aos{t}", [64, 2 * 512], BF16)) for t in range(2)]
    ps_s = [[p.psum(f"pss{t}{i}", [128, 512]) for i in range(2)] for t in range(2)]
    ps_o = [p.psum(f"pso{t}", [65, 512]) for t in range(2)]
    ps_b = [p.psum(f"psb{t}", [64, 512]) for t in range(2)]

    units = [(b, h) for b in range(2) for h in range(2)]
    lds = [p.sem("fldA"), p.sem("fldB")]
    ms = p.sem("fms")
    dma_block(p, [(msk[:, 512 * r:512 * (r + 1)], m_in[r]) for r in range(4)])
    bias0 = p.sbuf("bias0", [128, 1], F32)
    with p.nc.Block() as block:
        @block.vector
        def _(v):
            v.memset(ones64[:, :], 1.0)
            v.memset(bias0[:, :], -80.0 / TAU0)
            for u in range(4):
                ms.inc(v.memset(vh_sb[u][:, :], 1.0), 1)

        @block.sync
        def _(sync):
            sync.wait_ge(ms.h, 4)
            for u in (0, 1):
                b, h = units[u]
                lds[0].inc(sync.dma_start(out=q_sb[u][:, :], in_=q_in[b, h]), 16)
                lds[0].inc(sync.dma_start(out=k_sb[u][:, :], in_=k_in[b, h]), 16)
                dst = vh_sb[u].ap().rearrange("p (k c) -> p k c", c=65)[:, :, 0:64]
                src = v_in[b, h].rearrange("(k p) c -> p k c", p=128)
                lds[0].inc(sync.dma_start(out=dst, in_=src), 16)

        @block.gpsimd
        def _(g):
            g.wait_ge(ms.h, 4)
            for u in (2, 3):
                b, h = units[u]
                lds[1].inc(g.dma_start(out=q_sb[u][:, :], in_=q_in[b, h]), 16)
                lds[1].inc(g.dma_start(out=k_sb[u][:, :], in_=k_in[b, h]), 16)
                dst = vh_sb[u].ap().rearrange("p (k c) -> p k c", c=65)[:, :, 0:64]
                src = v_in[b, h].rearrange("(k p) c -> p k c", p=128)
                lds[1].inc(g.dma_start(out=dst, in_=src), 16)

    for pi in range(2):
        US = []
        for t in range(2):
            u = 2 * pi + t
            b, h = units[u]
            US.append(dict(
                dq=dq, q=q_sb[u], k=k_sb[u], vhat=vh_sb[u],
                pt=pt[4 * t:4 * t + 4], ps_s=ps_s[t], ps_o=ps_o[t],
                ps_b=ps_b[t], bias=bias0[:, :],
                ao_out=lambda m, b=b, h=h: ao_o[b, h, :, 512 * m:512 * (m + 1)],
                sems=flash_sems(p, t), **stg[t]))
        emit_flash_pair(p, US, msk=msk, ones64=ones64, exp_scale=1.0 / TAU0,
                        clip=(float(np.exp(-160.0 / TAU0)), 1.0),
                        wait_lds=[(lds[pi], 96)])
    return p


def prep_F0(st):
    maps = []
    masks = host_band_masks()
    ident = host_ident128()
    qc, qr, kc, kr, v = st["qc"], st["qrr"], st["kc"], st["krr"], st["v"]
    for c in range(NC):
        q = np.zeros((2, 2, 96, S), BF)
        k = np.zeros((2, 2, 96, S), BF)
        vv = np.zeros((2, 2, S, 64), BF)
        for bi in range(2):
            for hl in range(2):
                h = 2 * c + hl if 2 * c + hl < 16 else 2 * c + hl - 16
                q[bi, hl, 0:64] = qc[64 * h:64 * h + 64, bi * S:(bi + 1) * S]
                q[bi, hl, 64:96] = qr[32 * h:32 * h + 32, bi * S:(bi + 1) * S]
                k[bi, hl, 0:64] = kc[64 * h:64 * h + 64, bi * S:(bi + 1) * S]
                k[bi, hl, 64:96] = kr[:, bi * S:(bi + 1) * S]
                vv[bi, hl] = v[64 * h:64 * h + 64, bi * S:(bi + 1) * S].T
        maps.append(dict(q=q, k=k, v=vv, masks=masks))
    return maps




def emit_load_x(p: Prog, x_in_ap, x_sb):
    xsrc = x_in_ap.rearrange("(t p) c -> p t c", p=128)
    xdst = x_sb.ap().rearrange("p (t c) -> p t c", t=NT)
    dma_block(p, [(xdst, xsrc)])


def emit_store_x(p: Prog, x_sb, x_out_ap):
    xsrc = x_sb.ap().rearrange("p (t c) -> p t c", t=NT)
    xdst = x_out_ap.rearrange("(t p) c -> p t c", p=128)
    dma_block(p, [(xdst, xsrc)])


def emit_bf16_tiles(p: Prog, x_sb, xb_sb, scale_col=None, n_tiles=NT, cols=D):
    with p.nc.Block() as block:
        @block.scalar
        def _(sc):
            for t in range(n_tiles):
                src = x_sb[:, t * cols:(t + 1) * cols]
                dst = xb_sb[:, t * cols:(t + 1) * cols]
                if scale_col is None:
                    sc.copy(dst, src)
                else:
                    sc.activation(dst, src, AF.Copy, scale=scale_col(t))


def emit_addT_into_x(p: Prog, yT_sb, x_sb, ident, psb16, skip_chunks=(),
                     n_tiles=NT):
    jobs = []
    for t in range(n_tiles):
        for c in range(8):
            if c in skip_chunks:
                continue
            def mk(t=t, c=c):
                def evac(eng, pa):
                    xs = x_sb[:, t * D + c * 128:t * D + (c + 1) * 128]
                    return eng.tensor_add(xs, xs, pa)
                return evac
            jobs.append(dict(mms=[("t", chunkT(yT_sb, c, t * 128, 128), ident[:, :])],
                             out_w=128, engine="vector", evac=mk()))
    mm_grid(p, jobs, psb16, tag=f"addT{p.uid()}")


def emit_fm_proj(p: Prog, w_sb, wchunks, src_sb, dst_evac, psums, n_free=512,
                 n_macros=4, fchunks=8, tag=None, engine="scalar"):
    jobs = []
    for f in range(fchunks):
        for m in range(n_macros):
            jobs.append(dict(
                mms=[("m", wchunks(c, f), chunkT(src_sb, c, m * n_free, n_free))
                     for c in range(8)],
                out_w=n_free, engine=engine, evac=dst_evac(f, m)))
    mm_grid(p, jobs, psums, tag=tag)


def emit_gate_topk(p: Prog, lgT_sb, comb_sb, n_exp, renorm_eps, psb16, ident,
                   n_tiles=NT):
    u = p.uid()
    lg = p.sbuf(f"gt_l{u}", [128, n_exp], F32)
    sc1 = p.sbuf(f"gt_s{u}", [128, 8], F32)
    sv = p.sem("gt_v")
    ss = p.sem("gt_s")
    tr = p.sem("gt_t")
    fl = p.sem("gt_f")
    sv_b, ss_b, tr_b = sv.n, ss.n, tr.n
    E = n_exp
    with p.nc.Block() as block:
        @block.tensor
        def _(tensor):
            for t in range(n_tiles):
                if t >= 1:
                    tensor.wait_ge(sv.h, sv_b + t)
                tr.inc(tensor.transpose(psb16[0][0:128, 0:E],
                                        lgT_sb[0:E, t * 128:(t + 1) * 128],
                                        ident[0:E, 0:E]), 1)
        @block.vector
        def _(v):
            def flush(ins):
                fl.inc(ins, 1)
                v.wait_ge(fl.h, fl.n)
            for t in range(n_tiles):
                cb = comb_sb[:, t * E:(t + 1) * E]
                v.wait_ge(tr.h, tr_b + t + 1)
                flush(v.tensor_copy(lg[:, 0:E], psb16[0][0:128, 0:E]))
                flush(v.reduce_max(sc1[:, 0:1], lg[:, 0:E], axis=AX))
                v.tensor_scalar_mul(sc1[:, 1:2], sc1[:, 0:1], -1.0)
                flush(v.tensor_single_scalar(cb, lg[:, 0:E], sc1[:, 0:1], ALU.is_ge))
                flush(v.scalar_tensor_tensor(cb, cb, -1e30, lg[:, 0:E],
                                             ALU.mult, ALU.add))
                flush(v.reduce_max(sc1[:, 2:3], cb, axis=AX))
                sv.inc(v.tensor_single_scalar(cb, lg[:, 0:E], sc1[:, 2:3], ALU.is_ge), 1)
                v.wait_ge(ss.h, ss_b + t + 1)
                flush(v.tensor_mul(cb, cb, lg[:, 0:E]))
                flush(v.reduce_sum(sc1[:, 3:4], cb, axis=AX))
                if renorm_eps:
                    flush(v.tensor_scalar_add(sc1[:, 3:4], sc1[:, 3:4],
                                              float(renorm_eps)))
                flush(v.reciprocal(sc1[:, 4:5], sc1[:, 3:4]))
                v.tensor_single_scalar(cb, cb, sc1[:, 4:5], ALU.mult)
        @block.scalar
        def _(s):
            for t in range(n_tiles):
                s.wait_ge(sv.h, sv_b + t + 1)
                ss.inc(s.activation(lg[:, 0:E], lg[:, 0:E], AF.Exp,
                                    bias=sc1[:, 1:2]), 1)




def build_T0B():
    p = Prog("T0B")
    x_in = p.dram_in("x", [TLOC, D]).ap()
    ao_in = p.dram_in("aoT", [D, TLOC], BF16).ap()
    wo_in = p.dram_in("wo", [D, D]).ap()
    wg_in = p.dram_in("wg", [D, 8]).ap()
    w1_in = p.dram_in("w1", [8, D, 2048]).ap()
    w2_in = p.dram_in("w2", [8, 2048, D]).ap()
    wqkvT_in = p.dram_in("wqkvT", [D, 3072]).ap()
    identi = p.dram_in("ident", [128, 128], BF16).ap()

    x_out = p.dram_out("x2", [TLOC, D]).ap()
    qt_out = p.dram_out("qt", [D, TLOC], BF16).ap()
    ktm_out = p.dram_out("ktm", [TLOC, D], BF16).ap()
    vtm_out = p.dram_out("vtm", [TLOC, D], BF16).ap()
    tot_out = p.dram_out("totals", [64, 16 * 65]).ap()

    x_sb = p.sbuf("xsb", [128, NT * D], F32)
    bufC = p.sbuf("bufC", [128, 8 * TLOC], BF16)
    bufA = p.sbuf("bufA", [128, 16 * 1024], BF16)
    bufB = p.sbuf("bufB", [128, 16 * 1024], BF16)
    bufD = p.sbuf("bufD", [128, 16 * 512], BF16)
    comb = p.sbuf("comb", [128, NT * 8], F32)
    lgT = p.sbuf("lgT", [8, TLOC], BF16)
    stg = p.sbuf("stg", [128, 6 * 512], BF16)
    ident = p.sbuf("id", [128, 128], BF16)
    psums = [p.psum(f"ps{i}", [128, 512]) for i in range(4)]
    psb16 = [p.psum(f"psb{i}", [128, 512], BF16) for i in range(2)]

    emit_load_x(p, x_in, x_sb)
    dma_block(p, [(ident[:, :], identi)])
    dma_block(p, [(bufA[:, c * 2048:(c + 1) * 2048], ao_in[c * 128:(c + 1) * 128, :])
                  for c in range(8)], queue="gpsimd")
    dma_block(p, [(bufB[:, c * 1024:(c + 1) * 1024], wo_in[c * 128:(c + 1) * 128, :])
                  for c in range(8)], queue="gpsimd")

    def oevac(f, m):
        def evac(eng, pa):
            return eng.copy(chunkT(bufC, f, m * 512, 512), pa)
        return evac
    emit_fm_proj(p, None, lambda c, f: bufB[:, c * 2048 + 0:0] if False else
                 bufB[:, c * 1024 + f * 128: c * 1024 + (f + 1) * 128],
                 bufA, oevac, psums, tag="oproj")
    emit_addT_into_x(p, bufC, x_sb, ident, psb16)
    emit_ln(p, x_sb)
    emit_bf16_tiles(p, x_sb, bufA)
    emit_to_xT(p, bufA, bufC, ident, psb16[0], psb16[1])

    wg8 = p.sbuf("wg8", [128, 8 * 8], BF16)
    dma_block(p, [(wg8[:, c * 8:(c + 1) * 8], wg_in[c * 128:(c + 1) * 128, :])
                  for c in range(8)], queue="gpsimd")
    jobs = []
    for m in range(4):
        def mk(m=m):
            def evac(eng, pa):
                return eng.copy(lgT[0:8, m * 512:(m + 1) * 512], pa)
            return evac
        jobs.append(dict(mms=[("m", wg8[:, c * 8:(c + 1) * 8],
                               chunkT(bufC, c, m * 512, 512)) for c in range(8)],
                         out_w=512, rows=8, engine="scalar", evac=mk()))
    mm_grid(p, jobs, psums[:2], tag="gate")
    emit_gate_topk(p, lgT, comb, 8, None, psb16, ident)

    for e in range(8):
        dma_block(p, [(bufA[:, c * 2048:(c + 1) * 2048],
                       w1_in[e, c * 128:(c + 1) * 128, :]) for c in range(8)]
                  + [(bufB[:, c * 1024:(c + 1) * 1024],
                      w2_in[e, c * 128:(c + 1) * 128, :]) for c in range(16)],
                  queue="gpsimd")
        for m in range(4):
            jobs = []
            for f in range(16):
                def mk(f=f):
                    def evac(eng, pa):
                        return eng.activation(bufD[:, f * 512:(f + 1) * 512], pa,
                                              AF.Relu)
                    return evac
                jobs.append(dict(
                    mms=[("m", bufA[:, c * 2048 + f * 128:c * 2048 + (f + 1) * 128],
                          chunkT(bufC, c, m * 512, 512)) for c in range(8)],
                    out_w=512, engine="scalar", evac=mk()))
            mm_grid(p, jobs, psums, tag=f"h{e}_{m}")
            jobs = []
            for tt in range(4):
                for half in range(2):
                    def mk(tt=tt, half=half, m=m, e=e):
                        def evac(eng, pa):
                            tg = 4 * m + tt
                            xs = x_sb[:, tg * D + half * 512: tg * D + (half + 1) * 512]
                            return eng.scalar_tensor_tensor(
                                xs, pa, comb[:, tg * 8 + e:tg * 8 + e + 1], xs,
                                ALU.mult, ALU.add)
                        return evac
                    jobs.append(dict(
                        mms=[("m", bufD[:, fc * 512 + tt * 128:fc * 512 + (tt + 1) * 128],
                              bufB[:, fc * 1024 + half * 512:fc * 1024 + (half + 1) * 512])
                             for fc in range(16)],
                        out_w=512, engine="vector", evac=mk()))
            mm_grid(p, jobs, psums, tag=f"y{e}_{m}")

    emit_ln(p, x_sb)
    emit_store_x(p, x_sb, x_out)
    emit_bf16_tiles(p, x_sb, bufA)
    emit_to_xT(p, bufA, bufC, ident, psb16[0], psb16[1])

    dma_block(p, [(bufA[:, c * 1024:(c + 1) * 1024],
                   wqkvT_in[c * 128:(c + 1) * 128, 0:1024]) for c in range(8)],
              queue="gpsimd")
    jobs = []
    ji = 0
    for f in range(8):
        for m in range(4):
            sl = ji % 6
            ji += 1
            def mk(f=f, m=m, sl=sl):
                def evac(eng, pa):
                    return eng.activation(stg[:, sl * 512:(sl + 1) * 512], pa, AF.Exp)
                return evac
            jobs.append(dict(
                mms=[("m", bufA[:, c * 1024 + f * 128:c * 1024 + (f + 1) * 128],
                      chunkT(bufC, c, m * 512, 512)) for c in range(8)],
                out_w=512, engine="scalar", evac=mk(),
                dma=(qt_out[f * 128:(f + 1) * 128, m * 512:(m + 1) * 512],
                     stg[:, sl * 512:(sl + 1) * 512])))
    mm_grid(p, jobs, psums, tag="l1aq", nstage=6)
    for which, off, out_ap, act in (("k", 1024, ktm_out, AF.Exp),
                                    ("v", 2048, vtm_out, None)):
        dma_block(p, [(bufA[:, c * 1024:(c + 1) * 1024],
                       wqkvT_in[c * 128:(c + 1) * 128, off:off + 1024])
                      for c in range(8)], queue="gpsimd")
        jobs = []
        ji = 0
        for tt in range(16):
            for half in range(2):
                sl = ji % 6
                ji += 1
                def mk(sl=sl, act=act):
                    def evac(eng, pa):
                        if act is None:
                            return eng.copy(stg[:, sl * 512:(sl + 1) * 512], pa)
                        return eng.activation(stg[:, sl * 512:(sl + 1) * 512], pa, act)
                    return evac
                jobs.append(dict(
                    mms=[("m", chunkT(bufC, c, tt * 128, 128),
                          bufA[:, c * 1024 + half * 512:c * 1024 + (half + 1) * 512])
                         for c in range(8)],
                    out_w=512, engine="scalar", evac=mk(),
                    dma=(out_ap[tt * 128:(tt + 1) * 128, half * 512:(half + 1) * 512],
                         stg[:, sl * 512:(sl + 1) * 512])))
        mm_grid(p, jobs, psums, tag=f"l1a{which}", nstage=6)

    onescl = p.sbuf("onescl", [128, 1], BF16)
    totb = p.sbuf("totb", [64, 16 * 65], F32)
    with p.nc.Block() as block:
        @block.vector
        def _(v):
            v.memset(onescl[:, :], 1.0)
    tl = p.sem("t0btl")
    tm_ = p.sem("t0btm")
    te_ = p.sem("t0bte")
    groups0 = [list(range(7)), list(range(7, 14)), [14, 15]]
    with p.nc.Block() as block:
        @block.sync
        def _(sync):
            for tt in range(NT):
                if tt >= 2:
                    sync.wait_ge(tm_.h, 16 * (tt - 1))
                sl = (tt % 2) * 1024
                tl.inc(sync.dma_start(out=bufA[:, sl:sl + 1024],
                                      in_=ktm_out[tt * 128:(tt + 1) * 128, :]), 16)
                tl.inc(sync.dma_start(out=bufB[:, sl:sl + 1024],
                                      in_=vtm_out[tt * 128:(tt + 1) * 128, :]), 16)
        @block.tensor
        def _(tensor):
            for tt in range(NT):
                sl = (tt % 2) * 1024
                tensor.wait_ge(tl.h, 32 * (tt + 1))
                for gi, g in enumerate(groups0):
                    for j, h in enumerate(g):
                        kb = bufA[:, sl + 64 * h:sl + 64 * h + 64]
                        vb = bufB[:, sl + 64 * h:sl + 64 * h + 64]
                        tensor.matmul(psums[gi][0:64, 65 * j:65 * j + 64], kb, vb,
                                      start=(tt == 0), stop=(tt == NT - 1))
                        tm_.inc(tensor.matmul(psums[gi][0:64, 65 * j + 64:65 * j + 65],
                                              kb, onescl[:, :],
                                              start=(tt == 0), stop=(tt == NT - 1)), 1)
        @block.vector
        def _(v):
            v.wait_ge(tm_.h, tm_.n)
            for gi, g in enumerate(groups0):
                for j, h in enumerate(g):
                    te_.inc(v.tensor_copy(totb[:, 65 * h:65 * h + 65],
                                          psums[gi][0:64, 65 * j:65 * j + 65]), 1)
    dma_block(p, [(tot_out, totb[:, :])])
    return p


def prep_T0B(inputs, st):
    x = inputs["L0_attn_x"].reshape(B * S, D)
    wo = f32c(inputs["L0_attn_W_O"])
    wg = f32c(inputs["L0_ffn_Wg"])
    w1 = f32c(inputs["L0_ffn_W1"])
    w2 = f32c(inputs["L0_ffn_W2"])
    wqkvT = f32c(inputs["L1_attn_qkv_w"].T)
    ident = host_ident128()
    ao = np.zeros((D, B * S), BF)
    for c in range(NC):
        a = st["ao_units"][c]
        for bi in range(2):
            for hl in range(2):
                h = 2 * c + hl
                ao[64 * h:64 * h + 64, bi * S:(bi + 1) * S] = a[bi, hl]
    maps = []
    for c in range(NC):
        t0 = c * TLOC
        maps.append(dict(x=f32c(x[t0:t0 + TLOC]), aoT=np.ascontiguousarray(ao[:, t0:t0 + TLOC]),
                         wo=wo, wg=wg, w1=w1, w2=w2, wqkvT=wqkvT, ident=ident))
    return maps



EPS_L1A = 0.001
EPS_RMS = 1.1920929e-07


def emit_rms_cols(p: Prog, x_sb, rcols, eps, n_tiles=NT, cols=D):
    u = p.uid()
    st = p.sbuf(f"rmst{u}", [128, 12], F32)
    mv = p.sbuf(f"rmv{u}", [128, 6], F32)
    t4 = p.sbuf(f"rmt4{u}", [128, 4], F32)
    epsb = p.constcol(eps)
    sv = p.sem("lnsv")
    ss = p.sem("lnss")
    fl = p.sem("lnfl")
    sv_b, ss_b = sv.n, ss.n
    with p.nc.Block() as block:
        @block.vector
        def _(v):
            def flush(ins):
                fl.inc(ins, 1)
                v.wait_ge(fl.h, fl.n)
            for t in range(n_tiles):
                xt = x_sb[:, t * cols:(t + 1) * cols]
                v.bn_stats(st[:, 0:6], xt[:, 0:512])
                flush(v.bn_stats(st[:, 6:12], xt[:, 512:1024]))
                means = bass.AP(st, 1, [[12, 128], [3, 4]])
                m2s = bass.AP(st, 2, [[12, 128], [3, 4]])
                v.reduce_sum(mv[:, 1:2], m2s, axis=AX)
                flush(v.tensor_mul(t4[:, 0:4], means, means))
                flush(v.reduce_sum(mv[:, 2:3], t4[:, 0:4], axis=AX))
                sv.inc(v.scalar_tensor_tensor(mv[:, 3:4], mv[:, 2:3], 256.0,
                                              mv[:, 1:2], ALU.mult, ALU.add), 1)
        @block.scalar
        def _(sc):
            for t in range(n_tiles):
                sc.wait_ge(sv.h, sv_b + t + 1)
                sc.activation(mv[:, 4:5], mv[:, 3:4], AF.Ln, bias=epsb,
                              scale=1.0 / 1024.0)
                sc.activation(rcols[:, t:t + 1], mv[:, 4:5], AF.Exp, scale=-0.5)


def emit_toT_scaled(p: Prog, xb_sb, xT_sb, ident, psA, psB, gain_cols=None,
                    n_tiles=NT, cols=D, ntok=None):
    if ntok is None:
        ntok = n_tiles * 128
    jobs = []
    nch = cols // 128
    for c in range(nch):
        for t in range(n_tiles):
            def mk(c=c, t=t):
                def evac(eng, pa):
                    dst = xT_sb[:, c * ntok + t * 128: c * ntok + (t + 1) * 128]
                    if gain_cols is None:
                        return eng.tensor_copy(dst, pa)
                    return eng.tensor_single_scalar(dst, pa, gain_cols(c), ALU.mult)
                return evac
            jobs.append(dict(
                mms=[("t", xb_sb[:, t * cols + c * 128: t * cols + (c + 1) * 128],
                      ident[:, :])],
                out_w=128, engine="vector", evac=mk()))
    mm_grid(p, jobs, [psA, psB])


def build_T1():
    import os
    PH = int(os.environ.get("T1_PHASES", "9"))
    p = Prog("T1")
    x_in = p.dram_in("x", [TLOC, D]).ap()
    qt_in = p.dram_in("qt", [D, TLOC], BF16).ap()
    ktm_in = p.dram_in("ktm", [TLOC, D], BF16).ap()
    vtm_in = p.dram_in("vtm", [TLOC, D], BF16).ap()
    pt_in = [p.dram_in(f"ptot{i}", [64, 16 * 65]).ap() for i in range(3)]
    pwT_in = p.dram_in("projwT", [D, D]).ap()
    pb_in = p.dram_in("projb", [128, 8]).ap()
    n1_in = p.dram_in("n1w", [128, 8]).ap()
    wqT_in = p.dram_in("wqT", [D, D]).ap()
    wkT_in = p.dram_in("wkT", [D, D]).ap()
    wvT_in = p.dram_in("wvT", [D, D]).ap()
    qn_in = p.dram_in("qn2", [128, 1]).ap()
    kn_in = p.dram_in("kn2", [128, 1]).ap()
    identi = p.dram_in("ident", [128, 128], BF16).ap()

    x_out = p.dram_out("x3", [TLOC, D]).ap()
    tot_out = p.dram_out("totals", [64, 16 * 65]).ap()
    qh_out = p.dram_out("qh", [D, TLOC], BF16).ap()
    kh_out = p.dram_out("kh", [D, TLOC], BF16).ap()
    vh_out = p.dram_out("vh", [D, TLOC], BF16).ap()
    kss_out = p.dram_out("kss", [16, TLOC]).ap()

    p.make_arena(212480)
    x_sb = p.sbuf_at("xsb", [128, NT * D], F32, 0)
    OFF_A, OFF_B, OFF_C, OFF_W = 65536, 98304, 131072, 163840
    qA = p.sbuf_at("qA", [128, 8 * TLOC], BF16, OFF_A)
    yT = p.sbuf_at("yT", [128, 8 * TLOC], BF16, OFF_A)
    qhT = p.sbuf_at("qhT", [128, 8 * TLOC], BF16, OFF_A)
    at1 = p.sbuf_at("at1", [128, NT * D], BF16, OFF_B)
    xtT = p.sbuf_at("xtT", [128, 8 * TLOC], BF16, OFF_B)
    at1T = p.sbuf_at("at1T", [128, 8 * TLOC], BF16, OFF_C)
    khT = p.sbuf_at("khT", [128, 8 * TLOC], BF16, OFF_C)
    wW = p.sbuf_at("wW", [128, 8 * 1024], BF16, OFF_W)
    OFF_KV = OFF_W + 16384
    kt = p.sbuf_at("kt", [128, 2 * 1024], BF16, OFF_KV)
    vt = p.sbuf_at("vt", [128, 2 * 1024], BF16, OFF_KV + 4096)
    OFF_R = OFF_KV + 8192
    runb = p.sbuf_at("runb", [64, 16 * 66], F32, OFF_R)
    ptmp = p.sbuf_at("ptmp", [64, 16 * 65], F32, OFF_R + 4224)
    runbb = p.sbuf_at("runbb", [128, 16 * 66], BF16, OFF_R + 4224 + 4160)
    OFF_M = OFF_R + 4224 + 4160 + 2144
    kodd = p.sbuf_at("kodd", [64, 2 * 1024], BF16, OFF_R + 4224)
    vodd = p.sbuf_at("vodd", [64, 2 * 1024], BF16, OFF_M)
    stg = p.sbuf_at("stg", [128, 4 * 512], BF16, OFF_M)
    sml = p.sbuf_at("sml", [128, 24], F32, OFF_M + 4096)
    onesc = p.sbuf_at("onesc", [128, 1], BF16, OFF_M + 4224)
    bd = p.sbuf_at("bd", [128, 2], BF16, OFF_M + 4256)
    sel = p.sbuf_at("sel", [2, 128], BF16, OFF_M + 4288)
    rstd2 = p.sbuf_at("rstd2", [2, 2 * 512], BF16, OFF_M + 4544)
    sq = p.sbuf_at("sq", [128, 2 * 512], BF16, OFF_M + 6592)
    rc16 = p.sbuf_at("rc16", [128, 16], F32, OFF_M + 8640)
    ksg = p.sbuf_at("ksg", [2, 2 * 512], F32, OFF_M + 8704)
    pb8 = p.sbuf_at("pb8", [128, 8], F32, OFF_M + 12800)
    n1c = p.sbuf_at("n1c", [128, 8], F32, OFF_M + 12832)
    qnc = p.sbuf_at("qnc", [128, 1], F32, OFF_M + 12864)
    knc = p.sbuf_at("knc", [128, 1], F32, OFF_M + 12896)
    qkn = p.sbuf_at("qkn", [128, 1], F32, OFF_M + 12928)
    ident = p.sbuf_at("id", [128, 128], BF16, OFF_M + 12960)

    psums = [p.psum(f"ps{i}", [128, 512]) for i in range(4)]
    psb16 = [p.psum(f"psb{i}", [128, 512], BF16) for i in range(2)]

    dma_block(p, [(ident[:, :], identi), (qnc[:, :], qn_in), (knc[:, :], kn_in)])
    dma_block(p, [(qA[:, c * TLOC:(c + 1) * TLOC], qt_in[c * 128:(c + 1) * 128, :])
                  for c in range(8)], queue="gpsimd")
    dma_block(p, [(pb8[:, :], pb_in), (n1c[:, :], n1_in)], queue="gpsimd")
    with p.nc.Block() as block:
        @block.vector
        def _(v):
            v.memset(onesc[:, :], 1.0)
            v.memset(bd[:, :], 0.0)
            v.memset(bd[0:64, 0:1], 1.0)
            v.memset(bd[64:128, 1:2], 1.0)
            v.memset(runb[:, :], 0.0)
            v.memset(bass.AP(runb, 65, [[16 * 66, 64], [66, 16]]), 1.0)
            v.tensor_mul(qkn[:, :], qnc[:, :], knc[:, :])
    with p.nc.Block() as block:
        @block.tensor
        def _(t):
            t.transpose(psb16[0][0:2, 0:128], bd[:, 0:2], ident[:, :])
        @block.vector
        def _(v):
            pass
    with p.nc.Block() as block:
        @block.vector
        def _(v):
            v.tensor_copy(sel[0:2, :], psb16[0][0:2, 0:128])
    for i in range(3):
        dma_block(p, [(ptmp[:, :], pt_in[i])])
        with p.nc.Block() as block:
            @block.vector
            def _(v):
                dst = bass.AP(runb, 0, [[16 * 66, 64], [66, 16], [1, 65]])
                srcp = bass.AP(ptmp, 0, [[16 * 65, 64], [65, 16], [1, 65]])
                v.tensor_tensor(dst, dst, srcp, ALU.add)

    with p.nc.Block() as block:
        @block.vector
        def _(v):
            v.memset(runbb[:, :], 0.0)
            for h in range(16):
                b = 64 * (h % 2)
                v.tensor_copy(runbb[b:b + 64, 66 * h:66 * h + 66],
                              runb[:, 66 * h:66 * h + 66])

    if PH < 2:
        emit_load_x(p, x_in, x_sb)
        emit_store_x(p, x_sb, x_out)
        return p
    N_U = int(os.environ.get("T1_NU", "32"))
    sc = p.sem("t1sc")
    am = p.sem("t1am")
    cm = p.sem("t1cm")
    ev = p.sem("t1ev")
    eb = p.sem("t1eb")
    fl2 = p.sem("t1fl")
    groups = [list(range(7)), list(range(7, 14)), [14, 15]]
    with p.nc.Block() as block:
        @block.sync
        def _(sync):
            for tt in range((N_U + 1) // 2):
                if tt >= 2:
                    sync.wait_ge(cm.h, 16 * min(2 * tt - 2, N_U))
                sc.inc(sync.dma_start(out=kt[:, (tt % 2) * 1024:(tt % 2 + 1) * 1024],
                                      in_=ktm_in[tt * 128:(tt + 1) * 128, :]), 16)
                sc.inc(sync.dma_start(out=vt[:, (tt % 2) * 1024:(tt % 2 + 1) * 1024],
                                      in_=vtm_in[tt * 128:(tt + 1) * 128, :]), 16)
                sc.inc(sync.dma_start(out=kodd[:, (tt % 2) * 1024:(tt % 2 + 1) * 1024],
                                      in_=ktm_in[tt * 128 + 64:(tt + 1) * 128, :]), 16)
                sc.inc(sync.dma_start(out=vodd[:, (tt % 2) * 1024:(tt % 2 + 1) * 1024],
                                      in_=vtm_in[tt * 128 + 64:(tt + 1) * 128, :]), 16)

        @block.tensor
        def _(tensor):
            nA = 0
            for u in range(N_U):
                tt = u // 2
                tensor.wait_ge(sc.h, 64 * (tt + 1))
                pb = 64 * (u % 2)
                kofs = (tt % 2) * 1024 + pb
                for gi, g in enumerate(groups):
                    bank = (u * 3 + gi) % 2
                    idx = u * 3 + gi
                    if idx >= 2:
                        tensor.wait_ge(ev.h, idx - 1)
                        tensor.wait_ge(eb.h, idx - 1)
                    for j, h in enumerate(g):
                        if os.environ.get("T1_A", "1") == "0":
                            am.inc(tensor.matmul(
                                psums[bank][0:64, 66 * j:66 * j + 66],
                                qA[0:64, 0:64], runbb[0:64, 0:66],
                                start=True, stop=True), 1)
                            continue
                        am.inc(tensor.matmul(
                            psums[bank][pb:pb + 64, 66 * j:66 * j + 66],
                            qA[:, (h // 2) * TLOC + 64 * u:(h // 2) * TLOC + 64 * u + 64],
                            runbb[:, 66 * h:66 * h + 66],
                            start=True, stop=True), 1)
                    for j, h in enumerate(g):
                        cl = (tt % 2) * 1024 + 64 * h
                        if u % 2 == 0:
                            kb = kt[0:64, cl:cl + 64]
                            vb = vt[0:64, cl:cl + 64]
                        else:
                            kb = kodd[0:64, cl:cl + 64]
                            vb = vodd[0:64, cl:cl + 64]
                        if os.environ.get("T1_CTX", "1") == "0":
                            cm.inc(tensor.matmul(psums[2 + bank][0:64, 65 * j:65 * j + 64],
                                                 kb, vb, start=True, stop=True), 1)
                            continue
                        tensor.matmul(psums[2 + bank][0:64, 65 * j:65 * j + 64],
                                      kb, vb, start=True, stop=True)
                        cm.inc(tensor.matmul(psums[2 + bank][0:64, 65 * j + 64:65 * j + 65],
                                             kb, onesc[0:64, :],
                                             start=True, stop=True), 1)

        @block.vector
        def _(v):
            def flush(ins):
                fl2.inc(ins, 1)
                v.wait_ge(fl2.h, fl2.n)
            na = 0
            nc_ = 0
            for u in range(N_U):
                tt = u // 2
                pb = 64 * (u % 2)
                for gi, g in enumerate(groups):
                    bank = (u * 3 + gi) % 2
                    na += len(g)
                    nc_ += len(g)
                    v.wait_ge(am.h, na)
                    pa = psums[bank]
                    import os as _os
                    if _os.environ.get("T1_EVAC", "1") == "0":
                        last = v.tensor_copy(at1[pb:pb + 64, tt * D:tt * D + 64],
                                             pa[pb:pb + 64, 0:64])
                    else:
                        pjs = [pa[pb:pb + 64, 66 * j:66 * j + 66]
                               for j in range(len(g))]
                        for j in range(len(g)):
                            fl2.inc(v.tensor_single_scalar(
                                sml[pb:pb + 64, j:j + 1], pjs[j][:, 65:66],
                                8.0 * EPS_L1A, ALU.mult), 1)
                        v.wait_ge(fl2.h, fl2.n)
                        for j in range(len(g)):
                            fl2.inc(v.tensor_tensor(
                                sml[pb:pb + 64, 8 + j:9 + j], pjs[j][:, 64:65],
                                sml[pb:pb + 64, j:j + 1], ALU.max), 1)
                        v.wait_ge(fl2.h, fl2.n)
                        for j in range(len(g)):
                            fl2.inc(v.reciprocal(sml[pb:pb + 64, 16 + j:17 + j],
                                                 sml[pb:pb + 64, 8 + j:9 + j]), 1)
                        v.wait_ge(fl2.h, fl2.n)
                        for j, h in enumerate(g):
                            last = v.tensor_single_scalar(
                                at1[pb:pb + 64, tt * D + 64 * h:tt * D + 64 * h + 64],
                                pjs[j][:, 0:64], sml[pb:pb + 64, 16 + j:17 + j],
                                ALU.mult)
                    ev.inc(last, 1)
                    v.wait_ge(cm.h, nc_)
                    pc = psums[2 + bank]
                    n_g = len(g)
                    g0 = g[0]
                    rb_all = bass.AP(runb, 66 * g0,
                                     [[16 * 66, 64], [66, n_g], [1, 65]])
                    fl2.inc(v.tensor_tensor(rb_all, rb_all,
                                            pc[0:64, 0:65 * n_g], ALU.add), 1)
                    v.wait_ge(fl2.h, fl2.n)
                    ev_h = [h for h in g if h % 2 == 0]
                    od_h = [h for h in g if h % 2 == 1]
                    if ev_h:
                        e0 = ev_h[0]
                        dste = bass.AP(runbb, 66 * e0,
                                       [[16 * 66, 64], [132, len(ev_h)], [1, 65]])
                        srce = bass.AP(runb, 66 * e0,
                                       [[16 * 66, 64], [132, len(ev_h)], [1, 65]])
                        v.tensor_copy(dste, srce)
                    for h in od_h:
                        last = v.tensor_copy(
                            runbb[64:128, 66 * h:66 * h + 65],
                            runb[:, 66 * h:66 * h + 65])
                    eb.inc(last, 1)
    if PH < 3:
        emit_load_x(p, x_in, x_sb)
        emit_store_x(p, x_sb, x_out)
        return p
    with p.nc.Block() as block:
        @block.vector
        def _(v):
            dst = bass.AP(ptmp, 0, [[16 * 65, 64], [65, 16], [1, 65]])
            srcr = bass.AP(runb, 0, [[16 * 66, 64], [66, 16], [1, 65]])
            v.tensor_copy(dst, srcr)
    dma_block(p, [(tot_out, ptmp[:, :])])

    if os.environ.get("T1_DBG", "0") == "1":
        dbg_out = p.dram_out("dbg_at1", [TLOC, D], BF16).ap()
        dma_block(p, [(dbg_out.rearrange("(t p) c -> p t c", p=128),
                       at1.ap().rearrange("p (t c) -> p t c", t=NT))])
    if PH < 4:
        emit_load_x(p, x_in, x_sb)
        emit_store_x(p, x_sb, x_out)
        return p
    emit_load_x(p, x_in, x_sb)
    emit_to_xT(p, at1, at1T, ident, psb16[0], psb16[1])
    jobs = []
    for f in range(8):
        for m in range(4):
            def mk(f=f, m=m):
                def evac(eng, pa):
                    return eng.tensor_single_scalar(chunkT(yT, f, m * 512, 512), pa,
                                                    pb8[:, f:f + 1], ALU.add)
                return evac
            jobs.append(dict(
                mms=[("m", wW[:, c * 1024 + f * 128:c * 1024 + (f + 1) * 128],
                      chunkT(at1T, c, m * 512, 512)) for c in range(8)],
                out_w=512, engine="vector", evac=mk()))
    dma_block(p, [(wW[:, c * 1024:(c + 1) * 1024], pwT_in[c * 128:(c + 1) * 128, :])
                  for c in range(8)], queue="gpsimd")
    if os.environ.get("T1_DBG", "0") == "1":
        dbg3 = p.dram_out("dbg_at1T", [D, TLOC], BF16).ap()
        dma_block(p, [(dbg3[c * 128:(c + 1) * 128, :], at1T[:, c * TLOC:(c + 1) * TLOC])
                      for c in range(8)])
    mm_grid(p, jobs, psums, tag="l1aproj")
    if os.environ.get("T1_DBG", "0") == "1":
        dbg2 = p.dram_out("dbg_yT", [D, TLOC], BF16).ap()
        dma_block(p, [(dbg2[c * 128:(c + 1) * 128, :], yT[:, c * TLOC:(c + 1) * TLOC])
                      for c in range(8)])
    emit_addT_into_x(p, yT, x_sb, ident, psb16)
    if os.environ.get("T1_NOLN", "0") != "1":
        emit_ln(p, x_sb)
    emit_store_x(p, x_sb, x_out)

    if PH < 5:
        return p
    emit_rms_cols(p, x_sb, rc16, EPS_RMS)
    xb2 = at1T
    with p.nc.Block() as block:
        @block.scalar
        def _(scb):
            for t in range(NT):
                scb.activation(xb2[:, t * D:(t + 1) * D], x_sb[:, t * D:(t + 1) * D],
                               AF.Copy, scale=rc16[:, t:t + 1])
    emit_toT_scaled(p, xb2, xtT, ident, psb16[0], psb16[1],
                    gain_cols=lambda c: n1c[:, c:c + 1])

    if PH < 6:
        return p
    for wap, dst, out_ap in ((wqT_in, qhT, None), (wkT_in, khT, None),
                             (wvT_in, None, vh_out)):
        dma_block(p, [(wW[:, c * 1024:(c + 1) * 1024], wap[c * 128:(c + 1) * 128, :])
                      for c in range(8)], queue="gpsimd")
        jobs = []
        ji = 0
        for f in range(8):
            for m in range(4):
                if dst is not None:
                    def mk(f=f, m=m, dst=dst):
                        def evac(eng, pa):
                            return eng.copy(chunkT(dst, f, m * 512, 512), pa)
                        return evac
                    jobs.append(dict(
                        mms=[("m", wW[:, c * 1024 + f * 128:c * 1024 + (f + 1) * 128],
                              chunkT(xtT, c, m * 512, 512)) for c in range(8)],
                        out_w=512, engine="scalar", evac=mk()))
                else:
                    sl = ji % 4
                    ji += 1
                    def mk(sl=sl):
                        def evac(eng, pa):
                            return eng.copy(stg[:, sl * 512:(sl + 1) * 512], pa)
                        return evac
                    jobs.append(dict(
                        mms=[("m", wW[:, c * 1024 + f * 128:c * 1024 + (f + 1) * 128],
                              chunkT(xtT, c, m * 512, 512)) for c in range(8)],
                        out_w=512, engine="scalar", evac=mk(),
                        dma=(out_ap[f * 128:(f + 1) * 128, m * 512:(m + 1) * 512],
                             stg[:, sl * 512:(sl + 1) * 512])))
        mm_grid(p, jobs, psums, tag="qkv", nstage=(4 if dst is None else 0))

    if PH < 7:
        return p
    epsr = p.constcol(EPS_RMS, rows=2)
    sqm = p.sem("t1sqm")
    ssm = p.sem("t1ssm")
    rsm = p.sem("t1rsm")
    bcm = p.sem("t1bcm")
    apm = p.sem("t1apm")
    k2m = p.sem("t1k2m")
    ksm = p.sem("t1ksm")
    kem = p.sem("t1kem")
    kdm = p.sem("t1kdm")
    sfm = p.sem("t1sfm")
    for side, buf, gain in (("q", qhT, None), ("k", khT, qkn)):
        sq_b, ss_b, rs_b, bc_b, ap_b, k2_b, ks_b, ke_b, kd_b = (
            sqm.n, ssm.n, rsm.n, bcm.n, apm.n, k2m.n, ksm.n, kem.n, kdm.n)
        with p.nc.Block() as block:
            @block.scalar
            def _(s):
                i = 0
                for c in range(8):
                    for m in range(4):
                        if i >= 2:
                            s.wait_ge(ssm.h, ss_b + i - 1)
                            if side == "k":
                                s.wait_ge(ksm.h, ks_b + i - 1)
                        sqm.inc(s.activation(sq[:, (i % 2) * 512:(i % 2 + 1) * 512],
                                             chunkT(buf, c, m * 512, 512), AF.Square), 1)
                        s.wait_ge(ssm.h, ss_b + i + 1)
                        if i >= 2:
                            s.wait_ge(bcm.h, bc_b + i - 1)
                        rsl = rstd2[:, (i % 2) * 512:(i % 2 + 1) * 512]
                        sfm.inc(s.activation(rsl, psums[2][0:2, 0:512], AF.Ln,
                                             bias=epsr, scale=1.0 / 64.0), 1)
                        s.wait_ge(sfm.h, sfm.n)
                        rsm.inc(s.activation(rsl, rsl, AF.Exp, scale=-0.5), 1)
                        if side == "k":
                            s.wait_ge(apm.h, ap_b + i + 1)
                            k2m.inc(s.activation(sq[:, (i % 2) * 512:(i % 2 + 1) * 512],
                                                 chunkT(buf, c, m * 512, 512),
                                                 AF.Square), 1)
                        i += 1
            @block.tensor
            def _(tensor):
                i = 0
                for c in range(8):
                    for m in range(4):
                        tensor.wait_ge(sqm.h, sq_b + i + 1)
                        ssm.inc(tensor.matmul(psums[2][0:2, 0:512],
                                              bd[:, 0:2],
                                              sq[:, (i % 2) * 512:(i % 2 + 1) * 512],
                                              start=True, stop=True), 1)
                        tensor.wait_ge(rsm.h, rs_b + i + 1)
                        bcm.inc(tensor.matmul(psums[3][0:128, 0:512],
                                              sel[0:2, :],
                                              rstd2[:, (i % 2) * 512:(i % 2 + 1) * 512],
                                              start=True, stop=True), 1)
                        if side == "k":
                            tensor.wait_ge(k2m.h, k2_b + i + 1)
                            ksm.inc(tensor.matmul(psums[2][0:2, 0:512],
                                                  bd[:, 0:2],
                                                  sq[:, (i % 2) * 512:(i % 2 + 1) * 512],
                                                  start=True, stop=True), 1)
                            tensor.wait_ge(kem.h, ke_b + i + 1)
                        else:
                            tensor.wait_ge(apm.h, ap_b + i + 1)
                        i += 1
            @block.vector
            def _(v):
                i = 0
                for c in range(8):
                    for m in range(4):
                        v.wait_ge(bcm.h, bc_b + i + 1)
                        sl = chunkT(buf, c, m * 512, 512)
                        if gain is None:
                            apm.inc(v.tensor_tensor(sl, sl, psums[3][0:128, 0:512],
                                                    ALU.mult), 1)
                        else:
                            apm.inc(v.scalar_tensor_tensor(sl, sl, gain[:, :],
                                                           psums[3][0:128, 0:512],
                                                           ALU.mult, ALU.mult), 1)
                        if side == "k":
                            v.wait_ge(ksm.h, ks_b + i + 1)
                            if i >= 2:
                                v.wait_ge(kdm.h, kd_b + 16 * (i - 1))
                            kem.inc(v.tensor_copy(ksg[0:2, (i % 2) * 512:(i % 2 + 1) * 512],
                                                  psums[2][0:2, 0:512]), 1)
                        i += 1
            if side == "k":
                @block.sync
                def _(sync):
                    i = 0
                    for c in range(8):
                        for m in range(4):
                            sync.wait_ge(kem.h, ke_b + i + 1)
                            kdm.inc(sync.dma_start(
                                out=kss_out[2 * c:2 * c + 2, m * 512:(m + 1) * 512],
                                in_=ksg[0:2, (i % 2) * 512:(i % 2 + 1) * 512]), 16)
                            i += 1
                    sync.wait_ge(kdm.h, kdm.n)
    dma_block(p, [(qh_out[c * 128:(c + 1) * 128, :], qhT[:, c * TLOC:(c + 1) * TLOC])
                  for c in range(8)]
              + [(kh_out[c * 128:(c + 1) * 128, :], khT[:, c * TLOC:(c + 1) * TLOC])
                 for c in range(8)])
    return p


def prep_T1(inputs, st):
    maps = []
    ident = host_ident128()
    pwT = f32c(inputs["L1_attn_proj_w"])
    pwT = f32c(inputs["L1_attn_proj_w"].T)
    pb = f32c(inputs["L1_attn_proj_b"].reshape(8, 128).T)
    n1 = f32c(inputs["L1_ffn_n1_w"].reshape(8, 128).T)
    wqT = f32c(inputs["L1_ffn_wq"].T)
    wkT = f32c(inputs["L1_ffn_wk"].T)
    wvT = f32c(inputs["L1_ffn_wv"].T)
    qn2 = f32c(np.tile(inputs["L1_ffn_qn_w"], 2).reshape(128, 1))
    kn2 = f32c(np.tile(inputs["L1_ffn_kn_w"], 2).reshape(128, 1))
    zeros_tot = np.zeros((64, 16 * 65), np.float32)
    for c in range(NC):
        g0 = 4 * (c // 4)
        preds = [st["totals"][j] for j in range(g0, c)]
        while len(preds) < 3:
            preds.append(zeros_tot)
        m = dict(x=st["x2"][c], qt=st["qt"][c], ktm=st["ktm"][c], vtm=st["vtm"][c],
                 projwT=pwT, projb=pb, n1w=n1, wqT=wqT, wkT=wkT, wvT=wvT,
                 qn2=qn2, kn2=kn2, ident=ident)
        for i in range(3):
            m[f"ptot{i}"] = preds[i]
        maps.append(m)
    return maps




def build_F1():
    p = Prog("F1")
    q_in = p.dram_in("q", [2, 2, 64, S], BF16).ap()
    k_in = p.dram_in("k", [2, 2, 64, S], BF16).ap()
    v_in = p.dram_in("v", [2, 2, S, 64], BF16).ap()
    ks_in = p.dram_in("kss", [2, 2, S]).ap()
    m_in = p.dram_in("masks", [4, 128, 512], BF16).ap()
    ao_o = p.dram_out("ao", [2, 2, 64, S], BF16).ap()

    q_sb = [p.sbuf(f"q{i}", [96, S], BF16) for i in range(2)]
    k_sb = [p.sbuf(f"k{i}", [96, S], BF16) for i in range(2)]
    vh_sb = [p.sbuf(f"vh{i}", [128, 64 * 65], BF16) for i in range(4)]
    kssb = p.sbuf("kssb", [1, S], BF16)
    msk = p.sbuf("msk", [128, 4 * 512], BF16)
    ones64 = p.sbuf("ones64", [1, 64], BF16)
    ones128 = p.sbuf("ones128", [1, 128], BF16)
    csml = p.sbuf("csml", [1, 16], F32)
    cbf = p.sbuf("cbf", [1, 4], BF16)
    biasc = p.sbuf("biasc", [128, 4], F32)
    pt = [p.sbuf(f"pt{i}", [128, 512], BF16) for i in range(8)]
    stg = [dict(rd=p.sbuf(f"rd{t}", [1, 2 * 512], BF16),
                rf=p.sbuf(f"rf{t}", [1, 512], F32),
                num=p.sbuf(f"num{t}", [64, 2 * 512], BF16),
                ao=p.sbuf(f"aos{t}", [64, 2 * 512], BF16)) for t in range(2)]
    ps_s = [[p.psum(f"pss{t}{i}", [128, 512]) for i in range(2)] for t in range(2)]
    ps_o = [p.psum(f"pso{t}", [65, 512]) for t in range(2)]
    ps_b = [p.psum(f"psb{t}", [128, 512]) for t in range(2)]

    units = [(b, h) for b in range(2) for h in range(2)]
    lds = [p.sem("fldA"), p.sem("fldB")]
    ms = p.sem("fms")
    kl = p.sem("fkl")
    fc = p.sem("fcs")
    fs = p.sem("fsq")
    fv = p.sem("fvn")
    fm2 = p.sem("fm2")
    fv2 = p.sem("fv2")
    dma_block(p, [(msk[:, 512 * r:512 * (r + 1)], m_in[r]) for r in range(4)])

    with p.nc.Block() as block:
        @block.vector
        def _(v):
            v.memset(ones64[:, :], 1.0)
            v.memset(ones128[:, :], 1.0)
            for t in range(2):
                v.memset(q_sb[t][64:96, :], 0.0)
                v.memset(k_sb[t][64:96, :], 0.0)
            for u in range(4):
                ms.inc(v.memset(vh_sb[u][:, :], 1.0), 1)

    with p.nc.Block() as block:
        @block.gpsimd
        def _(g):
            for u, (b, h) in enumerate(units):
                if u >= 1:
                    g.wait_ge(fc.h, u)
                kl.inc(g.dma_start(out=kssb[:, :], in_=ks_in[b:b + 1, h, :]), 16)

        @block.vector
        def _(v):
            for u in range(4):
                v.wait_ge(kl.h, 16 * (u + 1))
                fc.inc(v.reduce_max(csml[:, u:u + 1], kssb[:, :], axis=AX), 1)

    with p.nc.Block() as block:
        @block.scalar
        def _(s):
            for u in range(4):
                fs.inc(s.activation(csml[:, 4 + u:5 + u], csml[:, u:u + 1],
                                    AF.Sqrt), 1)

        @block.vector
        def _(v):
            v.wait_ge(fs.h, 4)
            for u in range(4):
                fv.inc(v.tensor_scalar_mul(cbf[:, u:u + 1],
                                           csml[:, 4 + u:5 + u], -1.0), 1)

    with p.nc.Block() as block:
        @block.tensor
        def _(t):
            t.wait_ge(fv.h, 4)
            for u in range(4):
                if u >= 2:
                    t.wait_ge(fv2.h, u - 1)
                fm2.inc(t.matmul(ps_b[u % 2][0:128, 0:1], ones128[0:1, :],
                                 cbf[:, u:u + 1], start=True, stop=True), 1)

        @block.vector
        def _(v):
            for u in range(4):
                v.wait_ge(fm2.h, u + 1)
                fv2.inc(v.tensor_copy(biasc[:, u:u + 1], ps_b[u % 2][0:128, 0:1]), 1)

    with p.nc.Block() as block:
        @block.sync
        def _(sync):
            sync.wait_ge(ms.h, 4)
            for u in (0, 1):
                b, h = units[u]
                lds[0].inc(sync.dma_start(out=q_sb[u][0:64, :], in_=q_in[b, h]), 16)
                lds[0].inc(sync.dma_start(out=k_sb[u][0:64, :], in_=k_in[b, h]), 16)

        @block.gpsimd
        def _(g):
            g.wait_ge(ms.h, 4)
            for u in range(4):
                b, h = units[u]
                dst = vh_sb[u].ap().rearrange("p (k c) -> p k c", c=65)[:, :, 0:64]
                src = v_in[b, h].rearrange("(k p) c -> p k c", p=128)
                lds[1].inc(g.dma_start(out=dst, in_=src), 16)

    for pi in range(2):
        if pi == 1:
            with p.nc.Block() as block:
                @block.sync
                def _(sync):
                    for u in (2, 3):
                        b, h = units[u]
                        lds[0].inc(sync.dma_start(out=q_sb[u % 2][0:64, :],
                                                  in_=q_in[b, h]), 16)
                        lds[0].inc(sync.dma_start(out=k_sb[u % 2][0:64, :],
                                                  in_=k_in[b, h]), 16)
                    sync.wait_ge(lds[0].h, lds[0].n)
        US = []
        for t in range(2):
            u = 2 * pi + t
            b, h = units[u]
            US.append(dict(
                dq=96, q=q_sb[t], k=k_sb[t], vhat=vh_sb[u],
                pt=pt[4 * t:4 * t + 4], ps_s=ps_s[t], ps_o=ps_o[t],
                ps_b=ps_b[t], bias=biasc[:, u:u + 1],
                ao_out=lambda m, b=b, h=h: ao_o[b, h, :, 512 * m:512 * (m + 1)],
                sems=flash_sems(p, t), **stg[t]))
        emit_flash_pair(p, US, msk=msk, ones64=ones64, exp_scale=0.125,
                        clip=None,
                        wait_lds=[(lds[0], 64 * (pi + 1)), (lds[1], 64)])
    return p


def prep_F1(st):
    maps = []
    masks = host_band_masks()
    ident = host_ident128()
    qh, kh, vh, kss = st["qh"], st["kh"], st["vh"], st["kss"]
    for c in range(NC):
        q = np.zeros((2, 2, 64, S), BF)
        k = np.zeros((2, 2, 64, S), BF)
        vv = np.zeros((2, 2, S, 64), BF)
        ks = np.zeros((2, 2, S), np.float32)
        for bi in range(2):
            for hl in range(2):
                h = 2 * c + hl
                q[bi, hl] = qh[64 * h:64 * h + 64, bi * S:(bi + 1) * S]
                k[bi, hl] = kh[64 * h:64 * h + 64, bi * S:(bi + 1) * S]
                vv[bi, hl] = vh[64 * h:64 * h + 64, bi * S:(bi + 1) * S].T
                ks[bi, hl] = kss[h, bi * S:(bi + 1) * S]
        maps.append(dict(q=q, k=k, v=vv, kss=ks, masks=masks))
    return maps




def emit_fm_grid(p, wbuf, src_sb, evac_fn, psums, fchunks=8, nmac=4,
                 cchunks=8, wstride=None, engine="scalar", nstage=0, tag=None):
    if wstride is None:
        wstride = fchunks * 128
    jobs = []
    for f in range(fchunks):
        for m in range(nmac):
            jobs.append(dict(
                mms=[("m", wbuf[:, c * wstride + f * 128:c * wstride + (f + 1) * 128],
                      chunkT(src_sb, c, m * 512, 512)) for c in range(cchunks)],
                out_w=512, engine=engine, evac=evac_fn(f, m)))
    mm_grid(p, jobs, psums, tag=tag, nstage=nstage)


def ld_wT(p, wbuf, w_in, nchunk, ncols, col0=0):
    dma_block(p, [(wbuf[:, c * ncols:(c + 1) * ncols],
                   w_in[c * 128:(c + 1) * 128, col0:col0 + ncols])
                  for c in range(nchunk)], queue="gpsimd")



def wloads_fm(bufA_t, w_in, cchunks, fchunks, col0=0, reg0=0):
    G = fchunks // 2
    F = fchunks * 128
    gstride = cchunks * 256
    halves = []
    for h in range(2):
        g0 = h * (G // 2)
        ng = G // 2
        prs = []
        for c in range(cchunks):
            src = w_in[c * 128:(c + 1) * 128,
                       col0 + h * (F // 2):col0 + (h + 1) * (F // 2)]
            src = src.rearrange("p (g j) -> p g j", j=256)
            dst = bass.AP(bufA_t, reg0 + g0 * gstride + c * 256,
                          [[16384, 128], [gstride, ng], [1, 256]])
            prs.append((dst, src))
        halves.append(prs)
    return halves


def wloads_dn(bufA, w_in):
    return [[(bufA[:, h * 8192 + fc * 512:h * 8192 + (fc + 1) * 512],
              w_in[fc * 128:(fc + 1) * 128, h * 512:(h + 1) * 512])
             for fc in range(16)] for h in range(2)]


def jobs_h_pm(bufA, src, dstbuf, dstoff, m, act, fchunks=16, cchunks=8,
              mul_into=None):
    gstride = cchunks * 256
    jobs = []
    for f in range(fchunks):
        def mk(f=f):
            def evac(eng, pa):
                dst = dstbuf[:, dstoff + f * 512:dstoff + (f + 1) * 512]
                if mul_into is not None:
                    return eng.tensor_mul(dst, pa, dst)
                if act is None:
                    return eng.copy(dst, pa)
                return eng.activation(dst, pa, act)
            return evac
        col = (f // 2) * gstride + (f % 2) * 128
        jobs.append(dict(
            mms=[("m", bufA[:, col + c * 256:col + c * 256 + 128],
                  chunkT(src, c, m * 512, 512)) for c in range(cchunks)],
            out_w=512, engine="vector" if mul_into is not None else "scalar",
            evac=mk()))
    return jobs


def jobs_h2_pm(bufA, hbuf, srcoff, dstoff, act, fchunks=8, cchunks=16):
    gstride = cchunks * 256
    jobs = []
    for f in range(fchunks):
        def mk(f=f):
            def evac(eng, pa):
                dst = hbuf[:, dstoff + f * 512:dstoff + (f + 1) * 512]
                if act is None:
                    return eng.copy(dst, pa)
                return eng.activation(dst, pa, act)
            return evac
        col = (f // 2) * gstride + (f % 2) * 128
        jobs.append(dict(
            mms=[("m", bufA[:, col + c * 256:col + c * 256 + 128],
                  hbuf[:, srcoff + c * 512:srcoff + c * 512 + 512])
                 for c in range(cchunks)],
            out_w=512, engine="scalar", evac=mk()))
    return jobs


def jobs_dn_pm(bufA, bufB, x_sb, comb, m, e, hoff=0):
    jobs = []
    for half in range(2):
        for tt in range(4):
            def mk(m=m, tt=tt, half=half, e=e):
                def evac(eng, pa):
                    tg = 4 * m + tt
                    c0 = 256 if half == 0 else 0
                    xs = x_sb[:, tg * D + half * 512 + c0:
                              tg * D + (half + 1) * 512]
                    return eng.scalar_tensor_tensor(
                        xs, pa[:, c0:512], comb[:, tg * 4 + e:tg * 4 + e + 1],
                        xs, ALU.mult, ALU.add)
                return evac
            jobs.append(dict(
                mms=[("m", bufB[:, hoff + fc * 512 + 128 * tt:
                                hoff + fc * 512 + 128 * tt + 128],
                      bufA[:, half * 8192 + fc * 512:half * 8192 + (fc + 1) * 512])
                     for fc in range(16)],
                out_w=512, engine="vector", evac=mk()))
    return jobs


def run_wstages(p, stages, psums):
    dg = p.sem("dma_gp")
    fr = p.sem("fr_shared")
    dma_block(p, [pr for _, prs in stages[0]["loads"] for pr in prs],
              queue="gpsimd")
    region_free = [0, 0]
    for s, stg in enumerate(stages):
        jobs = stg["jobs"]
        for ji, cnt in stg.get("_jw", []):
            jobs[ji]["wait"] = (dg, cnt)
        for reg, after in stg["free"].items():
            region_free[reg] = fr.n + after
        gp_loads = []
        if s + 1 < len(stages):
            nxt = stages[s + 1]
            run = dg.n
            jw = []
            for reg, prs in nxt["loads"]:
                gp_loads.append((region_free[reg], prs))
                run += 16 * len(prs)
                jw.append((nxt["jstart"][reg], run))
            nxt["_jw"] = jw
        mm_grid(p, jobs, psums, tag=stg["tag"], gp_loads=gp_loads)


def build_T2():
    import os
    PH = int(os.environ.get("T2_PHASES", "9"))
    p = Prog("T2")
    x_in = p.dram_in("x", [TLOC, D]).ap()
    ao_in = p.dram_in("aoT", [D, TLOC], BF16).ap()
    woT_in = p.dram_in("woT", [D, D]).ap()
    n2_in = p.dram_in("n2w", [128, 8]).ap()
    gwT_in = p.dram_in("gwT", [D, 4]).ap()
    e0u_in = p.dram_in("e0upT", [D, 2048]).ap()
    e0d_in = p.dram_in("e0dnT", [2048, D]).ap()
    e1a_in = p.dram_in("e1w1T", [D, 2048]).ap()
    e1b_in = p.dram_in("e1w2T", [D, 2048]).ap()
    e1d_in = p.dram_in("e1dnT", [2048, D]).ap()
    e2a_in = p.dram_in("e2l1T", [D, 2048]).ap()
    e2b_in = p.dram_in("e2l2T", [2048, 2048]).ap()
    e2c_in = p.dram_in("e2l3T", [2048, 2048]).ap()
    e2d_in = p.dram_in("e2l4T", [2048, D]).ap()
    e3a_in = p.dram_in("e3dnT", [D, D]).ap()
    e3b_in = p.dram_in("e3upT", [D, 2048]).ap()
    e3c_in = p.dram_in("e3outT", [2048, D]).ap()
    wq_in = p.dram_in("wqT2", [D, D]).ap()
    wk_in = p.dram_in("wkT2", [D, D]).ap()
    wv_in = p.dram_in("wvT2", [D, D]).ap()
    bq_in = p.dram_in("bq", [128, 8]).ap()
    bk_in = p.dram_in("bk", [128, 8]).ap()
    bv_in = p.dram_in("bv", [128, 8]).ap()
    identi = p.dram_in("ident", [128, 128], BF16).ap()

    x_out = p.dram_out("x4", [TLOC, D]).ap()
    q_out = p.dram_out("q2", [D, TLOC], BF16).ap()
    k_out = p.dram_out("k2", [D, TLOC], BF16).ap()
    v_out = p.dram_out("v2", [TLOC, D], BF16).ap()
    qs_out = p.dram_out("qss2", [1, TLOC]).ap()
    ks_out = p.dram_out("kss2", [1, TLOC]).ap()

    p.make_arena(200704)
    x_sb = p.sbuf_at("xsb", [128, NT * D], F32, 0)
    OFF_C, OFF_A, OFF_B = 65536, 98304, 131072
    bufC = p.sbuf_at("bufC", [128, 8 * TLOC], BF16, OFF_C)
    bufA = p.sbuf_at("bufA", [128, 16 * 1024], BF16, OFF_A)
    bufB = p.sbuf_at("bufB", [128, 16 * 1024], BF16, OFF_B)
    OFF_M = 163840
    stg = p.sbuf_at("stg", [128, 4 * 512], BF16, OFF_M)
    comb = p.sbuf_at("comb", [128, NT * 4], F32, OFF_M + 4096)
    lgT = p.sbuf_at("lgT", [4, TLOC], BF16, OFF_M + 4352)
    rc16 = p.sbuf_at("rc16", [128, 16], F32, OFF_M + 8448)
    n2c = p.sbuf_at("n2c", [128, 8], F32, OFF_M + 8512)
    bqc = p.sbuf_at("bqc", [128, 8], F32, OFF_M + 8544)
    bkc = p.sbuf_at("bkc", [128, 8], F32, OFF_M + 8576)
    bvc = p.sbuf_at("bvc", [128, 8], F32, OFF_M + 8608)
    onesb = p.sbuf_at("onesb", [128, 1], BF16, OFF_M + 8640)
    sqs = p.sbuf_at("sqs", [128, 2 * 512], BF16, OFF_M + 8672)
    ssg = p.sbuf_at("ssg", [1, 4 * 512], F32, OFF_M + 12768)
    xrl = p.sbuf_at("xrl", [128, 2 * 1024], F32, OFF_M + 20960)
    ident = p.sbuf_at("id", [128, 128], BF16, OFF_M + 29152)

    psums = [p.psum(f"ps{i}", [128, 512]) for i in range(4)]
    psb16 = [p.psum(f"psb{i}", [128, 512], BF16) for i in range(2)]

    emit_load_x(p, x_in, x_sb)
    dma_block(p, [(ident[:, :], identi)])
    dma_block(p, [(bufA[:, c * 2048:(c + 1) * 2048],
                   ao_in[c * 128:(c + 1) * 128, :]) for c in range(8)], queue="gpsimd")
    dma_block(p, [(n2c[:, :], n2_in), (bqc[:, :], bq_in), (bkc[:, :], bk_in),
                  (bvc[:, :], bv_in)], queue="gpsimd")
    with p.nc.Block() as block:
        @block.vector
        def _(v):
            v.memset(onesb[:, :], 1.0)

    wW = p.sbuf_at("wW", [128, 8 * 1024], BF16, OFF_C)
    ld_wT(p, wW, woT_in, 8, 1024)
    jobs = []
    for f in range(2, 8):
        for m in range(4):
            def mk(f=f, m=m):
                def evac(eng, pa):
                    return eng.copy(chunkT(bufB, f, m * 512, 512), pa)
                return evac
            jobs.append(dict(
                mms=[("m", wW[:, c * 1024 + f * 128:c * 1024 + (f + 1) * 128],
                      bufA[:, c * 2048 + m * 512:c * 2048 + m * 512 + 512])
                     for c in range(8)],
                out_w=512, engine="scalar", evac=mk()))
    mm_grid(p, jobs, psums, tag="wo")
    emit_addT_into_x(p, bufB, x_sb, ident, psb16, skip_chunks=(0, 1))

    if PH < 2:
        emit_store_x(p, x_sb, x_out)
        return p
    emit_rms_cols(p, x_sb, rc16, EPS_RMS)
    with p.nc.Block() as block:
        @block.scalar
        def _(scb):
            for t in range(NT):
                scb.activation(bufB[:, t * D:(t + 1) * D], x_sb[:, t * D:(t + 1) * D],
                               AF.Copy, scale=rc16[:, t:t + 1])
    emit_toT_scaled(p, bufB, bufC, ident, psb16[0], psb16[1],
                    gain_cols=lambda c: n2c[:, c:c + 1])
    gw = p.sbuf_at("gw", [128, 8 * 4], BF16, OFF_M + 29152 + 256)
    ld_wT(p, gw, gwT_in, 8, 4)
    jobs = []
    for m in range(4):
        def mkg(m=m):
            def evac(eng, pa):
                return eng.copy(lgT[0:4, m * 512:(m + 1) * 512], pa)
            return evac
        jobs.append(dict(mms=[("m", gw[:, c * 4:(c + 1) * 4],
                               chunkT(bufC, c, m * 512, 512)) for c in range(8)],
                         out_w=512, rows=4, engine="scalar", evac=mkg()))
    mm_grid(p, jobs, psums[:2], tag="gate2")
    emit_gate_topk(p, lgT, comb, 4, 1e-8, psb16, ident)

    if PH < 3:
        emit_store_x(p, x_sb, x_out)
        return p
    bufA_t = bufA
    stages = []

    def add_stage(loads, jobs, tag, regions=2):
        n = len(jobs)
        if regions == 2:
            stages.append(dict(loads=[(0, loads[0]), (1, loads[1])], jobs=jobs,
                               free={0: n // 2, 1: n},
                               jstart={0: 0, 1: n // 2}, tag=tag))
        else:
            stages.append(dict(loads=[(0, loads[0]), (0, loads[1])], jobs=jobs,
                               free={0: n}, jstart={0: 0}, tag=tag))

    def add_stage2(loads, jA, jB, tag):
        n = len(jA)
        jobs = jA[:n // 2] + jB[:n // 2] + jA[n // 2:] + jB[n // 2:]
        stages.append(dict(loads=[(0, loads[0]), (1, loads[1])], jobs=jobs,
                           free={0: n, 1: 2 * n}, jstart={0: 0, 1: n},
                           tag=tag))

    for e in range(4):
        for m in range(4):
            if e == 0:
                add_stage(wloads_fm(bufA_t, e0u_in, 8, 16),
                          jobs_h_pm(bufA, bufC, bufB, 0, m, AF.Gelu), f"e0h{m}")
            elif e == 1:
                add_stage(wloads_fm(bufA_t, e1a_in, 8, 16),
                          jobs_h_pm(bufA, bufC, bufB, 0, m, AF.Silu), f"e1a{m}")
                add_stage(wloads_fm(bufA_t, e1b_in, 8, 16),
                          jobs_h_pm(bufA, bufC, bufB, 0, m, None, mul_into=True),
                          f"e1b{m}")
            elif e == 2:
                add_stage(wloads_fm(bufA_t, e2a_in, 8, 16),
                          jobs_h_pm(bufA, bufC, bufB, 0, m, AF.Gelu), f"e2a{m}")
                for half2 in range(2):
                    add_stage(wloads_fm(bufA_t, e2b_in, 16, 8, col0=half2 * 1024),
                              jobs_h2_pm(bufA, bufB, 0, 8192 + half2 * 4096,
                                         AF.Gelu), f"e2b{m}_{half2}")
                for half2 in range(2):
                    add_stage(wloads_fm(bufA_t, e2c_in, 16, 8, col0=half2 * 1024),
                              jobs_h2_pm(bufA, bufB, 8192, half2 * 4096,
                                         AF.Gelu), f"e2c{m}_{half2}")
            else:
                add_stage(wloads_fm(bufA_t, e3a_in, 8, 8),
                          jobs_h_pm(bufA, bufC, bufB, 8192, m, AF.Gelu,
                                    fchunks=8), f"e3a{m}", regions=1)
                add_stage(wloads_fm(bufA_t, e3b_in, 8, 16),
                          jobs_h2_pm(bufA, bufB, 8192, 0, AF.Gelu,
                                     fchunks=16, cchunks=8), f"e3b{m}")
            dn = (e0d_in, e1d_in, e2d_in, e3c_in)[e]
            add_stage(wloads_dn(bufA, dn),
                      jobs_dn_pm(bufA, bufB, x_sb, comb, m, e), f"dn{e}_{m}")
    run_wstages(p, stages, psums)

    if PH < 4:
        emit_store_x(p, x_sb, x_out)
        return p
    rl = p.sem("t2rl")
    ld2 = p.sem("t2ld")
    with p.nc.Block() as block:
        @block.sync
        def _(sync):
            for t in range(NT):
                if t >= 2:
                    sync.wait_ge(rl.h, t - 1)
                sl = (t % 2) * 1024
                ld2.inc(sync.dma_start(out=xrl[:, sl:sl + 1024],
                                       in_=x_in[t * 128:(t + 1) * 128, :]), 16)
        @block.vector
        def _(v):
            for t in range(NT):
                v.wait_ge(ld2.h, 16 * (t + 1))
                sl = (t % 2) * 1024
                rl.inc(v.tensor_add(x_sb[:, t * D:(t + 1) * D],
                                    x_sb[:, t * D:(t + 1) * D], xrl[:, sl:sl + 1024]), 1)
    emit_ln(p, x_sb)
    emit_store_x(p, x_sb, x_out)

    if PH < 5:
        return p
    SUB = int(os.environ.get("T2_SUB", "9"))
    emit_bf16_tiles(p, x_sb, bufB)
    emit_to_xT(p, bufB, bufC, ident, psb16[0], psb16[1])
    for w_in, bias, out_ap, ss_ap in ((wq_in, bqc, q_out, qs_out),
                                      (wk_in, bkc, k_out, ks_out))[:max(1, SUB - 0) if SUB >= 2 else 1]:
        ld_wT(p, bufA, w_in, 8, 1024)
        jobs = []
        ji = 0
        for f in range(8):
            for m in range(4):
                sl = ji % 4
                ji += 1
                def mk(f=f, m=m, sl=sl, bias=bias):
                    def evac(eng, pa):
                        eng.tensor_single_scalar(chunkT(bufB, f, m * 512, 512), pa,
                                                 bias[:, f:f + 1], ALU.add)
                        return eng.tensor_copy(stg[:, sl * 512:(sl + 1) * 512],
                                               chunkT(bufB, f, m * 512, 512))
                    return evac
                jobs.append(dict(
                    mms=[("m", bufA[:, c * 1024 + f * 128:c * 1024 + (f + 1) * 128],
                          chunkT(bufC, c, m * 512, 512)) for c in range(8)],
                    out_w=512, engine="vector", evac=mk(),
                    dma=(out_ap[f * 128:(f + 1) * 128, m * 512:(m + 1) * 512],
                         stg[:, sl * 512:(sl + 1) * 512])))
        mm_grid(p, jobs, psums, tag="l2qk", nstage=4)
        if SUB >= 2:
            emit_ssq_1024(p, bufB, sqs, ssg, onesb, ss_ap, psums)
        if SUB < 3:
            return p
    if SUB < 4:
        return p
    ld_wT(p, bufA, wv_in, 8, 1024)
    jobs = []
    for f in range(8):
        for m in range(4):
            def mkv(f=f, m=m):
                def evac(eng, pa):
                    return eng.tensor_single_scalar(chunkT(bufB, f, m * 512, 512),
                                                    pa, bvc[:, f:f + 1], ALU.add)
                return evac
            jobs.append(dict(
                mms=[("m", bufA[:, c * 1024 + f * 128:c * 1024 + (f + 1) * 128],
                      chunkT(bufC, c, m * 512, 512)) for c in range(8)],
                out_w=512, engine="vector", evac=mkv()))
    mm_grid(p, jobs, psums, tag="l2v")
    jobs = []
    ji = 0
    for t in range(NT):
        for c in range(8):
            sl = ji % 4
            ji += 1
            def mkt(t=t, c=c, sl=sl):
                def evac(eng, pa):
                    return eng.tensor_copy(stg[:, sl * 512:sl * 512 + 128], pa)
                return evac
            jobs.append(dict(mms=[("t", chunkT(bufB, c, t * 128, 128), ident[:, :])],
                             out_w=128, engine="vector", evac=mkt(),
                             dma=(v_out[t * 128:(t + 1) * 128, c * 128:(c + 1) * 128],
                                  stg[:, sl * 512:sl * 512 + 128])))
    mm_grid(p, jobs, psb16, tag="l2vt", nstage=4)
    return p


def emit_h_grid(p, wbuf, src, dstbuf, dstoff, m, act, psums, fchunks=16,
                wcols=2048, cchunks=8):
    jobs = []
    for f in range(fchunks):
        def mk(f=f):
            def evac(eng, pa):
                dst = dstbuf[:, dstoff + f * 512:dstoff + (f + 1) * 512]
                if act is None:
                    return eng.copy(dst, pa)
                return eng.activation(dst, pa, act)
            return evac
        jobs.append(dict(
            mms=[("m", wbuf[:, c * wcols + f * 128:c * wcols + (f + 1) * 128],
                  chunkT(src, c, m * 512, 512)) for c in range(cchunks)],
            out_w=512, engine="scalar", evac=mk()))
    mm_grid(p, jobs, psums, tag=f"h{p.uid()}")


def emit_h2_grid(p, wbuf, hbuf, srcoff, dstoff, act, psums, fchunks=8,
                 cchunks=16):
    jobs = []
    for f in range(fchunks):
        def mk(f=f):
            def evac(eng, pa):
                dst = hbuf[:, dstoff + f * 512:dstoff + (f + 1) * 512]
                if act is None:
                    return eng.copy(dst, pa)
                return eng.activation(dst, pa, act)
            return evac
        jobs.append(dict(
            mms=[("m", wbuf[:, c * (fchunks * 128) + f * 128:
                            c * (fchunks * 128) + (f + 1) * 128],
                  hbuf[:, srcoff + c * 512:srcoff + c * 512 + 512])
                 for c in range(cchunks)],
            out_w=512, engine="scalar", evac=mk()))
    mm_grid(p, jobs, psums, tag=f"h2{p.uid()}")


def emit_ssq_1024(p, srcT, sqs, ssg, onesb, out_ap, psums):
    sq2 = p.sem("t2sq")
    sm2 = p.sem("t2sm")
    se2 = p.sem("t2se")
    sd2 = p.sem("t2sd")
    b_sq, b_sm, b_se, b_sd = sq2.n, sm2.n, se2.n, sd2.n
    with p.nc.Block() as block:
        @block.scalar
        def _(s):
            i = 0
            for m in range(4):
                for c in range(8):
                    if i >= 2:
                        s.wait_ge(sm2.h, b_sm + i - 1)
                    sq2.inc(s.activation(sqs[:, (i % 2) * 512:(i % 2 + 1) * 512],
                                         chunkT(srcT, c, m * 512, 512), AF.Square), 1)
                    i += 1
        @block.tensor
        def _(t):
            i = 0
            for m in range(4):
                for c in range(8):
                    t.wait_ge(sq2.h, b_sq + i + 1)
                    if c == 0 and m >= 1:
                        t.wait_ge(se2.h, b_se + m)
                    sm2.inc(t.matmul(psums[2][0:1, 0:512], onesb[:, :],
                                     sqs[:, (i % 2) * 512:(i % 2 + 1) * 512],
                                     start=(c == 0), stop=(c == 7)), 1)
                    i += 1
        @block.vector
        def _(v):
            for m in range(4):
                v.wait_ge(sm2.h, b_sm + 8 * (m + 1))
                if m >= 2:
                    v.wait_ge(sd2.h, b_sd + 16 * (m - 1))
                se2.inc(v.tensor_copy(ssg[:, (m % 2) * 512:(m % 2 + 1) * 512],
                                      psums[2][0:1, 0:512]), 1)
        @block.sync
        def _(sync):
            for m in range(4):
                sync.wait_ge(se2.h, b_se + m + 1)
                sd2.inc(sync.dma_start(out=out_ap[:, m * 512:(m + 1) * 512],
                                       in_=ssg[:, (m % 2) * 512:(m % 2 + 1) * 512]), 16)
            sync.wait_ge(sd2.h, sd2.n)


def prep_T2(inputs, st):
    ident = host_ident128()
    woT = f32c(inputs["L1_ffn_wo"].T)
    n2 = f32c(inputs["L1_ffn_n2_w"].reshape(8, 128).T)
    gwT = f32c(inputs["L1_ffn_gate_w"].T)
    e0u = f32c(inputs["L1_ffn_e0_up"].T)
    e0d = f32c(inputs["L1_ffn_e0_down"].T)
    e1a = f32c(inputs["L1_ffn_e1_w1"].T)
    e1b = f32c(inputs["L1_ffn_e1_w2"].T)
    e1d = f32c(inputs["L1_ffn_e1_down"].T)
    e2a = f32c(inputs["L1_ffn_e2_l1"].T)
    e2b = f32c(inputs["L1_ffn_e2_l2"].T)
    e2c = f32c(inputs["L1_ffn_e2_l3"].T)
    e2d = f32c(inputs["L1_ffn_e2_l4"].T)
    e3a = f32c(inputs["L1_ffn_e3_down"].T)
    e3b = f32c(inputs["L1_ffn_e3_up"].T)
    e3c = f32c(inputs["L1_ffn_e3_out"].T)
    wq = f32c(inputs["L2_attn_Wq"].T)
    wk = f32c(inputs["L2_attn_Wk"].T)
    wv = f32c(inputs["L2_attn_Wv"].T)
    bq = f32c(inputs["L2_attn_bq"].reshape(8, 128).T)
    bk = f32c(inputs["L2_attn_bk"].reshape(8, 128).T)
    bv = f32c(inputs["L2_attn_bv"].reshape(8, 128).T)
    ao = np.zeros((D, B * S), BF)
    for c in range(NC):
        a = st["ao1_units"][c]
        for bi in range(2):
            for hl in range(2):
                h = 2 * c + hl
                ao[64 * h:64 * h + 64, bi * S:(bi + 1) * S] = a[bi, hl]
    maps = []
    for c in range(NC):
        t0 = c * TLOC
        maps.append(dict(
            x=st["x3"][c], aoT=np.ascontiguousarray(ao[:, t0:t0 + TLOC]),
            woT=woT, n2w=n2, gwT=gwT, e0upT=e0u, e0dnT=e0d, e1w1T=e1a,
            e1w2T=e1b, e1dnT=e1d, e2l1T=e2a, e2l2T=e2b, e2l3T=e2c, e2l4T=e2d,
            e3dnT=e3a, e3upT=e3b, e3outT=e3c, wqT2=wq, wkT2=wk, wvT2=wv,
            bq=bq, bk=bk, bv=bv, ident=ident))
    return maps



RT512 = float(np.sqrt(np.float64(512.0)))


def build_T3():
    p = Prog("T3")
    x_in = p.dram_in("x", [TLOC, D]).ap()
    q_in = p.dram_in("q2", [D, TLOC], BF16).ap()
    kf_in = p.dram_in("k2f", [D, S], BF16).ap()
    vf_in = p.dram_in("v2f", [S, D], BF16).ap()
    qs_in = p.dram_in("qss", [1, TLOC]).ap()
    ks_in = p.dram_in("kssf", [1, S]).ap()
    woT_in = p.dram_in("woT2", [D, D]).ap()
    bo_in = p.dram_in("bo", [128, 8]).ap()
    identi = p.dram_in("ident", [128, 128], BF16).ap()
    x_out = p.dram_out("x5", [TLOC, D]).ap()

    p.make_arena(209920)
    x_sb = p.sbuf_at("xsb", [128, NT * D], F32, 0)
    qsb = p.sbuf_at("qsb", [128, 8 * TLOC], BF16, 65536)
    at_tm = p.sbuf_at("at_tm", [128, NT * D], BF16, 98304)
    atT = p.sbuf_at("atT", [128, 8 * TLOC], BF16, 131072)
    wW = p.sbuf_at("wW", [128, 8 * 1024], BF16, 163840)
    ktile = p.sbuf_at("ktile", [128, 4 * 1024], BF16, 180224)
    vtile = p.sbuf_at("vtile", [128, 4 * 1024], BF16, 188416)
    pt = p.sbuf_at("pt", [128, 4 * 256], BF16, 196608)
    rdp = p.sbuf_at("rdp", [1, 256], BF16, 198656)
    rfp = p.sbuf_at("rfp", [1, 256], F32, 199680)
    rcol = p.sbuf_at("rcol", [128, 2], F32, 200704)
    qss_sb = p.sbuf_at("qss", [1, TLOC], BF16, 0)
    kss_sb = p.sbuf_at("kss", [1, S], BF16, 4096)
    csml = p.sbuf_at("csml", [1, 8], F32, 200832)
    cbf = p.sbuf_at("cbf", [1, 1], BF16, 200864)
    biasc = p.sbuf_at("biasc", [128, 1], F32, 200896)
    ones128 = p.sbuf_at("ones128", [1, 128], BF16, 200928)
    onesb = p.sbuf_at("onesb", [128, 1], BF16, 201184)
    bo8 = p.sbuf_at("bo8", [128, 8], F32, 201216)
    ident = p.sbuf_at("id", [128, 128], BF16, 201248)

    ps_s = [p.psum(f"pss{i}", [128, 256]) for i in range(2)]
    ps_o = [p.psum(f"pso{i}", [128, 512]) for i in range(4)]
    ps_d = p.psum("psd", [128, 512])
    psT = p.psum("psT", [128, 512], BF16)

    dma_block(p, [(ident[:, :], identi), (bo8[:, :], bo_in)])
    dma_block(p, [(qss_sb[:, :], qs_in), (kss_sb[:, :], ks_in)], queue="gpsimd")
    dma_block(p, [(qsb[:, c * TLOC:(c + 1) * TLOC], q_in[c * 128:(c + 1) * 128, :])
                  for c in range(8)], queue="gpsimd")
    with p.nc.Block() as block:
        @block.vector
        def _(v):
            v.memset(ones128[:, :], 1.0)
            v.memset(onesb[:, :], 1.0)

    fc = p.sem("t3fc")
    fcb = fc.n
    with p.nc.Block() as block:
        @block.vector
        def _(v):
            fc.inc(v.reduce_max(csml[:, 0:1], kss_sb[:, :], axis=AX), 1)
        @block.scalar
        def _(s):
            s.wait_ge(fc.h, fcb + 1)
            fc.inc(s.activation(csml[:, 3:4], csml[:, 0:1], AF.Sqrt), 1)

    ld = p.sem("t3ld")
    sc = p.sem("t3sc")
    ex = p.sem("t3ex")
    pv = p.sem("t3pv")
    e1 = p.sem("t3e1")
    tr3 = p.sem("t3tr")
    fl = p.sem("t3fl")
    NK = S // 128

    for m in range(8):
        fcb2 = fc.n
        with p.nc.Block() as blk:
            @blk.vector
            def _(v):
                fc.inc(v.reduce_sum(csml[:, 1:2], qss_sb[:, m * 256:(m + 1) * 256],
                                    axis=AX), 1)
                v.wait_ge(fc.h, fcb2 + 2)
                fc.inc(v.tensor_mul(csml[:, 4:5], csml[:, 2:3], csml[:, 3:4]), 1)
                v.wait_ge(fc.h, fcb2 + 3)
                fc.inc(v.tensor_scalar_mul(csml[:, 5:6], csml[:, 4:5],
                                           -1.0 / RT512), 1)
                v.wait_ge(fc.h, fcb2 + 4)
                fc.inc(v.tensor_copy(cbf[:, :], csml[:, 5:6]), 1)
            @blk.scalar
            def _(s):
                s.wait_ge(fc.h, fcb2 + 1)
                fc.inc(s.activation(csml[:, 2:3], csml[:, 1:2], AF.Sqrt,
                                    scale=1.0 / 256.0), 1)
            @blk.tensor
            def _(t):
                t.wait_ge(fc.h, fcb2 + 5)
                fc.inc(t.matmul(ps_d[0:128, 256:257], ones128[0:1, :], cbf[:, :],
                                start=True, stop=True), 1)
        with p.nc.Block() as blk:
            @blk.vector
            def _(v):
                v.tensor_copy(biasc[:, :], ps_d[0:128, 256:257])

        ld_b, sc_b, ex_b, pv_b, e1_b, tr_b = ld.n, sc.n, ex.n, pv.n, e1.n, tr3.n
        ksrc = kf_in.rearrange("(c p) k -> p c k", p=128)
        with p.nc.Block() as blk:
            @blk.sync
            def _(sync):
                for K in range(NK):
                    if K >= 4:
                        sync.wait_ge(pv.h, pv_b + 5 * (K - 3))
                    sl = K % 4
                    ld.inc(sync.dma_start(
                        out=ktile.ap().rearrange("p (s c k) -> p s c k", s=4, c=8)
                        [:, sl], in_=ksrc[:, :, 128 * K:128 * (K + 1)]), 16)
            @blk.gpsimd
            def _(g):
                for K in range(NK):
                    if K >= 4:
                        g.wait_ge(pv.h, pv_b + 5 * (K - 3))
                    sl = K % 4
                    ld.inc(g.dma_start(
                        out=vtile[:, sl * 1024:(sl + 1) * 1024],
                        in_=vf_in[128 * K:(K + 1) * 128, :]), 16)

            @blk.tensor
            def _(tensor):
                def emit_pv(pK):
                    psl = pK % 4
                    tensor.wait_ge(ex.h, ex_b + pK + 1)
                    for half in range(2):
                        for qt in range(2):
                            pv.inc(tensor.matmul(
                                ps_o[2 * qt + half][:, 0:512],
                                pt[:, psl * 256 + qt * 128:psl * 256 + (qt + 1) * 128],
                                vtile[:, psl * 1024 + half * 512:psl * 1024 + (half + 1) * 512],
                                start=(pK == 0), stop=(pK == NK - 1)), 1)
                    pv.inc(tensor.matmul(
                        ps_d[0:1, 0:256], onesb[:, :],
                        pt[:, psl * 256:(psl + 1) * 256],
                        start=(pK == 0), stop=(pK == NK - 1)), 1)

                for K in range(NK):
                    sl = K % 2
                    tensor.wait_ge(ld.h, ld_b + 32 * (K + 1))
                    if K >= 2:
                        tensor.wait_ge(ex.h, ex_b + K - 1)
                    last = None
                    for c in range(8):
                        last = tensor.matmul(
                            ps_s[sl][:, 0:256],
                            ktile[:, (K % 4) * 1024 + c * 128:(K % 4) * 1024 + (c + 1) * 128],
                            chunkT(qsb, c, m * 256, 256),
                            start=(c == 0), stop=(c == 7))
                    sc.inc(last, 1)
                    if K >= 1:
                        emit_pv(K - 1)
                emit_pv(NK - 1)
                tensor.wait_ge(e1.h, e1_b + 1)
                tr3.inc(tensor.transpose(psT[0:128, 0:1], rdp[0:1, 0:128],
                                         ident[0:1, 0:1]), 1)
                tr3.inc(tensor.transpose(psT[0:128, 2:3], rdp[0:1, 128:256],
                                         ident[0:1, 0:1]), 1)

            @blk.scalar
            def _(scalar):
                for K in range(NK):
                    sl = K % 2
                    scalar.wait_ge(sc.h, sc_b + K + 1)
                    if K >= 4:
                        scalar.wait_ge(pv.h, pv_b + 5 * (K - 3))
                    ex.inc(scalar.activation(pt[:, (K % 4) * 256:(K % 4 + 1) * 256],
                                             ps_s[sl][:, 0:256], AF.Exp,
                                             bias=biasc[:, 0:1], scale=1.0 / RT512), 1)
                scalar.wait_ge(pv.h, pv_b + 5 * NK)
                scalar.activation(rfp[:, :], ps_d[0:1, 0:256], AF.Ln)
                e1.inc(scalar.activation(rdp[0:1, 0:256], rfp[:, :], AF.Exp,
                                         scale=-1.0), 1)

            @blk.vector
            def _(v):
                v.wait_ge(tr3.h, tr_b + 2)
                v.tensor_copy(rcol[:, 0:1], psT[0:128, 0:1])
                fl.inc(v.tensor_copy(rcol[:, 1:2], psT[0:128, 2:3]), 1)
                v.wait_ge(fl.h, fl.n)
                for qt in range(2):
                    tok = 2 * m + qt
                    for half in range(2):
                        for cc in range(4):
                            c = half * 4 + cc
                            v.tensor_single_scalar(
                                at_tm[:, tok * D + c * 128:tok * D + (c + 1) * 128],
                                ps_o[2 * qt + half][:, cc * 128:(cc + 1) * 128],
                                rcol[:, qt:qt + 1], ALU.mult)

    import os as _os
    if _os.environ.get("T3_DBG", "0") == "1":
        dbg = p.dram_out("dbg_at", [TLOC, D], BF16).ap()
        dma_block(p, [(dbg.rearrange("(t p) c -> p t c", p=128),
                       at_tm.ap().rearrange("p (t c) -> p t c", t=NT))])
    emit_load_x(p, x_in, x_sb)
    emit_to_xT(p, at_tm, atT, ident, psT, psT)
    ld_wT(p, wW, woT_in, 8, 1024)
    jobs = []
    for f in range(8):
        for mm_ in range(4):
            def mk(f=f, mm_=mm_):
                def evac(eng, pa):
                    return eng.tensor_single_scalar(chunkT(qsb, f, mm_ * 512, 512),
                                                    pa, bo8[:, f:f + 1], ALU.add)
                return evac
            jobs.append(dict(
                mms=[("m", wW[:, c * 1024 + f * 128:c * 1024 + (f + 1) * 128],
                      chunkT(atT, c, mm_ * 512, 512)) for c in range(8)],
                out_w=512, engine="vector", evac=mk()))
    mm_grid(p, jobs, [ps_o[0], ps_o[1]], tag="l2wo")
    emit_addT_into_x(p, qsb, x_sb, ident, [psT])
    emit_ln(p, x_sb)
    emit_store_x(p, x_sb, x_out)
    return p


def prep_T3(inputs, st):
    ident = host_ident128()
    woT = f32c(inputs["L2_attn_Wo"].T)
    bo = f32c(inputs["L2_attn_bo"].reshape(8, 128).T)
    k2f = [np.concatenate([st["k2"][c] for c in range(4 * bq, 4 * bq + 4)], axis=1)
           for bq in range(2)]
    v2f = [np.concatenate([st["v2"][c] for c in range(4 * bq, 4 * bq + 4)], axis=0)
           for bq in range(2)]
    ksf = [np.concatenate([st["kss2"][c] for c in range(4 * bq, 4 * bq + 4)], axis=1)
           for bq in range(2)]
    maps = []
    for c in range(NC):
        bq = c // 4
        maps.append(dict(x=st["x4"][c], q2=st["q2"][c], k2f=k2f[bq], v2f=v2f[bq],
                         qss=st["qss2"][c], kssf=ksf[bq], woT2=woT, bo=bo,
                         ident=ident))
    return maps




def build_T4():
    p = Prog("T4")
    x_in = p.dram_in("x", [TLOC, D]).ap()
    xb_in = p.dram_in("xbound", [128, 8]).ap()
    mur_in = p.dram_in("mur", [128, 8]).ap()
    muk_in = p.dram_in("muk", [128, 8]).ap()
    wrT_in = p.dram_in("wrT", [D, D]).ap()
    wkT_in = p.dram_in("wkT", [D, D]).ap()
    wvT_in = p.dram_in("wvT", [D, D]).ap()
    wq3_in = p.dram_in("wq3", [D, D]).ap()
    wk3_in = p.dram_in("wk3", [D, 256]).ap()
    wv3_in = p.dram_in("wv3", [D, 256]).ap()
    identi = p.dram_in("ident", [128, 128], BF16).ap()

    x_out = p.dram_out("x6", [TLOC, D]).ap()
    q3_out = p.dram_out("q3", [D, TLOC], BF16).ap()
    k3_out = p.dram_out("k3", [256, TLOC], BF16).ap()
    v3_out = p.dram_out("v3", [256, TLOC], BF16).ap()
    qs_out = p.dram_out("qss3", [16, TLOC]).ap()
    ks_out = p.dram_out("kss3", [4, TLOC]).ap()

    p.make_arena(212480)
    x_sb = p.sbuf_at("xsb", [128, NT * D], F32, 0)
    xT = p.sbuf_at("xT", [128, 8 * TLOC], BF16, 65536)
    xsh = p.sbuf_at("xsh", [128, 8 * TLOC], BF16, 98304)
    mx = p.sbuf_at("mx", [128, 8 * TLOC], BF16, 131072)
    wW = p.sbuf_at("wW", [128, 8 * 1024], BF16, 163840)
    rT = p.sbuf_at("rT", [128, 8 * TLOC], BF16, 65536)
    stg = p.sbuf_at("stg", [128, 4 * 512], BF16, 180224)
    mrc = p.sbuf_at("mrc", [128, 8], F32, 184320)
    mrc1 = p.sbuf_at("mrc1", [128, 8], F32, 184352)
    mkc = p.sbuf_at("mkc", [128, 8], F32, 184384)
    mkc1 = p.sbuf_at("mkc1", [128, 8], F32, 184416)
    bcol = p.sbuf_at("bcol", [128, 8], BF16, 184448)
    bd = p.sbuf_at("bd", [128, 2], BF16, 184480)
    sq = p.sbuf_at("sq", [128, 2 * 512], BF16, 184512)
    ssg = p.sbuf_at("ssg", [4, 2 * 512], F32, 186560)
    ident = p.sbuf_at("id", [128, 128], BF16, 190656)
    psums = [p.psum(f"ps{i}", [128, 512]) for i in range(4)]
    psb16 = [p.psum(f"psb{i}", [128, 512], BF16) for i in range(2)]

    emit_load_x(p, x_in, x_sb)
    dma_block(p, [(ident[:, :], identi), (mrc[:, :], mur_in), (mkc[:, :], muk_in)])
    dma_block(p, [(bcol[:, :], xb_in)], queue="gpsimd")
    with p.nc.Block() as block:
        @block.vector
        def _(v):
            v.memset(bd[:, :], 0.0)
            v.memset(bd[0:64, 0:1], 1.0)
            v.memset(bd[64:128, 1:2], 1.0)
            v.tensor_scalar(mrc1[:, :], mrc[:, :], -1.0, 1.0, ALU.mult, ALU.add)
            v.tensor_scalar(mkc1[:, :], mkc[:, :], -1.0, 1.0, ALU.mult, ALU.add)
    emit_bf16_tiles(p, x_sb, xsh)
    emit_to_xT(p, xsh, xT, ident, psb16[0], psb16[1])
    with p.nc.Block() as block:
        @block.vector
        def _(v):
            for c in range(8):
                v.tensor_copy(xsh[:, c * TLOC + 1:(c + 1) * TLOC],
                              xT[:, c * TLOC:(c + 1) * TLOC - 1])
                v.tensor_copy(xsh[:, c * TLOC:c * TLOC + 1], bcol[:, c:c + 1])
    def mix(dst, mc, mc1):
        with p.nc.Block() as block:
            @block.vector
            def _(v):
                for c in range(8):
                    for mm_ in range(4):
                        sl_x = xT[:, c * TLOC + mm_ * 512:c * TLOC + (mm_ + 1) * 512]
                        sl_p = xsh[:, c * TLOC + mm_ * 512:c * TLOC + (mm_ + 1) * 512]
                        dd = dst[:, c * TLOC + mm_ * 512:c * TLOC + (mm_ + 1) * 512]
                        v.tensor_single_scalar(dd, sl_p, mc1[:, c:c + 1], ALU.mult)
                        v.scalar_tensor_tensor(dd, sl_x, mc[:, c:c + 1], dd,
                                               ALU.mult, ALU.add)
    mix(mx, mrc, mrc1)
    mix(xsh, mkc, mkc1)
    ld_wT(p, wW, wrT_in, 8, 1024)
    jobs = []
    for f in range(8):
        for m in range(4):
            def mk(f=f, m=m):
                def evac(eng, pa):
                    return eng.activation(chunkT(rT, f, m * 512, 512), pa, AF.Sigmoid)
                return evac
            jobs.append(dict(
                mms=[("m", wW[:, c * 1024 + f * 128:c * 1024 + (f + 1) * 128],
                      chunkT(mx, c, m * 512, 512)) for c in range(8)],
                out_w=512, engine="scalar", evac=mk()))
    mm_grid(p, jobs, psums, tag="l2fr")
    ld_wT(p, wW, wkT_in, 8, 1024)
    kaT = mx
    jobs = []
    for f in range(8):
        for m in range(4):
            def mk(f=f, m=m):
                def evac(eng, pa):
                    return eng.activation(chunkT(kaT, f, m * 512, 512), pa, AF.Relu)
                return evac
            jobs.append(dict(
                mms=[("m", wW[:, c * 1024 + f * 128:c * 1024 + (f + 1) * 128],
                      chunkT(xsh, c, m * 512, 512)) for c in range(8)],
                out_w=512, engine="scalar", evac=mk()))
    mm_grid(p, jobs, psums, tag="l2fk")
    with p.nc.Block() as block:
        @block.scalar
        def _(s):
            for c in range(8):
                for m in range(4):
                    sl = kaT[:, c * TLOC + m * 512:c * TLOC + (m + 1) * 512]
                    s.activation(sl, sl, AF.Square)
    ld_wT(p, wW, wvT_in, 8, 1024)
    jobs = []
    for f in range(8):
        for m in range(4):
            def mk(f=f, m=m):
                def evac(eng, pa):
                    return eng.tensor_tensor(chunkT(xsh, f, m * 512, 512),
                                             chunkT(rT, f, m * 512, 512), pa,
                                             ALU.mult)
                return evac
            jobs.append(dict(
                mms=[("m", wW[:, c * 1024 + f * 128:c * 1024 + (f + 1) * 128],
                      chunkT(kaT, c, m * 512, 512)) for c in range(8)],
                out_w=512, engine="vector", evac=mk()))
    mm_grid(p, jobs, psums, tag="l2fv")
    emit_addT_into_x(p, xsh, x_sb, ident, psb16)
    emit_ln(p, x_sb)
    emit_store_x(p, x_sb, x_out)

    emit_bf16_tiles(p, x_sb, xsh)
    emit_to_xT(p, xsh, xT, ident, psb16[0], psb16[1])

    def ld_w(wap, nchunk, ncols):
        dma_block(p, [(wW[:, c * ncols:(c + 1) * ncols],
                       wap[c * 128:(c + 1) * 128, :]) for c in range(nchunk)],
                  queue="gpsimd")

    for wap, fch, out_ap in ((wq3_in, 8, q3_out), (wk3_in, 2, k3_out),
                             (wv3_in, 2, v3_out)):
        ld_w(wap, 8, fch * 128)
        jobs = []
        ji = 0
        for f in range(fch):
            for m in range(4):
                sl = ji % 4
                ji += 1
                def mk(f=f, m=m, sl=sl, out_ap=out_ap, fch=fch):
                    def evac(eng, pa):
                        eng.copy(chunkT(mx, f, m * 512, 512), pa)
                        return eng.copy(stg[:, sl * 512:(sl + 1) * 512],
                                        chunkT(mx, f, m * 512, 512))
                    return evac
                jobs.append(dict(
                    mms=[("m", wW[:, c * fch * 128 + f * 128:
                                  c * fch * 128 + (f + 1) * 128],
                          chunkT(xT, c, m * 512, 512)) for c in range(8)],
                    out_w=512, engine="scalar", evac=mk(),
                    dma=(out_ap[f * 128:(f + 1) * 128, m * 512:(m + 1) * 512],
                         stg[:, sl * 512:(sl + 1) * 512])))
        mm_grid(p, jobs, psums, tag=f"l3p{fch}{p.uid()}", nstage=4)
        nheads2 = fch
        ss_ap = qs_out if out_ap is q3_out else (ks_out if out_ap is k3_out else None)
        if ss_ap is not None:
            emit_ssq_heads(p, mx, sq, ssg, bd, ss_ap, psums, fch)
    return p


def emit_ssq_heads(p, srcT, sq, ssg, bd, out_ap, psums, fch):
    a = p.sem("t4a")
    b = p.sem("t4b")
    c_ = p.sem("t4c")
    d_ = p.sem("t4d")
    ab, bb, cb, db = a.n, b.n, c_.n, d_.n
    with p.nc.Block() as block:
        @block.scalar
        def _(s):
            i = 0
            for ch in range(fch):
                for m in range(4):
                    if i >= 2:
                        s.wait_ge(b.h, bb + i - 1)
                    a.inc(s.activation(sq[:, (i % 2) * 512:(i % 2 + 1) * 512],
                                       chunkT(srcT, ch, m * 512, 512), AF.Square), 1)
                    i += 1
        @block.tensor
        def _(t):
            i = 0
            for ch in range(fch):
                for m in range(4):
                    t.wait_ge(a.h, ab + i + 1)
                    if i >= 1:
                        t.wait_ge(c_.h, cb + i)
                    b.inc(t.matmul(psums[2][0:2, 0:512], bd[:, 0:2],
                                   sq[:, (i % 2) * 512:(i % 2 + 1) * 512],
                                   start=True, stop=True), 1)
                    i += 1
        @block.vector
        def _(v):
            i = 0
            for ch in range(fch):
                for m in range(4):
                    v.wait_ge(b.h, bb + i + 1)
                    if i >= 2:
                        v.wait_ge(d_.h, db + 16 * (i - 1))
                    c_.inc(v.tensor_copy(ssg[0:2, (i % 2) * 512:(i % 2 + 1) * 512],
                                         psums[2][0:2, 0:512]), 1)
                    i += 1
        @block.sync
        def _(sync):
            i = 0
            for ch in range(fch):
                for m in range(4):
                    sync.wait_ge(c_.h, cb + i + 1)
                    d_.inc(sync.dma_start(
                        out=out_ap[2 * ch:2 * ch + 2, m * 512:(m + 1) * 512],
                        in_=ssg[0:2, (i % 2) * 512:(i % 2 + 1) * 512]), 16)
                    i += 1
            sync.wait_ge(d_.h, d_.n)


def prep_T4(inputs, st):
    ident = host_ident128()
    maps = []
    wrT = f32c(inputs["L2_ffn_W_r"].T)
    wkT = f32c(inputs["L2_ffn_W_k"].T)
    wvT = f32c(inputs["L2_ffn_W_v"].T)
    mur = f32c(inputs["L2_ffn_mu_r"].reshape(8, 128).T)
    muk = f32c(inputs["L2_ffn_mu_k"].reshape(8, 128).T)
    wq3 = f32c(inputs["L3_attn_Wq"])
    wk3 = f32c(inputs["L3_attn_Wk"])
    wv3 = f32c(inputs["L3_attn_Wv"])
    for c in range(NC):
        if c % 4 == 0:
            xb = np.zeros((128, 8), np.float32)
        else:
            xb = f32c(st["x5"][c - 1][TLOC - 1].reshape(8, 128).T)
        maps.append(dict(x=st["x5"][c], xbound=xb, mur=mur, muk=muk, wrT=wrT,
                         wkT=wkT, wvT=wvT, wq3=wq3, wk3=wk3, wv3=wv3,
                         ident=ident))
    return maps




def build_F3():
    p = Prog("F3")
    q_in = p.dram_in("q", [2, 2, 64, S], BF16).ap()
    k_in = p.dram_in("k", [2, 64, S], BF16).ap()
    v_in = p.dram_in("v", [2, S, 64], BF16).ap()
    qs_in = p.dram_in("qss", [2, 2, S]).ap()
    ks_in = p.dram_in("kss", [2, S]).ap()
    m_in = p.dram_in("masks", [4, 128, 512], BF16).ap()
    ao_o = p.dram_out("ao", [2, 2, 64, S], BF16).ap()

    q_sb = [p.sbuf(f"q{i}", [96, S], BF16) for i in range(2)]
    k_sb = [p.sbuf(f"k{i}", [96, S], BF16) for i in range(2)]
    vh_sb = [p.sbuf(f"vh{i}", [128, 64 * 65], BF16) for i in range(2)]
    kssb = p.sbuf("kssb", [1, S], BF16)
    qssb = p.sbuf("qssb", [1, S], BF16)
    msk = p.sbuf("msk", [128, 4 * 512], BF16)
    ones64 = p.sbuf("ones64", [1, 64], BF16)
    ones128 = p.sbuf("ones128", [1, 128], BF16)
    csml = p.sbuf("csml", [1, 144], F32)
    crowb = p.sbuf("crowb", [1, 64], BF16)
    biasm = p.sbuf("biasm", [128, 64], F32)
    pt = [p.sbuf(f"pt{i}", [128, 512], BF16) for i in range(8)]
    stg = [dict(rd=p.sbuf(f"rd{t}", [1, 2 * 512], BF16),
                rf=p.sbuf(f"rf{t}", [1, 512], F32),
                num=p.sbuf(f"num{t}", [64, 2 * 512], BF16),
                ao=p.sbuf(f"aos{t}", [64, 2 * 512], BF16)) for t in range(2)]
    ps_s = [[p.psum(f"pss{t}{i}", [128, 512]) for i in range(2)] for t in range(2)]
    ps_o = [p.psum(f"pso{t}", [65, 512]) for t in range(2)]
    ps_b = [p.psum(f"psb{t}", [128, 512]) for t in range(2)]

    units = [(b, h) for b in range(2) for h in range(2)]
    lds = [p.sem("fldA"), p.sem("fldB")]
    ms = p.sem("fms")
    kl = p.sem("fkl")
    fcc = p.sem("fcs")
    fs2 = p.sem("fsq")
    fq = p.sem("fqr")
    fsq = p.sem("fq2")
    fvx = p.sem("fvx")
    fm2 = p.sem("fm2")
    fv2 = p.sem("fv2")
    dma_block(p, [(msk[:, 512 * r:512 * (r + 1)], m_in[r]) for r in range(4)])

    with p.nc.Block() as block:
        @block.vector
        def _(v):
            v.memset(ones64[:, :], 1.0)
            v.memset(ones128[:, :], 1.0)
            for t in range(2):
                v.memset(q_sb[t][64:96, :], 0.0)
                v.memset(k_sb[t][64:96, :], 0.0)
            for t in range(2):
                ms.inc(v.memset(vh_sb[t][:, :], 1.0), 1)

    kb = kl.n
    fb = fcc.n
    with p.nc.Block() as block:
        @block.gpsimd
        def _(g):
            for b in range(2):
                if b == 1:
                    g.wait_ge(fcc.h, fb + 1)
                kl.inc(g.dma_start(out=kssb[:, :], in_=ks_in[b:b + 1, :]), 16)

        @block.vector
        def _(v):
            for b in range(2):
                v.wait_ge(kl.h, kb + 16 * (b + 1))
                fcc.inc(v.reduce_max(csml[:, b:b + 1], kssb[:, :], axis=AX), 1)

        @block.scalar
        def _(s):
            for b in range(2):
                s.wait_ge(fcc.h, fb + b + 1)
                fs2.inc(s.activation(csml[:, 2 + b:3 + b], csml[:, b:b + 1],
                                     AF.Sqrt, scale=1.0 / 64.0), 1)

    with p.nc.Block() as block:
        @block.vector
        def _(v):
            v.wait_ge(fs2.h, 2)
            for b in range(2):
                v.tensor_scalar_mul(csml[:, 4 + b:5 + b], csml[:, 2 + b:3 + b],
                                    -1.0)

    qb = kl.n
    with p.nc.Block() as block:
        @block.gpsimd
        def _(g):
            for u, (b, h) in enumerate(units):
                if u >= 1:
                    g.wait_ge(fq.h, u)
                kl.inc(g.dma_start(out=qssb[:, :], in_=qs_in[b:b + 1, h, :]), 16)

        @block.vector
        def _(v):
            for u in range(4):
                v.wait_ge(kl.h, qb + 16 * (u + 1))
                qv = bass.AP(qssb, 0, [[S, 1], [512, 16], [1, 512]])
                fq.inc(v.reduce_sum(csml[:, 8 + 16 * u:24 + 16 * u], qv, axis=AX), 1)

        @block.scalar
        def _(s):
            for u in range(4):
                s.wait_ge(fq.h, u + 1)
                fsq.inc(s.activation(csml[:, 72 + 16 * u:88 + 16 * u],
                                     csml[:, 8 + 16 * u:24 + 16 * u],
                                     AF.Sqrt, scale=1.0 / 512.0), 1)

    with p.nc.Block() as block:
        @block.vector
        def _(v):
            v.wait_ge(fsq.h, 4)
            for u, (b, h) in enumerate(units):
                fvx.inc(v.tensor_single_scalar(
                    crowb[:, 16 * u:16 * u + 16],
                    csml[:, 72 + 16 * u:88 + 16 * u],
                    csml[:, 4 + b:5 + b], ALU.mult), 1)

    with p.nc.Block() as block:
        @block.tensor
        def _(t):
            for u in range(4):
                if u >= 2:
                    t.wait_ge(fv2.h, u - 1)
                fm2.inc(t.matmul(ps_b[u % 2][0:128, 0:16], ones128[0:1, :],
                                 crowb[:, 16 * u:16 * u + 16],
                                 start=True, stop=True), 1)

        @block.vector
        def _(v):
            for u in range(4):
                v.wait_ge(fm2.h, u + 1)
                fv2.inc(v.tensor_copy(biasm[:, 16 * u:16 * u + 16],
                                      ps_b[u % 2][0:128, 0:16]), 1)

    with p.nc.Block() as block:
        @block.sync
        def _(sync):
            for t in range(2):
                lds[0].inc(sync.dma_start(out=k_sb[t][0:64, :], in_=k_in[t]), 16)
            for u in (0, 1):
                b, h = units[u]
                lds[0].inc(sync.dma_start(out=q_sb[u][0:64, :], in_=q_in[b, h]), 16)

        @block.gpsimd
        def _(g):
            g.wait_ge(ms.h, 2)
            for t in range(2):
                dst = vh_sb[t].ap().rearrange("p (k c) -> p k c", c=65)[:, :, 0:64]
                src = v_in[t].rearrange("(k p) c -> p k c", p=128)
                lds[1].inc(g.dma_start(out=dst, in_=src), 16)

    for pi in range(2):
        if pi == 1:
            with p.nc.Block() as block:
                @block.sync
                def _(sync):
                    for u in (2, 3):
                        b, h = units[u]
                        lds[0].inc(sync.dma_start(out=q_sb[u % 2][0:64, :],
                                                  in_=q_in[b, h]), 16)
                    sync.wait_ge(lds[0].h, lds[0].n)
        US = []
        for t in range(2):
            u = 2 * pi + t
            b, h = units[u]
            US.append(dict(
                dq=96, q=q_sb[t], k=k_sb[b], vhat=vh_sb[b],
                pt=pt[4 * t:4 * t + 4], ps_s=ps_s[t], ps_o=ps_o[t],
                ps_b=ps_b[t],
                bias=(lambda m, u=u: biasm[:, 16 * u + m:16 * u + m + 1]),
                ao_out=lambda m, b=b, h=h: ao_o[b, h, :, 512 * m:512 * (m + 1)],
                sems=flash_sems(p, t), **stg[t]))
        emit_flash_pair(p, US, msk=msk, ones64=ones64, exp_scale=0.125,
                        clip=None,
                        wait_lds=[(lds[0], lds[0].n), (lds[1], 32)])
    return p


def prep_F3(st):
    maps = []
    masks = host_band_masks()
    q3, k3, v3, qss, kss = st["q3"], st["k3"], st["v3"], st["qss3"], st["kss3"]
    for c in range(NC):
        kv = c // 2
        q = np.zeros((2, 2, 64, S), BF)
        k = np.zeros((2, 64, S), BF)
        vv = np.zeros((2, S, 64), BF)
        qs = np.zeros((2, 2, S), np.float32)
        ks = np.zeros((2, S), np.float32)
        for bi in range(2):
            k[bi] = k3[64 * kv:64 * kv + 64, bi * S:(bi + 1) * S]
            vv[bi] = v3[64 * kv:64 * kv + 64, bi * S:(bi + 1) * S].T
            ks[bi] = kss[kv, bi * S:(bi + 1) * S]
            for hl in range(2):
                h = 2 * c + hl
                q[bi, hl] = q3[64 * h:64 * h + 64, bi * S:(bi + 1) * S]
                qs[bi, hl] = qss[h, bi * S:(bi + 1) * S]
        maps.append(dict(q=q, k=k, v=vv, qss=qs, kss=ks, masks=masks))
    return maps




def build_T5():
    import os
    PH5 = int(os.environ.get("T5_PHASES", "9"))
    p = Prog("T5")
    x_in = p.dram_in("x", [TLOC, D]).ap()
    ao_in = p.dram_in("aoT", [D, TLOC], BF16).ap()
    wo3_in = p.dram_in("wo3", [D, D]).ap()
    kern_in = p.dram_in("kern", [D, D]).ap()
    b3_in = p.dram_in("bias3", [128, 8]).ap()
    a3_in = p.dram_in("a3", [D, 16]).ap()
    bl_in = p.dram_in("b3lora", [16, D]).ap()
    identi = p.dram_in("ident", [128, 128], BF16).ap()
    identfi = p.dram_in("identf", [128, 128]).ap()
    x_out = p.dram_out("x8", [TLOC, D]).ap()

    p.make_arena(204800)
    x_sb = p.sbuf_at("xsb", [128, NT * D], F32, 0)
    bufA = p.sbuf_at("bufA", [128, 8 * TLOC], BF16, 65536)
    bufB = p.sbuf_at("bufB", [128, 8 * TLOC], BF16, 98304)
    bufC = p.sbuf_at("bufC", [128, 8 * TLOC], BF16, 131072)
    wW = p.sbuf_at("wW", [128, 8 * 1024], BF16, 163840)
    qf = p.sbuf_at("qf", [128, 512], F32, 180224)
    qi = p.sbuf_at("qi", [128, 512], I32, 182272)
    mc8 = p.sbuf_at("mc8", [128, 16], F32, 184320)
    row1 = p.sbuf_at("row1", [1, 128], F32, 184416)
    g11 = p.sbuf_at("g11", [1, 8], F32, 184928)
    onesf = p.sbuf_at("onesf", [1, 128], F32, 184960)
    bsc = p.sbuf_at("bsc", [128, 2], F32, 185536)

    la = p.sbuf_at("la", [16, TLOC], BF16, 185600)
    b3l = p.sbuf_at("b3l", [16, D], BF16, 189696)
    bi8 = p.sbuf_at("bi8", [128, 8], F32, 191744)
    ones128 = p.sbuf_at("ones128", [1, 128], BF16, 191776)
    ident = p.sbuf_at("id", [128, 128], BF16, 192032)
    identf = p.sbuf_at("identf", [128, 128], F32, 192288)
    psums = [p.psum(f"ps{i}", [128, 512]) for i in range(4)]
    psb16 = [p.psum(f"psb{i}", [128, 512], BF16) for i in range(2)]

    emit_load_x(p, x_in, x_sb)
    dma_block(p, [(ident[:, :], identi), (bi8[:, :], b3_in)])
    dma_block(p, [(identf[:, :], identfi)])
    dma_block(p, [(bufA[:, c * 2048:(c + 1) * 2048],
                   ao_in[c * 128:(c + 1) * 128, :]) for c in range(8)]
              + [(b3l[:, :], bl_in)], queue="gpsimd")
    with p.nc.Block() as block:
        @block.vector
        def _(v):
            v.memset(ones128[:, :], 1.0)
            v.memset(onesf[:, :], 1.0)
    ld_wT(p, wW, wo3_in, 8, 1024)
    jobs = []
    for f in range(8):
        for m in range(4):
            def mk(f=f, m=m):
                def evac(eng, pa):
                    return eng.copy(chunkT(bufB, f, m * 512, 512), pa)
                return evac
            jobs.append(dict(
                mms=[("m", wW[:, c * 1024 + f * 128:c * 1024 + (f + 1) * 128],
                      bufA[:, c * 2048 + m * 512:c * 2048 + (m + 1) * 512])
                     for c in range(8)],
                out_w=512, engine="scalar", evac=mk()))
    mm_grid(p, jobs, psums, tag="wo3")
    emit_addT_into_x(p, bufB, x_sb, ident, psb16)
    emit_ln(p, x_sb)
    if PH5 < 2:
        emit_store_x(p, x_sb, x_out)
        return p
    emit_bf16_tiles(p, x_sb, bufB)
    emit_to_xT(p, bufB, bufC, ident, psb16[0], psb16[1])

    kernf = p.sbuf_at("kernf", [128, 8 * 1024], F32, 98304)
    dma_block(p, [(kernf[:, c * 1024:(c + 1) * 1024],
                   kern_in[c * 128:(c + 1) * 128, :]) for c in range(8)])
    fq = p.sem("t5fq")
    fqb = fq.n
    with p.nc.Block() as block:
        @block.vector
        def _(v):
            def flush(ins):
                fq.inc(ins, 1)
                v.wait_ge(fq.h, fq.n)
            for c in range(7):
                v.reduce_max(mc8[:, c:c + 1], kernf[:, c * 1024:(c + 1) * 1024],
                             axis=AX, apply_absolute_value=True)
            flush(v.reduce_max(mc8[:, 7:8], kernf[:, 7 * 1024:8 * 1024],
                               axis=AX, apply_absolute_value=True))
            flush(v.reduce_max(mc8[:, 8:9], mc8[:, 0:8], axis=AX))
            fq.inc(v.engine_nop(), 1)
        @block.tensor
        def _(t):
            t.wait_ge(fq.h, fqb + 3)
            fq.inc(t.transpose(psums[0][0:1, 0:128], mc8[:, 8:9], identf[:, :]), 1)
        @block.scalar
        def _(s):
            pass
    with p.nc.Block() as block:
        @block.vector
        def _(v):
            def flush(ins):
                fq.inc(ins, 1)
                v.wait_ge(fq.h, fq.n)
            flush(v.tensor_copy(row1[:, :], psums[0][0:1, 0:128]))
            flush(v.reduce_max(g11[:, 0:1], row1[:, :], axis=AX))
            v.wait_ge(fq.h, fq.n + 2)
            flush(v.tensor_single_scalar(g11[:, 3:4], g11[:, 0:1], 1.0 / 7.0,
                                         ALU.mult))
        @block.scalar
        def _(s):
            s.wait_ge(fq.h, fqb + 6)
            fq.inc(s.activation(g11[:, 1:2], g11[:, 0:1], AF.Ln, scale=1.0 / 7.0), 1)
            fq.inc(s.activation(g11[:, 2:3], g11[:, 1:2], AF.Exp, scale=-1.0), 1)
    with p.nc.Block() as block:
        @block.tensor
        def _(t):
            t.matmul(psums[0][0:128, 0:2], onesf[0:1, :], g11[:, 2:4],
                     start=True, stop=True)
        @block.vector
        def _(v):
            pass
    with p.nc.Block() as block:
        @block.vector
        def _(v):
            v.tensor_copy(bsc[:, 1:2], psums[0][0:128, 0:1])
            v.tensor_copy(bsc[:, 0:1], psums[0][0:128, 1:2])
    with p.nc.Block() as block:
        @block.vector
        def _(v):
            def flush(ins):
                fq.inc(ins, 1)
                v.wait_ge(fq.h, fq.n)
            for i in range(16):
                srcf = kernf[:, i * 512:(i + 1) * 512]
                dst = bufA[:, i * 512:(i + 1) * 512]
                flush(v.tensor_single_scalar(qf[:, :], srcf, bsc[:, 1:2], ALU.mult))
                flush(v.tensor_copy(qi[:, :], qf[:, :]))
                flush(v.tensor_copy(qf[:, :], qi[:, :]))
                flush(v.tensor_scalar(qf[:, :], qf[:, :], 7.0, -8.0, ALU.min,
                                      ALU.max))
                flush(v.tensor_single_scalar(dst, qf[:, :], bsc[:, 0:1], ALU.mult))

    if os.environ.get("T5_DBG", "0") == "1":
        dbgw = p.dram_out("dbg_wq", [D, D], BF16).ap()
        dma_block(p, [(dbgw[c * 128:(c + 1) * 128, :], bufA[:, c * 1024:(c + 1) * 1024])
                      for c in range(8)])
    ld_wT(p, wW, a3_in, 8, 16)
    jobs = []
    for m in range(4):
        def mka(m=m):
            def evac(eng, pa):
                return eng.copy(la[0:16, m * 512:(m + 1) * 512], pa)
            return evac
        jobs.append(dict(mms=[("m", wW[:, c * 16:(c + 1) * 16],
                               chunkT(bufC, c, m * 512, 512)) for c in range(8)],
                         out_w=512, rows=16, engine="scalar", evac=mka()))
    mm_grid(p, jobs, psums[:2], tag="lora")
    jobs = []
    for f in range(8):
        for m in range(4):
            def mk(f=f, m=m):
                def evac(eng, pa):
                    return eng.tensor_single_scalar(chunkT(bufB, f, m * 512, 512),
                                                    pa, bi8[:, f:f + 1], ALU.add)
                return evac
            mms = [("m", bufA[:, c * 1024 + f * 128:c * 1024 + (f + 1) * 128],
                    chunkT(bufC, c, m * 512, 512)) for c in range(8)]
            mms.append(("m", b3l[0:16, f * 128:(f + 1) * 128],
                        la[0:16, m * 512:(m + 1) * 512]))
            jobs.append(dict(mms=mms, out_w=512, engine="vector", evac=mk()))
    mm_grid(p, jobs, psums, tag="l3f")
    emit_addT_into_x(p, bufB, x_sb, ident, psb16)
    emit_ln(p, x_sb)
    emit_store_x(p, x_sb, x_out)
    return p


def prep_T5(inputs, st):
    ident = host_ident128()
    wo3 = f32c(inputs["L3_attn_Wo"])
    kern = f32c(inputs["L3_ffn_kernel"])
    b3 = f32c(inputs["L3_ffn_bias"].reshape(8, 128).T)
    a3 = f32c(inputs["L3_ffn_a"])
    bl = f32c(inputs["L3_ffn_b"])
    ao = np.zeros((D, B * S), BF)
    for c in range(NC):
        a = st["ao3_units"][c]
        for bi in range(2):
            for hl in range(2):
                h = 2 * c + hl
                ao[64 * h:64 * h + 64, bi * S:(bi + 1) * S] = a[bi, hl]
    maps = []
    for c in range(NC):
        t0 = c * TLOC
        maps.append(dict(x=st["x6"][c], aoT=np.ascontiguousarray(ao[:, t0:t0 + TLOC]),
                         wo3=wo3, kern=kern, bias3=b3, a3=a3, b3lora=bl,
                         ident=ident, identf=np.eye(128, dtype=np.float32)))
    return maps


# version 11 (final)
# speedup vs baseline: 1.5040x; 1.0106x over previous
import numpy as np
import ml_dtypes

import concourse.bass as bass
import concourse.mybir as mybir
from concourse.bass_utils import run_bass_kernel_spmd

F32 = mybir.dt.float32
BF16 = mybir.dt.bfloat16
I32 = mybir.dt.int32
AF = mybir.ActivationFunctionType
ALU = mybir.AluOpType
AX = mybir.AxisListType.X

NC = 8
B = 2
S = 8192
D = 1024
NH = 16
HD = 64
TLOC = (B * S) // NC
NT = TLOC // 128
BF = np.dtype(ml_dtypes.bfloat16)



def host_ident128():
    return np.eye(128, dtype=BF)


def host_band_masks():
    m = np.zeros((4, 128, 512), dtype=np.float32)
    for r in range(4):
        for k in range(128):
            m[r, k, 128 * r + k:] = 1.0
    return np.ascontiguousarray(m.astype(BF))


def host_rope_tables(positions, d=32, theta=10000.0):
    inv = theta ** (-(np.arange(0, d, 2, dtype=np.float64)) / d)
    ang = positions[:, None].astype(np.float64) * inv[None, :]
    c = np.cos(ang).T
    s = np.sin(ang).T
    T = positions.shape[0]
    cos = np.empty((d, T), np.float64)
    sin = np.empty((d, T), np.float64)
    cos[0::2] = c
    cos[1::2] = c
    sin[0::2] = -s
    sin[1::2] = s
    return np.ascontiguousarray(cos.astype(np.float32)), np.ascontiguousarray(sin.astype(np.float32))


def swap_cols_pairwise(w):
    w2 = np.empty_like(w)
    w2[:, 0::2] = w[:, 1::2]
    w2[:, 1::2] = w[:, 0::2]
    return w2


def f32c(a):
    return np.ascontiguousarray(a, dtype=np.float32)




class Sem:
    def __init__(self, handle):
        self.h = handle
        self.n = 0

    def inc(self, instr, k=1):
        instr.then_inc(self.h, k)
        self.n += k
        return instr


class Prog:
    def __init__(self, name):
        self.nc = bass.Bass("TRN2", target_bir_lowering=False)
        self.name = name
        self.ctx = []
        self._sems = {}
        self._uid = 0

    def uid(self):
        self._uid += 1
        return self._uid

    def dram_in(self, name, shape, dtype=F32):
        return self.nc.dram_tensor(name, list(shape), dtype, kind="ExternalInput")

    def dram_out(self, name, shape, dtype=F32):
        return self.nc.dram_tensor(name, list(shape), dtype, kind="ExternalOutput")

    def sbuf(self, name, shape, dtype):
        h = self.nc.sbuf_tensor(name, list(shape), dtype)
        t = h.__enter__()
        self.ctx.append((name, h))
        return t

    def make_arena(self, nbytes):
        self._arena = self.sbuf(f"arena{self.uid()}", [128, nbytes], mybir.dt.uint8)
        self._arena_addr = self.nc.lookup_mloc(self._arena).addr
        return self._arena

    def sbuf_at(self, name, shape, dtype, off):
        return self.nc.alloc_sbuf_tensor_at(name, list(shape), dtype,
                                            offset=self._arena_addr + off)

    def psum(self, name, shape, dtype=F32):
        h = self.nc.psum_tensor(name, list(shape), dtype)
        t = h.__enter__()
        self.ctx.append((name, h))
        return t

    def constcol(self, val, rows=128):
        key = (float(val), rows)
        if not hasattr(self, "_consts"):
            self._consts = {}
        if key not in self._consts:
            t = self.sbuf(f"const{self.uid()}", [rows, 1], F32)
            with self.nc.Block() as block:
                @block.vector
                def _(v):
                    v.memset(t[:, :], float(val))
            self._consts[key] = t
        return self._consts[key][:, :]

    def sem(self, name):
        if name not in self._sems:
            h = self.nc.semaphore(name)
            s = h.__enter__()
            self.ctx.append((name, h))
            self._sems[name] = Sem(s)
        return self._sems[name]


def mm_grid(p: Prog, jobs, psum_banks, wait_sems=(), tag=None, nstage=0,
            stage_free_extra=0, gp_loads=()):
    mm = p.sem("mm_shared")
    fr = p.sem("fr_shared")
    st = p.sem("st_shared")
    nbank = len(psum_banks)
    mm_base, fr_base, st_base = mm.n, fr.n, st.n
    has_v = any(j["engine"] == "vector" for j in jobs)
    has_s = any(j["engine"] == "scalar" for j in jobs)
    assert not (has_v and has_s), "mm_grid: use one evac engine per grid"
    dma_jobs = [i for i, j in enumerate(jobs) if "dma" in j]

    with p.nc.Block() as block:
        @block.tensor
        def _(tensor):
            for s, cnt in wait_sems:
                tensor.wait_ge(s.h, cnt)
            for i, job in enumerate(jobs):
                if "wait" in job:
                    ws, wc = job["wait"]
                    tensor.wait_ge(ws.h, wc)
                if i >= nbank:
                    tensor.wait_ge(fr.h, fr_base + i - nbank + 1)
                pa = psum_banks[i % nbank]
                rows = job.get("rows", 128)
                pb = job.get("pbase", 0)
                w = job["out_w"]
                ents = job["mms"]
                n_m = sum(1 for e in ents if e[0] == "m")
                mi = 0
                last = None
                for e in ents:
                    if e[0] == "m":
                        _, lhsT, rhs = e
                        last = tensor.matmul(pa[pb:pb + rows, 0:w], lhsT, rhs,
                                             start=(mi == 0), stop=(mi == n_m - 1))
                        mi += 1
                    else:
                        _, src, ident = e
                        last = tensor.transpose(pa[pb:pb + rows, 0:w], src, ident)
                mm.inc(last, 1)

        def section(eng):
            ndma = 0
            for i, job in enumerate(jobs):
                eng.wait_ge(mm.h, mm_base + i + 1)
                if nstage and "dma" in job:
                    k = dma_jobs.index(i)
                    if k >= nstage - stage_free_extra:
                        eng.wait_ge(st.h, st_base + 16 * (k - nstage + 1 + stage_free_extra))
                pa = psum_banks[i % nbank]
                pb = job.get("pbase", 0)
                last = job["evac"](eng, pa[pb:pb + job.get("rows", 128), 0:job["out_w"]])
                fr.inc(last, 1)

        if has_v:
            @block.vector
            def _(vector):
                section(vector)
        if has_s:
            @block.scalar
            def _(scalar):
                section(scalar)

        if dma_jobs:
            @block.sync
            def _(sync):
                for k, i in enumerate(dma_jobs):
                    sync.wait_ge(fr.h, fr_base + i + 1)
                    dram_ap, stage_ap = jobs[i]["dma"]
                    st.inc(sync.dma_start(out=dram_ap, in_=stage_ap), 16)
                sync.wait_ge(st.h, st.n)

        if gp_loads:
            dg = p.sem("dma_gp")

            @block.gpsimd
            def _(g):
                for cnt, prs in gp_loads:
                    if cnt:
                        g.wait_ge(fr.h, cnt)
                    for dst, src in prs:
                        dg.inc(g.dma_start(out=dst, in_=src), 16)


def dma_block(p: Prog, pairs, queue="sync", tag=None, wait=True):
    s = p.sem("dma_shared" if queue == "sync" else "dma_gp")

    with p.nc.Block() as block:
        def body(e):
            for dst, src in pairs:
                s.inc(e.dma_start(out=dst, in_=src), 16)
            if wait:
                e.wait_ge(s.h, s.n)
        if queue == "sync":
            @block.sync
            def _(e):
                body(e)
        else:
            @block.gpsimd
            def _(e):
                body(e)
    return s


def chunkT(buf, c, t0, w, ntok=TLOC):
    return buf[:, c * ntok + t0: c * ntok + t0 + w]


def emit_to_xT(p: Prog, xb_sb, xT_sb, ident, psA, psB, n_tiles=NT, cols=D,
               ntok=None):
    banks = [psA] if psA is psB else [psA, psB]
    """PE-transpose token-major bf16 tiles into chunk-major feature-major buf."""
    if ntok is None:
        ntok = n_tiles * 128
    jobs = []
    nch = cols // 128
    for c in range(nch):
        for t in range(n_tiles):
            def mk(c=c, t=t):
                def evac(eng, pa):
                    return eng.tensor_copy(
                        xT_sb[:, c * ntok + t * 128: c * ntok + (t + 1) * 128], pa)
                return evac
            jobs.append(dict(
                mms=[("t", xb_sb[:, t * cols + c * 128: t * cols + (c + 1) * 128],
                      ident[:, :])],
                out_w=128, engine="vector", evac=mk()))
    mm_grid(p, jobs, banks, tag=f"xT{p.uid()}")


def emit_ln(p: Prog, x_sb, n_tiles=NT, cols=D, eps=1e-5):
    u = p.uid()
    st = p.sbuf(f"lnst{u}", [128, 12], F32)
    mv = p.sbuf(f"lnmv{u}", [128, 8], F32)
    t4 = p.sbuf(f"lnt4{u}", [128, 4], F32)
    epsb = p.constcol(eps)
    sv = p.sem("lnsv")
    ss = p.sem("lnss")
    fl = p.sem("lnfl")
    sv_b, ss_b = sv.n, ss.n
    assert cols == 1024
    with p.nc.Block() as block:
        @block.vector
        def _(v):
            def flush(ins):
                fl.inc(ins, 1)
                v.wait_ge(fl.h, fl.n)
            for t in range(n_tiles):
                xt = x_sb[:, t * cols:(t + 1) * cols]
                v.bn_stats(st[:, 0:6], xt[:, 0:512])
                flush(v.bn_stats(st[:, 6:12], xt[:, 512:1024]))
                means = bass.AP(st, 1, [[12, 128], [3, 4]])
                m2s = bass.AP(st, 2, [[12, 128], [3, 4]])
                v.reduce_sum(mv[:, 0:1], means, axis=AX)
                v.reduce_sum(mv[:, 1:2], m2s, axis=AX)
                flush(v.tensor_mul(t4[:, 0:4], means, means))
                v.tensor_scalar_mul(mv[:, 4:5], mv[:, 0:1], 0.25)
                v.tensor_scalar_mul(mv[:, 5:6], mv[:, 0:1], -0.25)
                flush(v.reduce_sum(mv[:, 2:3], t4[:, 0:4], axis=AX))
                v.tensor_mul(t4[:, 0:1], mv[:, 4:5], mv[:, 4:5])
                flush(v.scalar_tensor_tensor(mv[:, 3:4], mv[:, 2:3], 256.0,
                                             mv[:, 1:2], ALU.mult, ALU.add))
                sv.inc(v.scalar_tensor_tensor(mv[:, 6:7], mv[:, 3:4], 1.0 / 1024.0,
                                              t4[:, 0:1], ALU.mult, ALU.subtract), 1)
                v.wait_ge(ss.h, ss_b + t + 1)
                v.tensor_scalar(xt, xt, mv[:, 5:6], mv[:, 7:8], ALU.add, ALU.mult)
        @block.scalar
        def _(sc):
            for t in range(n_tiles):
                sc.wait_ge(sv.h, sv_b + t + 1)
                sc.activation(mv[:, 6:7], mv[:, 6:7], AF.Ln, bias=epsb)
                ss.inc(sc.activation(mv[:, 7:8], mv[:, 6:7], AF.Exp, scale=-0.5), 1)


def load_w_bf16(p: Prog, pairs):
    dma_block(p, pairs, queue="gpsimd")




def build_T0A():
    p = Prog("T0A")
    nc = p.nc
    x_in = p.dram_in("x", [TLOC, D]).ap()
    wdq = p.dram_in("wdq", [D, 256]).ap()
    wdkv = p.dram_in("wdkv", [D, 256]).ap()
    wkr = p.dram_in("wkr", [D, 32]).ap()
    wkrs = p.dram_in("wkrs", [D, 32]).ap()
    wuq = p.dram_in("wuq", [256, D]).ap()
    wqr = p.dram_in("wqr", [256, 512]).ap()
    wqrs = p.dram_in("wqrs", [256, 512]).ap()
    wuk = p.dram_in("wuk", [256, D]).ap()
    wuv = p.dram_in("wuv", [256, D]).ap()
    identi = p.dram_in("ident", [128, 128], BF16).ap()
    ropec = p.dram_in("ropec", [128, TLOC]).ap()
    ropes = p.dram_in("ropes", [128, TLOC]).ap()

    o_qc = p.dram_out("qc", [D, TLOC], BF16).ap()
    o_qr = p.dram_out("qrr", [512, TLOC], BF16).ap()
    o_kc = p.dram_out("kc", [D, TLOC], BF16).ap()
    o_kr = p.dram_out("krr", [32, TLOC], BF16).ap()
    o_v = p.dram_out("v", [D, TLOC], BF16).ap()

    xb = p.sbuf("xb", [128, NT * D], BF16)
    xT = p.sbuf("xT", [128, 8 * TLOC], BF16)
    cq = p.sbuf("cq", [128, 2 * TLOC], BF16)
    ckv = p.sbuf("ckv", [128, 2 * TLOC], BF16)
    w1 = p.sbuf("w1", [128, 8 * 256], BF16)
    w2 = p.sbuf("w2", [128, 8 * 256], BF16)
    w3 = p.sbuf("w3", [128, 8 * 64], BF16)
    w4 = p.sbuf("w4", [128, 2 * D], BF16)
    w5 = p.sbuf("w5", [128, 2 * 1024], BF16)
    w6 = p.sbuf("w6", [128, 2 * D], BF16)
    w7 = p.sbuf("w7", [128, 2 * D], BF16)
    ident = p.sbuf("id", [128, 128], BF16)
    rc = p.sbuf("rc", [128, TLOC], F32)
    rs = p.sbuf("rs", [128, TLOC], F32)
    stg = p.sbuf("stg", [128, 6 * 512], BF16)
    tq = p.sbuf("tq", [128, 2 * 512], BF16)
    tk = p.sbuf("tk", [32, 3 * 512], BF16)

    psums = [p.psum(f"ps{i}", [128, 512]) for i in range(4)]
    psb = [p.psum(f"psb{i}", [128, 512], BF16) for i in range(2)]

    xsrc = x_in.rearrange("(t p) c -> p t c", p=128)
    xdst = xb.ap().rearrange("p (t c) -> p t c", t=NT)
    dma_block(p, [(ident[:, :], identi), (rc[:, :], ropec), (rs[:, :], ropes)])
    dma_block(p, [(xdst, xsrc)], queue="gpsimd")
    wpairs = []
    for c in range(8):
        wpairs.append((w1[:, c * 256:(c + 1) * 256], wdq[c * 128:(c + 1) * 128, :]))
        wpairs.append((w2[:, c * 256:(c + 1) * 256], wdkv[c * 128:(c + 1) * 128, :]))
        wpairs.append((w3[:, c * 64:c * 64 + 32], wkr[c * 128:(c + 1) * 128, :]))
        wpairs.append((w3[:, c * 64 + 32:c * 64 + 64], wkrs[c * 128:(c + 1) * 128, :]))
    for c in range(2):
        wpairs.append((w4[:, c * D:(c + 1) * D], wuq[c * 128:(c + 1) * 128, :]))
        wpairs.append((w5[:, c * 1024:c * 1024 + 512], wqr[c * 128:(c + 1) * 128, :]))
        wpairs.append((w5[:, c * 1024 + 512:(c + 1) * 1024], wqrs[c * 128:(c + 1) * 128, :]))
        wpairs.append((w6[:, c * D:(c + 1) * D], wuk[c * 128:(c + 1) * 128, :]))
        wpairs.append((w7[:, c * D:(c + 1) * D], wuv[c * 128:(c + 1) * 128, :]))
    load_w_bf16(p, wpairs)

    emit_to_xT(p, xb, xT, ident, psb[0], psb[1])

    jobs = []
    for dst, wsb in ((cq, w1), (ckv, w2)):
        for f in range(2):
            for m in range(4):
                def mk(dst=dst, f=f, m=m):
                    def evac(eng, pa):
                        return eng.tensor_copy(chunkT(dst, f, m * 512, 512), pa)
                    return evac
                jobs.append(dict(
                    mms=[("m", wsb[:, c * 256 + f * 128:(c * 256) + (f + 1) * 128],
                          chunkT(xT, c, m * 512, 512)) for c in range(8)],
                    out_w=512, engine="vector", evac=mk()))
    mm_grid(p, jobs, psums, tag="lat")

    krtmp = p.sbuf("krtmp", [32, 4 * 512], BF16)
    jobs = []
    for m in range(4):
        def mkA(m=m):
            def evac(eng, pa):
                return eng.tensor_copy(krtmp[:, m * 512:(m + 1) * 512], pa)
            return evac
        jobs.append(dict(
            mms=[("m", w3[:, c * 64:c * 64 + 32], chunkT(xT, c, m * 512, 512))
                 for c in range(8)],
            out_w=512, rows=32, engine="vector", evac=mkA()))
    mm_grid(p, jobs, psums[:2], tag="kra")
    jobs = []
    for m in range(4):
        def mkB(m=m):
            def evac(eng, pa):
                eng.tensor_tensor(tk[:, 0:512], pa, rs[0:32, m * 512:(m + 1) * 512], ALU.mult)
                eng.tensor_mul(tk[:, 512:1024], krtmp[:, m * 512:(m + 1) * 512],
                               rc[0:32, m * 512:(m + 1) * 512])
                return eng.tensor_add(tk[:, 1024:1536], tk[:, 0:512], tk[:, 512:1024])
            return evac
        jobs.append(dict(
            mms=[("m", w3[:, c * 64 + 32:c * 64 + 64], chunkT(xT, c, m * 512, 512))
                 for c in range(8)],
            out_w=512, rows=32, engine="vector", evac=mkB(),
            dma=(o_kr[:, m * 512:(m + 1) * 512], tk[:, 1024:1536])))
    mm_grid(p, jobs, psums[:2], tag="krb", nstage=1)

    jobs = []
    ji = 0
    for dst, wsb, srcb in ((o_qc, w4, cq), (o_kc, w6, ckv), (o_v, w7, ckv)):
        for f in range(8):
            for m in range(4):
                sl = ji % 6
                ji += 1
                def mk(sl=sl):
                    def evac(eng, pa):
                        return eng.tensor_copy(stg[:, sl * 512:(sl + 1) * 512], pa)
                    return evac
                jobs.append(dict(
                    mms=[("m", wsb[:, c * D + f * 128:c * D + (f + 1) * 128],
                          chunkT(srcb, c, m * 512, 512)) for c in range(2)],
                    out_w=512, engine="vector", evac=mk(),
                    dma=(dst[f * 128:(f + 1) * 128, m * 512:(m + 1) * 512],
                         stg[:, sl * 512:(sl + 1) * 512])))
    mm_grid(p, jobs, psums, tag="kv", nstage=6)

    qrtmp = p.sbuf("qrtmp", [128, 4 * 512], BF16)
    for ch in range(4):
        jobs = []
        for m in range(4):
            def mkA(ch=ch, m=m):
                def evac(eng, pa):
                    return eng.tensor_copy(qrtmp[:, m * 512:(m + 1) * 512], pa)
                return evac
            jobs.append(dict(
                mms=[("m", w5[:, c * 1024 + ch * 128:c * 1024 + (ch + 1) * 128],
                      chunkT(cq, c, m * 512, 512)) for c in range(2)],
                out_w=512, engine="vector", evac=mkA()))
        mm_grid(p, jobs, psums[:2], tag=f"qra{ch}")
        jobs = []
        for m in range(4):
            def mkB(ch=ch, m=m):
                def evac(eng, pa):
                    eng.tensor_tensor(tq[:, 0:512], pa,
                                      rs[:, m * 512:(m + 1) * 512], ALU.mult)
                    eng.tensor_tensor(tq[:, 512:1024], qrtmp[:, m * 512:(m + 1) * 512],
                                      rc[:, m * 512:(m + 1) * 512], ALU.mult)
                    return eng.tensor_add(tq[:, 0:512], tq[:, 0:512], tq[:, 512:1024])
                return evac
            jobs.append(dict(
                mms=[("m", w5[:, c * 1024 + 512 + ch * 128:c * 1024 + 512 + (ch + 1) * 128],
                      chunkT(cq, c, m * 512, 512)) for c in range(2)],
                out_w=512, engine="vector", evac=mkB(),
                dma=(o_qr[ch * 128:(ch + 1) * 128, m * 512:(m + 1) * 512],
                     tq[:, 0:512])))
        mm_grid(p, jobs, psums[:2], tag=f"qrb{ch}", nstage=1)

    return p



_PROGS = {}


def get_prog(name, builder):
    if name not in _PROGS:
        _PROGS[name] = builder()
    return _PROGS[name]


def run_launch(name, builder, in_maps):
    p = get_prog(name, builder)
    return run_bass_kernel_spmd(p.nc, in_maps, core_ids=list(range(NC)))


def prep_T0A(inputs):
    x = inputs["L0_attn_x"].reshape(B * S, D)
    wdq = f32c(inputs["L0_attn_W_DQ"])
    wdkv = f32c(inputs["L0_attn_W_DKV"])
    wkr = f32c(inputs["L0_attn_W_KR"])
    wkrs = f32c(swap_cols_pairwise(wkr))
    wuq = f32c(inputs["L0_attn_W_UQ"])
    wqr = f32c(inputs["L0_attn_W_QR"])
    wqrs = f32c(swap_cols_pairwise(wqr))
    wuk = f32c(inputs["L0_attn_W_UK"])
    wuv = f32c(inputs["L0_attn_W_UV"])
    ident = host_ident128()
    maps = []
    for c in range(NC):
        t0 = c * TLOC
        pos = (np.arange(t0, t0 + TLOC) % S)
        rc, rs = host_rope_tables(pos)
        rc = np.ascontiguousarray(np.tile(rc, (4, 1)))
        rs = np.ascontiguousarray(np.tile(rs, (4, 1)))
        maps.append(dict(x=f32c(x[t0:t0 + TLOC]), wdq=wdq, wdkv=wdkv, wkr=wkr,
                         wkrs=wkrs, wuq=wuq, wqr=wqr, wqrs=wqrs, wuk=wuk,
                         wuv=wuv, ident=ident, ropec=rc, ropes=rs))
    return maps


def run_launch(name, builder, maps):
    p = get_prog(name, builder)
    return run_bass_kernel_spmd(p.nc, maps, core_ids=list(range(NC)))


def kernel(**inputs):
    st = {}

    res = run_launch("T0A", build_T0A, prep_T0A(inputs))
    for k in ("qc", "qrr", "kc", "krr", "v"):
        st[k] = np.concatenate([res.results[c][k] for c in range(NC)], axis=1)

    res = run_launch("F0", build_F0, prep_F0(st))
    st["ao_units"] = [res.results[c]["ao"] for c in range(NC)]

    res = run_launch("T0B", build_T0B, prep_T0B(inputs, st))
    st["x2"] = [res.results[c]["x2"] for c in range(NC)]
    st["qt"] = [res.results[c]["qt"] for c in range(NC)]
    st["ktm"] = [res.results[c]["ktm"] for c in range(NC)]
    st["vtm"] = [res.results[c]["vtm"] for c in range(NC)]
    st["totals"] = [res.results[c]["totals"] for c in range(NC)]

    res = run_launch("T1", build_T1, prep_T1(inputs, st))
    st["x3"] = [res.results[c]["x3"] for c in range(NC)]
    st["qh"] = np.concatenate([res.results[c]["qh"] for c in range(NC)], axis=1)
    st["kh"] = np.concatenate([res.results[c]["kh"] for c in range(NC)], axis=1)
    st["vh"] = np.concatenate([res.results[c]["vh"] for c in range(NC)], axis=1)
    st["kss"] = np.concatenate([res.results[c]["kss"] for c in range(NC)], axis=1)

    res = run_launch("F1", build_F1, prep_F1(st))
    st["ao1_units"] = [res.results[c]["ao"] for c in range(NC)]

    res = run_launch("T2", build_T2, prep_T2(inputs, st))
    st["x4"] = [res.results[c]["x4"] for c in range(NC)]
    st["q2"] = [res.results[c]["q2"] for c in range(NC)]
    st["k2"] = [res.results[c]["k2"] for c in range(NC)]
    st["v2"] = [res.results[c]["v2"] for c in range(NC)]
    st["qss2"] = [res.results[c]["qss2"] for c in range(NC)]
    st["kss2"] = [res.results[c]["kss2"] for c in range(NC)]

    res = run_launch("T3", build_T3, prep_T3(inputs, st))
    st["x5"] = [res.results[c]["x5"] for c in range(NC)]

    res = run_launch("T4", build_T4, prep_T4(inputs, st))
    st["x6"] = [res.results[c]["x6"] for c in range(NC)]
    st["q3"] = np.concatenate([res.results[c]["q3"] for c in range(NC)], axis=1)
    st["k3"] = np.concatenate([res.results[c]["k3"] for c in range(NC)], axis=1)
    st["v3"] = np.concatenate([res.results[c]["v3"] for c in range(NC)], axis=1)
    st["qss3"] = np.concatenate([res.results[c]["qss3"] for c in range(NC)], axis=1)
    st["kss3"] = np.concatenate([res.results[c]["kss3"] for c in range(NC)], axis=1)

    res = run_launch("F3", build_F3, prep_F3(st))
    st["ao3_units"] = [res.results[c]["ao"] for c in range(NC)]

    res = run_launch("T5", build_T5, prep_T5(inputs, st))
    x8 = np.concatenate([res.results[c]["x8"] for c in range(NC)], axis=0)
    return np.ascontiguousarray(x8.reshape(B, S, D), dtype=np.float32)



TAU0 = float(np.sqrt(np.float64(96.0)))


def emit_flash_unit(p: Prog, u, *, dq, q_sb, k_sb, vhat_sb, msk, ones64,
                    exp_scale, exp_bias, clip, ao_out, pt, ps_s, ps_o, ps_b,
                    stage, C_bc=None):
    sc = p.sem("f_sc")
    ex = p.sem("f_ex")
    pr = p.sem("f_pr")
    pv = p.sem("f_pv")
    orr = p.sem("f_or")
    e1 = p.sem("f_e1")
    bc = p.sem("f_bc")
    ar = p.sem("f_ar")
    st = p.sem("f_st")

    iters = [(m, K) for m in range(16) for K in range(4 * m + 4)]
    n_it = len(iters)
    base = dict(sc=sc.n, ex=ex.n, pr=pr.n, pv=pv.n, orr=orr.n,
                e1=e1.n, bc=bc.n, ar=ar.n, st=st.n)

    def pv_cum(m):
        return base["pv"] + 2 * (m + 1) * (m + 2)

    with p.nc.Block() as block:
        @block.tensor
        def _(tensor):
            def emit_bc(j):
                tensor.wait_ge(e1.h, base["e1"] + j + 1)
                if j >= 1:
                    tensor.wait_ge(ar.h, base["ar"] + j)
                rd = stage["rd"][0:1, 512 * (j % 2):512 * (j % 2 + 1)]
                bc.inc(tensor.matmul(ps_b[0:64, 0:512], ones64[0:1, 0:64], rd,
                                     start=True, stop=True), 1)

            def emit_pv(nit):
                pm, pK = iters[nit]
                tensor.wait_ge(pr.h, base["pr"] + nit + 1)
                if pK == 0 and pm >= 2:
                    tensor.wait_ge(orr.h, base["orr"] + pm - 1)
                pv.inc(tensor.matmul(
                    ps_o[pm % 2][0:65, 0:512],
                    vhat_sb[:, 65 * pK:65 * pK + 65],
                    pt[nit % 4][:, 0:512],
                    start=(pK == 0), stop=(pK == 4 * pm + 3)), 1)

            nit = 0
            for m in range(16):
                if m >= 2:
                    emit_bc(m - 2)
                for K in range(4 * m + 4):
                    if nit >= 2:
                        tensor.wait_ge(ex.h, base["ex"] + nit - 1)
                    sc.inc(tensor.matmul(
                        ps_s[nit % 2][:, 0:512],
                        k_sb[0:dq, 128 * K:128 * (K + 1)],
                        q_sb[0:dq, 512 * m:512 * (m + 1)],
                        start=True, stop=True), 1)
                    if nit >= 1:
                        emit_pv(nit - 1)
                    nit += 1
            emit_pv(n_it - 1)
            emit_bc(14)
            emit_bc(15)

        @block.scalar
        def _(scalar):
            def epilogue(j):
                scalar.wait_ge(pv.h, pv_cum(j))
                rf = stage["rf"][0:1, 0:512]
                rd = stage["rd"][0:1, 512 * (j % 2):512 * (j % 2 + 1)]
                scalar.activation(rf, ps_o[j % 2][64:65, 0:512], AF.Ln)
                if j >= 2:
                    scalar.wait_ge(bc.h, base["bc"] + j - 1)
                e1.inc(scalar.activation(rd, rf, AF.Exp, scale=-1.0), 1)
                if j >= 2:
                    scalar.wait_ge(ar.h, base["ar"] + j - 1)
                nm = stage["num"][:, 512 * (j % 2):512 * (j % 2 + 1)]
                orr.inc(scalar.copy(nm[0:64, 0:512], ps_o[j % 2][0:64, 0:512]), 1)

            nit = 0
            for m in range(16):
                for K in range(4 * m + 4):
                    scalar.wait_ge(sc.h, base["sc"] + nit + 1)
                    if nit >= 4:
                        scalar.wait_ge(pv.h, base["pv"] + nit - 3)
                    bias_ap = exp_bias if C_bc is None else C_bc(m)
                    ex.inc(scalar.activation(pt[nit % 4][:, 0:512],
                                             ps_s[nit % 2][:, 0:512], AF.Exp,
                                             bias=bias_ap, scale=float(exp_scale)), 1)
                    nit += 1
                if m >= 1:
                    epilogue(m - 1)
            epilogue(15)

        @block.vector
        def _(vector):
            def emit_tt(j):
                vector.wait_ge(bc.h, base["bc"] + j + 1)
                vector.wait_ge(orr.h, base["orr"] + j + 1)
                if j >= 2:
                    vector.wait_ge(st.h, base["st"] + 16 * (j - 1))
                ao = stage["ao"][:, 512 * (j % 2):512 * (j % 2 + 1)]
                nm = stage["num"][:, 512 * (j % 2):512 * (j % 2 + 1)]
                ar.inc(vector.tensor_tensor(ao[0:64, 0:512], nm[0:64, 0:512],
                                            ps_b[0:64, 0:512], ALU.mult), 1)

            nit = 0
            for m in range(16):
                for K in range(4 * m + 4):
                    vector.wait_ge(ex.h, base["ex"] + nit + 1)
                    t = pt[nit % 4][:, 0:512]
                    last = None
                    if clip is not None:
                        last = vector.tensor_scalar(t, t, float(clip[1]), float(clip[0]),
                                                    ALU.min, ALU.max)
                    if K >= 4 * m:
                        r = K - 4 * m
                        last = vector.tensor_tensor(t, t, msk[:, 512 * r:512 * (r + 1)],
                                                    ALU.mult)
                    if last is None:
                        last = vector.engine_nop()
                    pr.inc(last, 1)
                    nit += 1
                if m >= 2:
                    emit_tt(m - 2)
            emit_tt(14)
            emit_tt(15)

        @block.sync
        def _(sync):
            for m in range(16):
                sync.wait_ge(ar.h, base["ar"] + m + 1)
                ao = stage["ao"][:, 512 * (m % 2):512 * (m % 2 + 1)]
                st.inc(sync.dma_start(out=ao_out(m), in_=ao[0:64, 0:512]), 16)
            sync.wait_ge(st.h, st.n)


def emit_flash_pair(p: Prog, US, *, msk, ones64, exp_scale, clip, wait_lds=()):
    iters = [(m, K) for m in range(16) for K in range(4 * m + 4)]
    n_it = len(iters)
    for U in US:
        U["base"] = {k: s.n for k, s in U["sems"].items()}

    def B(U, k):
        return U["base"][k]

    def SS(U, k):
        return U["sems"][k]

    def pv_cum(U, m):
        return B(U, "pv") + 2 * (m + 1) * (m + 2)

    with p.nc.Block() as block:
        @block.tensor
        def _(tensor):
            for s, cnt in wait_lds:
                tensor.wait_ge(s.h, cnt)

            def emit_bc(U, j):
                tensor.wait_ge(SS(U, "e1").h, B(U, "e1") + j + 1)
                if j >= 1:
                    tensor.wait_ge(SS(U, "ar").h, B(U, "ar") + j)
                rd = U["rd"][0:1, 512 * (j % 2):512 * (j % 2 + 1)]
                SS(U, "bc").inc(
                    tensor.matmul(U["ps_b"][0:64, 0:512], ones64[0:1, 0:64], rd,
                                  start=True, stop=True), 1)

            def emit_pv(U, nit):
                pm, pK = iters[nit]
                tensor.wait_ge(SS(U, "pr").h, B(U, "pr") + nit + 1)
                if pK == 0 and pm >= 1:
                    tensor.wait_ge(SS(U, "orr").h, B(U, "orr") + pm)
                SS(U, "pv").inc(tensor.matmul(
                    U["ps_o"][0:65, 0:512],
                    U["vhat"][:, 65 * pK:65 * pK + 65],
                    U["pt"][nit % 4][:, 0:512],
                    start=(pK == 0), stop=(pK == 4 * pm + 3)), 1)

            nit = 0
            for m in range(16):
                for U in US:
                    if m >= 2:
                        emit_bc(U, m - 2)
                for K in range(4 * m + 4):
                    for U in US:
                        if nit >= 2:
                            tensor.wait_ge(SS(U, "ex").h, B(U, "ex") + nit - 1)
                        dq = U["dq"]
                        SS(U, "sc").inc(tensor.matmul(
                            U["ps_s"][nit % 2][:, 0:512],
                            U["k"][0:dq, 128 * K:128 * (K + 1)],
                            U["q"][0:dq, 512 * m:512 * (m + 1)],
                            start=True, stop=True), 1)
                        if nit >= 1:
                            emit_pv(U, nit - 1)
                    nit += 1
            for U in US:
                emit_pv(U, n_it - 1)
                emit_bc(U, 14)
                emit_bc(U, 15)

        @block.scalar
        def _(scalar):
            def epilogue(U, j):
                scalar.wait_ge(SS(U, "pv").h, pv_cum(U, j))
                rf = U["rf"][0:1, 0:512]
                rd = U["rd"][0:1, 512 * (j % 2):512 * (j % 2 + 1)]
                scalar.activation(rf, U["ps_o"][64:65, 0:512], AF.Ln)
                if j >= 2:
                    scalar.wait_ge(SS(U, "bc").h, B(U, "bc") + j - 1)
                SS(U, "e1").inc(scalar.activation(rd, rf, AF.Exp, scale=-1.0), 1)
                if j >= 2:
                    scalar.wait_ge(SS(U, "ar").h, B(U, "ar") + j - 1)
                nm = U["num"][:, 512 * (j % 2):512 * (j % 2 + 1)]
                SS(U, "orr").inc(
                    scalar.copy(nm[0:64, 0:512], U["ps_o"][0:64, 0:512]), 1)

            nit = 0
            for m in range(16):
                for K in range(4 * m + 4):
                    for U in US:
                        scalar.wait_ge(SS(U, "sc").h, B(U, "sc") + nit + 1)
                        if nit >= 4:
                            scalar.wait_ge(SS(U, "pv").h, B(U, "pv") + nit - 3)
                        bias_ap = U["bias"](m) if callable(U["bias"]) else U["bias"]
                        SS(U, "ex").inc(scalar.activation(
                            U["pt"][nit % 4][:, 0:512],
                            U["ps_s"][nit % 2][:, 0:512], AF.Exp,
                            bias=bias_ap, scale=float(exp_scale)), 1)
                    if K == 0 and m >= 1:
                        for U in US:
                            epilogue(U, m - 1)
                    nit += 1
            for U in US:
                epilogue(U, 15)

        @block.vector
        def _(vector):
            def emit_tt(U, j):
                vector.wait_ge(SS(U, "bc").h, B(U, "bc") + j + 1)
                vector.wait_ge(SS(U, "orr").h, B(U, "orr") + j + 1)
                if j >= 2:
                    vector.wait_ge(SS(U, "st").h, B(U, "st") + 16 * (j - 1))
                ao = U["ao"][:, 512 * (j % 2):512 * (j % 2 + 1)]
                nm = U["num"][:, 512 * (j % 2):512 * (j % 2 + 1)]
                SS(U, "ar").inc(vector.tensor_tensor(
                    ao[0:64, 0:512], nm[0:64, 0:512],
                    U["ps_b"][0:64, 0:512], ALU.mult), 1)

            nit = 0
            for m in range(16):
                for K in range(4 * m + 4):
                    for U in US:
                        vector.wait_ge(SS(U, "ex").h, B(U, "ex") + nit + 1)
                        t = U["pt"][nit % 4][:, 0:512]
                        last = None
                        if clip is not None:
                            last = vector.tensor_scalar(
                                t, t, float(clip[1]), float(clip[0]),
                                ALU.min, ALU.max)
                        if K >= 4 * m:
                            r = K - 4 * m
                            last = vector.tensor_tensor(
                                t, t, msk[:, 512 * r:512 * (r + 1)], ALU.mult)
                        if last is None:
                            last = vector.engine_nop()
                        SS(U, "pr").inc(last, 1)
                    nit += 1
                if m >= 2:
                    for U in US:
                        emit_tt(U, m - 2)
            for U in US:
                emit_tt(U, 14)
                emit_tt(U, 15)

        @block.sync
        def _(sync):
            for m in range(16):
                for U in US:
                    sync.wait_ge(SS(U, "ar").h, B(U, "ar") + m + 1)
                    ao = U["ao"][:, 512 * (m % 2):512 * (m % 2 + 1)]
                    SS(U, "st").inc(
                        sync.dma_start(out=U["ao_out"](m), in_=ao[0:64, 0:512]), 16)
            for U in US:
                sync.wait_ge(SS(U, "st").h, SS(U, "st").n)


def flash_sems(p, t):
    return {k: p.sem(f"f_{k}{t}")
            for k in ("sc", "ex", "pr", "pv", "orr", "e1", "bc", "ar", "st")}


def build_F0():
    p = Prog("F0")
    dq = 96
    q_in = p.dram_in("q", [2, 2, 96, S], BF16).ap()
    k_in = p.dram_in("k", [2, 2, 96, S], BF16).ap()
    v_in = p.dram_in("v", [2, 2, S, 64], BF16).ap()
    m_in = p.dram_in("masks", [4, 128, 512], BF16).ap()
    ao_o = p.dram_out("ao", [2, 2, 64, S], BF16).ap()

    q_sb = [p.sbuf(f"q{i}", [96, S], BF16) for i in range(4)]
    k_sb = [p.sbuf(f"k{i}", [96, S], BF16) for i in range(4)]
    vh_sb = [p.sbuf(f"vh{i}", [128, 64 * 65], BF16) for i in range(4)]
    msk = p.sbuf("msk", [128, 4 * 512], BF16)
    ones64 = p.sbuf("ones64", [1, 64], BF16)
    pt = [p.sbuf(f"pt{i}", [128, 512], BF16) for i in range(8)]
    stg = [dict(rd=p.sbuf(f"rd{t}", [1, 2 * 512], BF16),
                rf=p.sbuf(f"rf{t}", [1, 512], F32),
                num=p.sbuf(f"num{t}", [64, 2 * 512], BF16),
                ao=p.sbuf(f"aos{t}", [64, 2 * 512], BF16)) for t in range(2)]
    ps_s = [[p.psum(f"pss{t}{i}", [128, 512]) for i in range(2)] for t in range(2)]
    ps_o = [p.psum(f"pso{t}", [65, 512]) for t in range(2)]
    ps_b = [p.psum(f"psb{t}", [64, 512]) for t in range(2)]

    units = [(b, h) for b in range(2) for h in range(2)]
    lds = [p.sem("fldA"), p.sem("fldB")]
    ms = p.sem("fms")
    dma_block(p, [(msk[:, 512 * r:512 * (r + 1)], m_in[r]) for r in range(4)])
    bias0 = p.sbuf("bias0", [128, 1], F32)
    with p.nc.Block() as block:
        @block.vector
        def _(v):
            v.memset(ones64[:, :], 1.0)
            v.memset(bias0[:, :], -80.0 / TAU0)
            for u in range(4):
                ms.inc(v.memset(vh_sb[u][:, :], 1.0), 1)

        @block.sync
        def _(sync):
            sync.wait_ge(ms.h, 4)
            for u in (0, 1):
                b, h = units[u]
                lds[0].inc(sync.dma_start(out=q_sb[u][:, :], in_=q_in[b, h]), 16)
                lds[0].inc(sync.dma_start(out=k_sb[u][:, :], in_=k_in[b, h]), 16)
                dst = vh_sb[u].ap().rearrange("p (k c) -> p k c", c=65)[:, :, 0:64]
                src = v_in[b, h].rearrange("(k p) c -> p k c", p=128)
                lds[0].inc(sync.dma_start(out=dst, in_=src), 16)

        @block.gpsimd
        def _(g):
            g.wait_ge(ms.h, 4)
            for u in (2, 3):
                b, h = units[u]
                lds[1].inc(g.dma_start(out=q_sb[u][:, :], in_=q_in[b, h]), 16)
                lds[1].inc(g.dma_start(out=k_sb[u][:, :], in_=k_in[b, h]), 16)
                dst = vh_sb[u].ap().rearrange("p (k c) -> p k c", c=65)[:, :, 0:64]
                src = v_in[b, h].rearrange("(k p) c -> p k c", p=128)
                lds[1].inc(g.dma_start(out=dst, in_=src), 16)

    for pi in range(2):
        US = []
        for t in range(2):
            u = 2 * pi + t
            b, h = units[u]
            US.append(dict(
                dq=dq, q=q_sb[u], k=k_sb[u], vhat=vh_sb[u],
                pt=pt[4 * t:4 * t + 4], ps_s=ps_s[t], ps_o=ps_o[t],
                ps_b=ps_b[t], bias=bias0[:, :],
                ao_out=lambda m, b=b, h=h: ao_o[b, h, :, 512 * m:512 * (m + 1)],
                sems=flash_sems(p, t), **stg[t]))
        emit_flash_pair(p, US, msk=msk, ones64=ones64, exp_scale=1.0 / TAU0,
                        clip=(float(np.exp(-160.0 / TAU0)), 1.0),
                        wait_lds=[(lds[pi], 96)])
    return p


def prep_F0(st):
    maps = []
    masks = host_band_masks()
    ident = host_ident128()
    qc, qr, kc, kr, v = st["qc"], st["qrr"], st["kc"], st["krr"], st["v"]
    for c in range(NC):
        q = np.zeros((2, 2, 96, S), BF)
        k = np.zeros((2, 2, 96, S), BF)
        vv = np.zeros((2, 2, S, 64), BF)
        for bi in range(2):
            for hl in range(2):
                h = 2 * c + hl if 2 * c + hl < 16 else 2 * c + hl - 16
                q[bi, hl, 0:64] = qc[64 * h:64 * h + 64, bi * S:(bi + 1) * S]
                q[bi, hl, 64:96] = qr[32 * h:32 * h + 32, bi * S:(bi + 1) * S]
                k[bi, hl, 0:64] = kc[64 * h:64 * h + 64, bi * S:(bi + 1) * S]
                k[bi, hl, 64:96] = kr[:, bi * S:(bi + 1) * S]
                vv[bi, hl] = v[64 * h:64 * h + 64, bi * S:(bi + 1) * S].T
        maps.append(dict(q=q, k=k, v=vv, masks=masks))
    return maps




def emit_load_x(p: Prog, x_in_ap, x_sb):
    xsrc = x_in_ap.rearrange("(t p) c -> p t c", p=128)
    xdst = x_sb.ap().rearrange("p (t c) -> p t c", t=NT)
    dma_block(p, [(xdst, xsrc)])


def emit_store_x(p: Prog, x_sb, x_out_ap):
    xsrc = x_sb.ap().rearrange("p (t c) -> p t c", t=NT)
    xdst = x_out_ap.rearrange("(t p) c -> p t c", p=128)
    dma_block(p, [(xdst, xsrc)])


def emit_bf16_tiles(p: Prog, x_sb, xb_sb, scale_col=None, n_tiles=NT, cols=D):
    with p.nc.Block() as block:
        @block.scalar
        def _(sc):
            for t in range(n_tiles):
                src = x_sb[:, t * cols:(t + 1) * cols]
                dst = xb_sb[:, t * cols:(t + 1) * cols]
                if scale_col is None:
                    sc.copy(dst, src)
                else:
                    sc.activation(dst, src, AF.Copy, scale=scale_col(t))


def emit_addT_into_x(p: Prog, yT_sb, x_sb, ident, psb16, skip_chunks=(),
                     n_tiles=NT):
    jobs = []
    for t in range(n_tiles):
        for c in range(8):
            if c in skip_chunks:
                continue
            def mk(t=t, c=c):
                def evac(eng, pa):
                    xs = x_sb[:, t * D + c * 128:t * D + (c + 1) * 128]
                    return eng.tensor_add(xs, xs, pa)
                return evac
            jobs.append(dict(mms=[("t", chunkT(yT_sb, c, t * 128, 128), ident[:, :])],
                             out_w=128, engine="vector", evac=mk()))
    mm_grid(p, jobs, psb16, tag=f"addT{p.uid()}")


def emit_fm_proj(p: Prog, w_sb, wchunks, src_sb, dst_evac, psums, n_free=512,
                 n_macros=4, fchunks=8, tag=None, engine="scalar"):
    jobs = []
    for f in range(fchunks):
        for m in range(n_macros):
            jobs.append(dict(
                mms=[("m", wchunks(c, f), chunkT(src_sb, c, m * n_free, n_free))
                     for c in range(8)],
                out_w=n_free, engine=engine, evac=dst_evac(f, m)))
    mm_grid(p, jobs, psums, tag=tag)


def emit_gate_topk(p: Prog, lgT_sb, comb_sb, n_exp, renorm_eps, psb16, ident,
                   n_tiles=NT):
    u = p.uid()
    lg = p.sbuf(f"gt_l{u}", [128, n_exp], F32)
    sc1 = p.sbuf(f"gt_s{u}", [128, 8], F32)
    sv = p.sem("gt_v")
    ss = p.sem("gt_s")
    tr = p.sem("gt_t")
    fl = p.sem("gt_f")
    sv_b, ss_b, tr_b = sv.n, ss.n, tr.n
    E = n_exp
    with p.nc.Block() as block:
        @block.tensor
        def _(tensor):
            for t in range(n_tiles):
                if t >= 1:
                    tensor.wait_ge(sv.h, sv_b + t)
                tr.inc(tensor.transpose(psb16[0][0:128, 0:E],
                                        lgT_sb[0:E, t * 128:(t + 1) * 128],
                                        ident[0:E, 0:E]), 1)
        @block.vector
        def _(v):
            def flush(ins):
                fl.inc(ins, 1)
                v.wait_ge(fl.h, fl.n)
            for t in range(n_tiles):
                cb = comb_sb[:, t * E:(t + 1) * E]
                v.wait_ge(tr.h, tr_b + t + 1)
                flush(v.tensor_copy(lg[:, 0:E], psb16[0][0:128, 0:E]))
                flush(v.reduce_max(sc1[:, 0:1], lg[:, 0:E], axis=AX))
                v.tensor_scalar_mul(sc1[:, 1:2], sc1[:, 0:1], -1.0)
                flush(v.tensor_single_scalar(cb, lg[:, 0:E], sc1[:, 0:1], ALU.is_ge))
                flush(v.scalar_tensor_tensor(cb, cb, -1e30, lg[:, 0:E],
                                             ALU.mult, ALU.add))
                flush(v.reduce_max(sc1[:, 2:3], cb, axis=AX))
                sv.inc(v.tensor_single_scalar(cb, lg[:, 0:E], sc1[:, 2:3], ALU.is_ge), 1)
                v.wait_ge(ss.h, ss_b + t + 1)
                flush(v.tensor_mul(cb, cb, lg[:, 0:E]))
                flush(v.reduce_sum(sc1[:, 3:4], cb, axis=AX))
                if renorm_eps:
                    flush(v.tensor_scalar_add(sc1[:, 3:4], sc1[:, 3:4],
                                              float(renorm_eps)))
                flush(v.reciprocal(sc1[:, 4:5], sc1[:, 3:4]))
                v.tensor_single_scalar(cb, cb, sc1[:, 4:5], ALU.mult)
        @block.scalar
        def _(s):
            for t in range(n_tiles):
                s.wait_ge(sv.h, sv_b + t + 1)
                ss.inc(s.activation(lg[:, 0:E], lg[:, 0:E], AF.Exp,
                                    bias=sc1[:, 1:2]), 1)




def build_T0B():
    p = Prog("T0B")
    x_in = p.dram_in("x", [TLOC, D]).ap()
    ao_in = p.dram_in("aoT", [D, TLOC], BF16).ap()
    wo_in = p.dram_in("wo", [D, D]).ap()
    wg_in = p.dram_in("wg", [D, 8]).ap()
    w1_in = p.dram_in("w1", [8, D, 2048]).ap()
    w2_in = p.dram_in("w2", [8, 2048, D]).ap()
    wqkvT_in = p.dram_in("wqkvT", [D, 3072]).ap()
    identi = p.dram_in("ident", [128, 128], BF16).ap()

    x_out = p.dram_out("x2", [TLOC, D]).ap()
    qt_out = p.dram_out("qt", [D, TLOC], BF16).ap()
    ktm_out = p.dram_out("ktm", [TLOC, D], BF16).ap()
    vtm_out = p.dram_out("vtm", [TLOC, D], BF16).ap()
    tot_out = p.dram_out("totals", [64, 16 * 65]).ap()

    x_sb = p.sbuf("xsb", [128, NT * D], F32)
    bufC = p.sbuf("bufC", [128, 8 * TLOC], BF16)
    bufA = p.sbuf("bufA", [128, 16 * 1024], BF16)
    bufB = p.sbuf("bufB", [128, 16 * 1024], BF16)
    bufD = p.sbuf("bufD", [128, 16 * 512], BF16)
    comb = p.sbuf("comb", [128, NT * 8], F32)
    lgT = p.sbuf("lgT", [8, TLOC], BF16)
    stg = p.sbuf("stg", [128, 6 * 512], BF16)
    ident = p.sbuf("id", [128, 128], BF16)
    psums = [p.psum(f"ps{i}", [128, 512]) for i in range(4)]
    psb16 = [p.psum(f"psb{i}", [128, 512], BF16) for i in range(2)]

    emit_load_x(p, x_in, x_sb)
    dma_block(p, [(ident[:, :], identi)])
    dma_block(p, [(bufA[:, c * 2048:(c + 1) * 2048], ao_in[c * 128:(c + 1) * 128, :])
                  for c in range(8)], queue="gpsimd")
    dma_block(p, [(bufB[:, c * 1024:(c + 1) * 1024], wo_in[c * 128:(c + 1) * 128, :])
                  for c in range(8)], queue="gpsimd")

    def oevac(f, m):
        def evac(eng, pa):
            return eng.copy(chunkT(bufC, f, m * 512, 512), pa)
        return evac
    emit_fm_proj(p, None, lambda c, f: bufB[:, c * 2048 + 0:0] if False else
                 bufB[:, c * 1024 + f * 128: c * 1024 + (f + 1) * 128],
                 bufA, oevac, psums, tag="oproj")
    emit_addT_into_x(p, bufC, x_sb, ident, psb16)
    emit_ln(p, x_sb)
    emit_bf16_tiles(p, x_sb, bufA)
    emit_to_xT(p, bufA, bufC, ident, psb16[0], psb16[1])

    wg8 = p.sbuf("wg8", [128, 8 * 8], BF16)
    dma_block(p, [(wg8[:, c * 8:(c + 1) * 8], wg_in[c * 128:(c + 1) * 128, :])
                  for c in range(8)], queue="gpsimd")
    jobs = []
    for m in range(4):
        def mk(m=m):
            def evac(eng, pa):
                return eng.copy(lgT[0:8, m * 512:(m + 1) * 512], pa)
            return evac
        jobs.append(dict(mms=[("m", wg8[:, c * 8:(c + 1) * 8],
                               chunkT(bufC, c, m * 512, 512)) for c in range(8)],
                         out_w=512, rows=8, engine="scalar", evac=mk()))
    mm_grid(p, jobs, psums[:2], tag="gate")
    emit_gate_topk(p, lgT, comb, 8, None, psb16, ident)

    for e in range(8):
        dma_block(p, [(bufA[:, c * 2048:(c + 1) * 2048],
                       w1_in[e, c * 128:(c + 1) * 128, :]) for c in range(8)]
                  + [(bufB[:, c * 1024:(c + 1) * 1024],
                      w2_in[e, c * 128:(c + 1) * 128, :]) for c in range(16)],
                  queue="gpsimd")
        for m in range(4):
            jobs = []
            for f in range(16):
                def mk(f=f):
                    def evac(eng, pa):
                        return eng.activation(bufD[:, f * 512:(f + 1) * 512], pa,
                                              AF.Relu)
                    return evac
                jobs.append(dict(
                    mms=[("m", bufA[:, c * 2048 + f * 128:c * 2048 + (f + 1) * 128],
                          chunkT(bufC, c, m * 512, 512)) for c in range(8)],
                    out_w=512, engine="scalar", evac=mk()))
            mm_grid(p, jobs, psums, tag=f"h{e}_{m}")
            jobs = []
            for tt in range(4):
                for half in range(2):
                    def mk(tt=tt, half=half, m=m, e=e):
                        def evac(eng, pa):
                            tg = 4 * m + tt
                            xs = x_sb[:, tg * D + half * 512: tg * D + (half + 1) * 512]
                            return eng.scalar_tensor_tensor(
                                xs, pa, comb[:, tg * 8 + e:tg * 8 + e + 1], xs,
                                ALU.mult, ALU.add)
                        return evac
                    jobs.append(dict(
                        mms=[("m", bufD[:, fc * 512 + tt * 128:fc * 512 + (tt + 1) * 128],
                              bufB[:, fc * 1024 + half * 512:fc * 1024 + (half + 1) * 512])
                             for fc in range(16)],
                        out_w=512, engine="vector", evac=mk()))
            mm_grid(p, jobs, psums, tag=f"y{e}_{m}")

    emit_ln(p, x_sb)
    emit_store_x(p, x_sb, x_out)
    emit_bf16_tiles(p, x_sb, bufA)
    emit_to_xT(p, bufA, bufC, ident, psb16[0], psb16[1])

    dma_block(p, [(bufA[:, c * 1024:(c + 1) * 1024],
                   wqkvT_in[c * 128:(c + 1) * 128, 0:1024]) for c in range(8)],
              queue="gpsimd")
    jobs = []
    ji = 0
    for f in range(8):
        for m in range(4):
            sl = ji % 6
            ji += 1
            def mk(f=f, m=m, sl=sl):
                def evac(eng, pa):
                    return eng.activation(stg[:, sl * 512:(sl + 1) * 512], pa, AF.Exp)
                return evac
            jobs.append(dict(
                mms=[("m", bufA[:, c * 1024 + f * 128:c * 1024 + (f + 1) * 128],
                      chunkT(bufC, c, m * 512, 512)) for c in range(8)],
                out_w=512, engine="scalar", evac=mk(),
                dma=(qt_out[f * 128:(f + 1) * 128, m * 512:(m + 1) * 512],
                     stg[:, sl * 512:(sl + 1) * 512])))
    mm_grid(p, jobs, psums, tag="l1aq", nstage=6)
    for which, off, out_ap, act in (("k", 1024, ktm_out, AF.Exp),
                                    ("v", 2048, vtm_out, None)):
        dma_block(p, [(bufA[:, c * 1024:(c + 1) * 1024],
                       wqkvT_in[c * 128:(c + 1) * 128, off:off + 1024])
                      for c in range(8)], queue="gpsimd")
        jobs = []
        ji = 0
        for tt in range(16):
            for half in range(2):
                sl = ji % 6
                ji += 1
                def mk(sl=sl, act=act):
                    def evac(eng, pa):
                        if act is None:
                            return eng.copy(stg[:, sl * 512:(sl + 1) * 512], pa)
                        return eng.activation(stg[:, sl * 512:(sl + 1) * 512], pa, act)
                    return evac
                jobs.append(dict(
                    mms=[("m", chunkT(bufC, c, tt * 128, 128),
                          bufA[:, c * 1024 + half * 512:c * 1024 + (half + 1) * 512])
                         for c in range(8)],
                    out_w=512, engine="scalar", evac=mk(),
                    dma=(out_ap[tt * 128:(tt + 1) * 128, half * 512:(half + 1) * 512],
                         stg[:, sl * 512:(sl + 1) * 512])))
        mm_grid(p, jobs, psums, tag=f"l1a{which}", nstage=6)

    onescl = p.sbuf("onescl", [128, 1], BF16)
    totb = p.sbuf("totb", [64, 16 * 65], F32)
    with p.nc.Block() as block:
        @block.vector
        def _(v):
            v.memset(onescl[:, :], 1.0)
    tl = p.sem("t0btl")
    tm_ = p.sem("t0btm")
    te_ = p.sem("t0bte")
    groups0 = [list(range(7)), list(range(7, 14)), [14, 15]]
    with p.nc.Block() as block:
        @block.sync
        def _(sync):
            for tt in range(NT):
                if tt >= 2:
                    sync.wait_ge(tm_.h, 16 * (tt - 1))
                sl = (tt % 2) * 1024
                tl.inc(sync.dma_start(out=bufA[:, sl:sl + 1024],
                                      in_=ktm_out[tt * 128:(tt + 1) * 128, :]), 16)
                tl.inc(sync.dma_start(out=bufB[:, sl:sl + 1024],
                                      in_=vtm_out[tt * 128:(tt + 1) * 128, :]), 16)
        @block.tensor
        def _(tensor):
            for tt in range(NT):
                sl = (tt % 2) * 1024
                tensor.wait_ge(tl.h, 32 * (tt + 1))
                for gi, g in enumerate(groups0):
                    for j, h in enumerate(g):
                        kb = bufA[:, sl + 64 * h:sl + 64 * h + 64]
                        vb = bufB[:, sl + 64 * h:sl + 64 * h + 64]
                        tensor.matmul(psums[gi][0:64, 65 * j:65 * j + 64], kb, vb,
                                      start=(tt == 0), stop=(tt == NT - 1))
                        tm_.inc(tensor.matmul(psums[gi][0:64, 65 * j + 64:65 * j + 65],
                                              kb, onescl[:, :],
                                              start=(tt == 0), stop=(tt == NT - 1)), 1)
        @block.vector
        def _(v):
            v.wait_ge(tm_.h, tm_.n)
            for gi, g in enumerate(groups0):
                for j, h in enumerate(g):
                    te_.inc(v.tensor_copy(totb[:, 65 * h:65 * h + 65],
                                          psums[gi][0:64, 65 * j:65 * j + 65]), 1)
    dma_block(p, [(tot_out, totb[:, :])])
    return p


def prep_T0B(inputs, st):
    x = inputs["L0_attn_x"].reshape(B * S, D)
    wo = f32c(inputs["L0_attn_W_O"])
    wg = f32c(inputs["L0_ffn_Wg"])
    w1 = f32c(inputs["L0_ffn_W1"])
    w2 = f32c(inputs["L0_ffn_W2"])
    wqkvT = f32c(inputs["L1_attn_qkv_w"].T)
    ident = host_ident128()
    ao = np.zeros((D, B * S), BF)
    for c in range(NC):
        a = st["ao_units"][c]
        for bi in range(2):
            for hl in range(2):
                h = 2 * c + hl
                ao[64 * h:64 * h + 64, bi * S:(bi + 1) * S] = a[bi, hl]
    maps = []
    for c in range(NC):
        t0 = c * TLOC
        maps.append(dict(x=f32c(x[t0:t0 + TLOC]), aoT=np.ascontiguousarray(ao[:, t0:t0 + TLOC]),
                         wo=wo, wg=wg, w1=w1, w2=w2, wqkvT=wqkvT, ident=ident))
    return maps



EPS_L1A = 0.001
EPS_RMS = 1.1920929e-07


def emit_rms_cols(p: Prog, x_sb, rcols, eps, n_tiles=NT, cols=D):
    u = p.uid()
    st = p.sbuf(f"rmst{u}", [128, 12], F32)
    mv = p.sbuf(f"rmv{u}", [128, 6], F32)
    t4 = p.sbuf(f"rmt4{u}", [128, 4], F32)
    epsb = p.constcol(eps)
    sv = p.sem("lnsv")
    ss = p.sem("lnss")
    fl = p.sem("lnfl")
    sv_b, ss_b = sv.n, ss.n
    with p.nc.Block() as block:
        @block.vector
        def _(v):
            def flush(ins):
                fl.inc(ins, 1)
                v.wait_ge(fl.h, fl.n)
            for t in range(n_tiles):
                xt = x_sb[:, t * cols:(t + 1) * cols]
                v.bn_stats(st[:, 0:6], xt[:, 0:512])
                flush(v.bn_stats(st[:, 6:12], xt[:, 512:1024]))
                means = bass.AP(st, 1, [[12, 128], [3, 4]])
                m2s = bass.AP(st, 2, [[12, 128], [3, 4]])
                v.reduce_sum(mv[:, 1:2], m2s, axis=AX)
                flush(v.tensor_mul(t4[:, 0:4], means, means))
                flush(v.reduce_sum(mv[:, 2:3], t4[:, 0:4], axis=AX))
                sv.inc(v.scalar_tensor_tensor(mv[:, 3:4], mv[:, 2:3], 256.0,
                                              mv[:, 1:2], ALU.mult, ALU.add), 1)
        @block.scalar
        def _(sc):
            for t in range(n_tiles):
                sc.wait_ge(sv.h, sv_b + t + 1)
                sc.activation(mv[:, 4:5], mv[:, 3:4], AF.Ln, bias=epsb,
                              scale=1.0 / 1024.0)
                sc.activation(rcols[:, t:t + 1], mv[:, 4:5], AF.Exp, scale=-0.5)


def emit_toT_scaled(p: Prog, xb_sb, xT_sb, ident, psA, psB, gain_cols=None,
                    n_tiles=NT, cols=D, ntok=None):
    if ntok is None:
        ntok = n_tiles * 128
    jobs = []
    nch = cols // 128
    for c in range(nch):
        for t in range(n_tiles):
            def mk(c=c, t=t):
                def evac(eng, pa):
                    dst = xT_sb[:, c * ntok + t * 128: c * ntok + (t + 1) * 128]
                    if gain_cols is None:
                        return eng.tensor_copy(dst, pa)
                    return eng.tensor_single_scalar(dst, pa, gain_cols(c), ALU.mult)
                return evac
            jobs.append(dict(
                mms=[("t", xb_sb[:, t * cols + c * 128: t * cols + (c + 1) * 128],
                      ident[:, :])],
                out_w=128, engine="vector", evac=mk()))
    mm_grid(p, jobs, [psA, psB])


def build_T1():
    import os
    PH = int(os.environ.get("T1_PHASES", "9"))
    p = Prog("T1")
    x_in = p.dram_in("x", [TLOC, D]).ap()
    qt_in = p.dram_in("qt", [D, TLOC], BF16).ap()
    ktm_in = p.dram_in("ktm", [TLOC, D], BF16).ap()
    vtm_in = p.dram_in("vtm", [TLOC, D], BF16).ap()
    pt_in = [p.dram_in(f"ptot{i}", [64, 16 * 65]).ap() for i in range(3)]
    pwT_in = p.dram_in("projwT", [D, D]).ap()
    pb_in = p.dram_in("projb", [128, 8]).ap()
    n1_in = p.dram_in("n1w", [128, 8]).ap()
    wqT_in = p.dram_in("wqT", [D, D]).ap()
    wkT_in = p.dram_in("wkT", [D, D]).ap()
    wvT_in = p.dram_in("wvT", [D, D]).ap()
    qn_in = p.dram_in("qn2", [128, 1]).ap()
    kn_in = p.dram_in("kn2", [128, 1]).ap()
    identi = p.dram_in("ident", [128, 128], BF16).ap()

    x_out = p.dram_out("x3", [TLOC, D]).ap()
    tot_out = p.dram_out("totals", [64, 16 * 65]).ap()
    qh_out = p.dram_out("qh", [D, TLOC], BF16).ap()
    kh_out = p.dram_out("kh", [D, TLOC], BF16).ap()
    vh_out = p.dram_out("vh", [D, TLOC], BF16).ap()
    kss_out = p.dram_out("kss", [16, TLOC]).ap()

    p.make_arena(212480)
    x_sb = p.sbuf_at("xsb", [128, NT * D], F32, 0)
    OFF_A, OFF_B, OFF_C, OFF_W = 65536, 98304, 131072, 163840
    qA = p.sbuf_at("qA", [128, 8 * TLOC], BF16, OFF_A)
    yT = p.sbuf_at("yT", [128, 8 * TLOC], BF16, OFF_A)
    qhT = p.sbuf_at("qhT", [128, 8 * TLOC], BF16, OFF_A)
    at1 = p.sbuf_at("at1", [128, NT * D], BF16, OFF_B)
    xtT = p.sbuf_at("xtT", [128, 8 * TLOC], BF16, OFF_B)
    at1T = p.sbuf_at("at1T", [128, 8 * TLOC], BF16, OFF_C)
    khT = p.sbuf_at("khT", [128, 8 * TLOC], BF16, OFF_C)
    wW = p.sbuf_at("wW", [128, 8 * 1024], BF16, OFF_W)
    OFF_KV = OFF_W + 16384
    kt = p.sbuf_at("kt", [128, 2 * 1024], BF16, OFF_KV)
    vt = p.sbuf_at("vt", [128, 2 * 1024], BF16, OFF_KV + 4096)
    OFF_R = OFF_KV + 8192
    runb = p.sbuf_at("runb", [64, 16 * 66], F32, OFF_R)
    ptmp = p.sbuf_at("ptmp", [64, 16 * 65], F32, OFF_R + 4224)
    runbb = p.sbuf_at("runbb", [128, 16 * 66], BF16, OFF_R + 4224 + 4160)
    OFF_M = OFF_R + 4224 + 4160 + 2144
    kodd = p.sbuf_at("kodd", [64, 2 * 1024], BF16, OFF_R + 4224)
    vodd = p.sbuf_at("vodd", [64, 2 * 1024], BF16, OFF_M)
    stg = p.sbuf_at("stg", [128, 4 * 512], BF16, OFF_M)
    sml = p.sbuf_at("sml", [128, 24], F32, OFF_M + 4096)
    onesc = p.sbuf_at("onesc", [128, 1], BF16, OFF_M + 4224)
    bd = p.sbuf_at("bd", [128, 2], BF16, OFF_M + 4256)
    sel = p.sbuf_at("sel", [2, 128], BF16, OFF_M + 4288)
    rstd2 = p.sbuf_at("rstd2", [2, 2 * 512], BF16, OFF_M + 4544)
    sq = p.sbuf_at("sq", [128, 2 * 512], BF16, OFF_M + 6592)
    rc16 = p.sbuf_at("rc16", [128, 16], F32, OFF_M + 8640)
    ksg = p.sbuf_at("ksg", [2, 2 * 512], F32, OFF_M + 8704)
    pb8 = p.sbuf_at("pb8", [128, 8], F32, OFF_M + 12800)
    n1c = p.sbuf_at("n1c", [128, 8], F32, OFF_M + 12832)
    qnc = p.sbuf_at("qnc", [128, 1], F32, OFF_M + 12864)
    knc = p.sbuf_at("knc", [128, 1], F32, OFF_M + 12896)
    qkn = p.sbuf_at("qkn", [128, 1], F32, OFF_M + 12928)
    ident = p.sbuf_at("id", [128, 128], BF16, OFF_M + 12960)

    psums = [p.psum(f"ps{i}", [128, 512]) for i in range(4)]
    psb16 = [p.psum(f"psb{i}", [128, 512], BF16) for i in range(2)]

    dma_block(p, [(ident[:, :], identi), (qnc[:, :], qn_in), (knc[:, :], kn_in)])
    dma_block(p, [(qA[:, c * TLOC:(c + 1) * TLOC], qt_in[c * 128:(c + 1) * 128, :])
                  for c in range(8)], queue="gpsimd")
    dma_block(p, [(pb8[:, :], pb_in), (n1c[:, :], n1_in)], queue="gpsimd")
    with p.nc.Block() as block:
        @block.vector
        def _(v):
            v.memset(onesc[:, :], 1.0)
            v.memset(bd[:, :], 0.0)
            v.memset(bd[0:64, 0:1], 1.0)
            v.memset(bd[64:128, 1:2], 1.0)
            v.memset(runb[:, :], 0.0)
            v.memset(bass.AP(runb, 65, [[16 * 66, 64], [66, 16]]), 1.0)
            v.tensor_mul(qkn[:, :], qnc[:, :], knc[:, :])
    with p.nc.Block() as block:
        @block.tensor
        def _(t):
            t.transpose(psb16[0][0:2, 0:128], bd[:, 0:2], ident[:, :])
        @block.vector
        def _(v):
            pass
    with p.nc.Block() as block:
        @block.vector
        def _(v):
            v.tensor_copy(sel[0:2, :], psb16[0][0:2, 0:128])
    for i in range(3):
        dma_block(p, [(ptmp[:, :], pt_in[i])])
        with p.nc.Block() as block:
            @block.vector
            def _(v):
                dst = bass.AP(runb, 0, [[16 * 66, 64], [66, 16], [1, 65]])
                srcp = bass.AP(ptmp, 0, [[16 * 65, 64], [65, 16], [1, 65]])
                v.tensor_tensor(dst, dst, srcp, ALU.add)

    with p.nc.Block() as block:
        @block.vector
        def _(v):
            v.memset(runbb[:, :], 0.0)
            for h in range(16):
                b = 64 * (h % 2)
                v.tensor_copy(runbb[b:b + 64, 66 * h:66 * h + 66],
                              runb[:, 66 * h:66 * h + 66])

    if PH < 2:
        emit_load_x(p, x_in, x_sb)
        emit_store_x(p, x_sb, x_out)
        return p
    N_U = int(os.environ.get("T1_NU", "32"))
    sc = p.sem("t1sc")
    am = p.sem("t1am")
    cm = p.sem("t1cm")
    ev = p.sem("t1ev")
    eb = p.sem("t1eb")
    fl2 = p.sem("t1fl")
    groups = [list(range(7)), list(range(7, 14)), [14, 15]]
    with p.nc.Block() as block:
        @block.sync
        def _(sync):
            for tt in range((N_U + 1) // 2):
                if tt >= 2:
                    sync.wait_ge(cm.h, 16 * min(2 * tt - 2, N_U))
                sc.inc(sync.dma_start(out=kt[:, (tt % 2) * 1024:(tt % 2 + 1) * 1024],
                                      in_=ktm_in[tt * 128:(tt + 1) * 128, :]), 16)
                sc.inc(sync.dma_start(out=vt[:, (tt % 2) * 1024:(tt % 2 + 1) * 1024],
                                      in_=vtm_in[tt * 128:(tt + 1) * 128, :]), 16)
                sc.inc(sync.dma_start(out=kodd[:, (tt % 2) * 1024:(tt % 2 + 1) * 1024],
                                      in_=ktm_in[tt * 128 + 64:(tt + 1) * 128, :]), 16)
                sc.inc(sync.dma_start(out=vodd[:, (tt % 2) * 1024:(tt % 2 + 1) * 1024],
                                      in_=vtm_in[tt * 128 + 64:(tt + 1) * 128, :]), 16)

        @block.tensor
        def _(tensor):
            nA = 0
            for u in range(N_U):
                tt = u // 2
                tensor.wait_ge(sc.h, 64 * (tt + 1))
                pb = 64 * (u % 2)
                kofs = (tt % 2) * 1024 + pb
                for gi, g in enumerate(groups):
                    bank = (u * 3 + gi) % 2
                    idx = u * 3 + gi
                    if idx >= 2:
                        tensor.wait_ge(ev.h, idx - 1)
                        tensor.wait_ge(eb.h, idx - 1)
                    for j, h in enumerate(g):
                        if os.environ.get("T1_A", "1") == "0":
                            am.inc(tensor.matmul(
                                psums[bank][0:64, 66 * j:66 * j + 66],
                                qA[0:64, 0:64], runbb[0:64, 0:66],
                                start=True, stop=True), 1)
                            continue
                        am.inc(tensor.matmul(
                            psums[bank][pb:pb + 64, 66 * j:66 * j + 66],
                            qA[:, (h // 2) * TLOC + 64 * u:(h // 2) * TLOC + 64 * u + 64],
                            runbb[:, 66 * h:66 * h + 66],
                            start=True, stop=True), 1)
                    for j, h in enumerate(g):
                        cl = (tt % 2) * 1024 + 64 * h
                        if u % 2 == 0:
                            kb = kt[0:64, cl:cl + 64]
                            vb = vt[0:64, cl:cl + 64]
                        else:
                            kb = kodd[0:64, cl:cl + 64]
                            vb = vodd[0:64, cl:cl + 64]
                        if os.environ.get("T1_CTX", "1") == "0":
                            cm.inc(tensor.matmul(psums[2 + bank][0:64, 65 * j:65 * j + 64],
                                                 kb, vb, start=True, stop=True), 1)
                            continue
                        tensor.matmul(psums[2 + bank][0:64, 65 * j:65 * j + 64],
                                      kb, vb, start=True, stop=True)
                        cm.inc(tensor.matmul(psums[2 + bank][0:64, 65 * j + 64:65 * j + 65],
                                             kb, onesc[0:64, :],
                                             start=True, stop=True), 1)

        @block.vector
        def _(v):
            def flush(ins):
                fl2.inc(ins, 1)
                v.wait_ge(fl2.h, fl2.n)
            na = 0
            nc_ = 0
            for u in range(N_U):
                tt = u // 2
                pb = 64 * (u % 2)
                for gi, g in enumerate(groups):
                    bank = (u * 3 + gi) % 2
                    na += len(g)
                    nc_ += len(g)
                    v.wait_ge(am.h, na)
                    pa = psums[bank]
                    import os as _os
                    if _os.environ.get("T1_EVAC", "1") == "0":
                        last = v.tensor_copy(at1[pb:pb + 64, tt * D:tt * D + 64],
                                             pa[pb:pb + 64, 0:64])
                    else:
                        pjs = [pa[pb:pb + 64, 66 * j:66 * j + 66]
                               for j in range(len(g))]
                        for j in range(len(g)):
                            fl2.inc(v.tensor_single_scalar(
                                sml[pb:pb + 64, j:j + 1], pjs[j][:, 65:66],
                                8.0 * EPS_L1A, ALU.mult), 1)
                        v.wait_ge(fl2.h, fl2.n)
                        for j in range(len(g)):
                            fl2.inc(v.tensor_tensor(
                                sml[pb:pb + 64, 8 + j:9 + j], pjs[j][:, 64:65],
                                sml[pb:pb + 64, j:j + 1], ALU.max), 1)
                        v.wait_ge(fl2.h, fl2.n)
                        for j in range(len(g)):
                            fl2.inc(v.reciprocal(sml[pb:pb + 64, 16 + j:17 + j],
                                                 sml[pb:pb + 64, 8 + j:9 + j]), 1)
                        v.wait_ge(fl2.h, fl2.n)
                        for j, h in enumerate(g):
                            last = v.tensor_single_scalar(
                                at1[pb:pb + 64, tt * D + 64 * h:tt * D + 64 * h + 64],
                                pjs[j][:, 0:64], sml[pb:pb + 64, 16 + j:17 + j],
                                ALU.mult)
                    ev.inc(last, 1)
                    v.wait_ge(cm.h, nc_)
                    pc = psums[2 + bank]
                    n_g = len(g)
                    g0 = g[0]
                    rb_all = bass.AP(runb, 66 * g0,
                                     [[16 * 66, 64], [66, n_g], [1, 65]])
                    pc3 = pc[0:64, 0:65 * n_g].rearrange("p (j c) -> p j c",
                                                         c=65)
                    fl2.inc(v.tensor_tensor(rb_all, rb_all, pc3, ALU.add), 1)
                    v.wait_ge(fl2.h, fl2.n)
                    ev_h = [h for h in g if h % 2 == 0]
                    od_h = [h for h in g if h % 2 == 1]
                    if ev_h:
                        e0 = ev_h[0]
                        dste = bass.AP(runbb, 66 * e0,
                                       [[16 * 66, 64], [132, len(ev_h)], [1, 65]])
                        srce = bass.AP(runb, 66 * e0,
                                       [[16 * 66, 64], [132, len(ev_h)], [1, 65]])
                        v.tensor_copy(dste, srce)
                    for h in od_h:
                        last = v.tensor_copy(
                            runbb[64:128, 66 * h:66 * h + 65],
                            runb[:, 66 * h:66 * h + 65])
                    eb.inc(last, 1)
    if PH < 3:
        emit_load_x(p, x_in, x_sb)
        emit_store_x(p, x_sb, x_out)
        return p
    with p.nc.Block() as block:
        @block.vector
        def _(v):
            dst = bass.AP(ptmp, 0, [[16 * 65, 64], [65, 16], [1, 65]])
            srcr = bass.AP(runb, 0, [[16 * 66, 64], [66, 16], [1, 65]])
            v.tensor_copy(dst, srcr)
    dma_block(p, [(tot_out, ptmp[:, :])])

    if os.environ.get("T1_DBG", "0") == "1":
        dbg_out = p.dram_out("dbg_at1", [TLOC, D], BF16).ap()
        dma_block(p, [(dbg_out.rearrange("(t p) c -> p t c", p=128),
                       at1.ap().rearrange("p (t c) -> p t c", t=NT))])
    if PH < 4:
        emit_load_x(p, x_in, x_sb)
        emit_store_x(p, x_sb, x_out)
        return p
    emit_load_x(p, x_in, x_sb)
    emit_to_xT(p, at1, at1T, ident, psb16[0], psb16[1])
    jobs = []
    for f in range(8):
        for m in range(4):
            def mk(f=f, m=m):
                def evac(eng, pa):
                    return eng.tensor_single_scalar(chunkT(yT, f, m * 512, 512), pa,
                                                    pb8[:, f:f + 1], ALU.add)
                return evac
            jobs.append(dict(
                mms=[("m", wW[:, c * 1024 + f * 128:c * 1024 + (f + 1) * 128],
                      chunkT(at1T, c, m * 512, 512)) for c in range(8)],
                out_w=512, engine="vector", evac=mk()))
    dma_block(p, [(wW[:, c * 1024:(c + 1) * 1024], pwT_in[c * 128:(c + 1) * 128, :])
                  for c in range(8)], queue="gpsimd")
    if os.environ.get("T1_DBG", "0") == "1":
        dbg3 = p.dram_out("dbg_at1T", [D, TLOC], BF16).ap()
        dma_block(p, [(dbg3[c * 128:(c + 1) * 128, :], at1T[:, c * TLOC:(c + 1) * TLOC])
                      for c in range(8)])
    mm_grid(p, jobs, psums, tag="l1aproj")
    if os.environ.get("T1_DBG", "0") == "1":
        dbg2 = p.dram_out("dbg_yT", [D, TLOC], BF16).ap()
        dma_block(p, [(dbg2[c * 128:(c + 1) * 128, :], yT[:, c * TLOC:(c + 1) * TLOC])
                      for c in range(8)])
    emit_addT_into_x(p, yT, x_sb, ident, psb16)
    if os.environ.get("T1_NOLN", "0") != "1":
        emit_ln(p, x_sb)
    emit_store_x(p, x_sb, x_out)

    if PH < 5:
        return p
    emit_rms_cols(p, x_sb, rc16, EPS_RMS)
    xb2 = at1T
    with p.nc.Block() as block:
        @block.scalar
        def _(scb):
            for t in range(NT):
                scb.activation(xb2[:, t * D:(t + 1) * D], x_sb[:, t * D:(t + 1) * D],
                               AF.Copy, scale=rc16[:, t:t + 1])
    emit_toT_scaled(p, xb2, xtT, ident, psb16[0], psb16[1],
                    gain_cols=lambda c: n1c[:, c:c + 1])

    if PH < 6:
        return p
    for wap, dst, out_ap in ((wqT_in, qhT, None), (wkT_in, khT, None),
                             (wvT_in, None, vh_out)):
        dma_block(p, [(wW[:, c * 1024:(c + 1) * 1024], wap[c * 128:(c + 1) * 128, :])
                      for c in range(8)], queue="gpsimd")
        jobs = []
        ji = 0
        for f in range(8):
            for m in range(4):
                if dst is not None:
                    def mk(f=f, m=m, dst=dst):
                        def evac(eng, pa):
                            return eng.copy(chunkT(dst, f, m * 512, 512), pa)
                        return evac
                    jobs.append(dict(
                        mms=[("m", wW[:, c * 1024 + f * 128:c * 1024 + (f + 1) * 128],
                              chunkT(xtT, c, m * 512, 512)) for c in range(8)],
                        out_w=512, engine="scalar", evac=mk()))
                else:
                    sl = ji % 4
                    ji += 1
                    def mk(sl=sl):
                        def evac(eng, pa):
                            return eng.copy(stg[:, sl * 512:(sl + 1) * 512], pa)
                        return evac
                    jobs.append(dict(
                        mms=[("m", wW[:, c * 1024 + f * 128:c * 1024 + (f + 1) * 128],
                              chunkT(xtT, c, m * 512, 512)) for c in range(8)],
                        out_w=512, engine="scalar", evac=mk(),
                        dma=(out_ap[f * 128:(f + 1) * 128, m * 512:(m + 1) * 512],
                             stg[:, sl * 512:(sl + 1) * 512])))
        mm_grid(p, jobs, psums, tag="qkv", nstage=(4 if dst is None else 0))

    if PH < 7:
        return p
    epsr = p.constcol(EPS_RMS, rows=2)
    sqm = p.sem("t1sqm")
    ssm = p.sem("t1ssm")
    rsm = p.sem("t1rsm")
    bcm = p.sem("t1bcm")
    apm = p.sem("t1apm")
    k2m = p.sem("t1k2m")
    ksm = p.sem("t1ksm")
    kem = p.sem("t1kem")
    kdm = p.sem("t1kdm")
    sfm = p.sem("t1sfm")
    for side, buf, gain in (("q", qhT, None), ("k", khT, qkn)):
        sq_b, ss_b, rs_b, bc_b, ap_b, k2_b, ks_b, ke_b, kd_b = (
            sqm.n, ssm.n, rsm.n, bcm.n, apm.n, k2m.n, ksm.n, kem.n, kdm.n)
        with p.nc.Block() as block:
            @block.scalar
            def _(s):
                i = 0
                for c in range(8):
                    for m in range(4):
                        if i >= 2:
                            s.wait_ge(ssm.h, ss_b + i - 1)
                            if side == "k":
                                s.wait_ge(ksm.h, ks_b + i - 1)
                        sqm.inc(s.activation(sq[:, (i % 2) * 512:(i % 2 + 1) * 512],
                                             chunkT(buf, c, m * 512, 512), AF.Square), 1)
                        s.wait_ge(ssm.h, ss_b + i + 1)
                        if i >= 2:
                            s.wait_ge(bcm.h, bc_b + i - 1)
                        rsl = rstd2[:, (i % 2) * 512:(i % 2 + 1) * 512]
                        sfm.inc(s.activation(rsl, psums[2][0:2, 0:512], AF.Ln,
                                             bias=epsr, scale=1.0 / 64.0), 1)
                        s.wait_ge(sfm.h, sfm.n)
                        rsm.inc(s.activation(rsl, rsl, AF.Exp, scale=-0.5), 1)
                        if side == "k":
                            s.wait_ge(apm.h, ap_b + i + 1)
                            k2m.inc(s.activation(sq[:, (i % 2) * 512:(i % 2 + 1) * 512],
                                                 chunkT(buf, c, m * 512, 512),
                                                 AF.Square), 1)
                        i += 1
            @block.tensor
            def _(tensor):
                i = 0
                for c in range(8):
                    for m in range(4):
                        tensor.wait_ge(sqm.h, sq_b + i + 1)
                        ssm.inc(tensor.matmul(psums[2][0:2, 0:512],
                                              bd[:, 0:2],
                                              sq[:, (i % 2) * 512:(i % 2 + 1) * 512],
                                              start=True, stop=True), 1)
                        tensor.wait_ge(rsm.h, rs_b + i + 1)
                        bcm.inc(tensor.matmul(psums[3][0:128, 0:512],
                                              sel[0:2, :],
                                              rstd2[:, (i % 2) * 512:(i % 2 + 1) * 512],
                                              start=True, stop=True), 1)
                        if side == "k":
                            tensor.wait_ge(k2m.h, k2_b + i + 1)
                            ksm.inc(tensor.matmul(psums[2][0:2, 0:512],
                                                  bd[:, 0:2],
                                                  sq[:, (i % 2) * 512:(i % 2 + 1) * 512],
                                                  start=True, stop=True), 1)
                            tensor.wait_ge(kem.h, ke_b + i + 1)
                        else:
                            tensor.wait_ge(apm.h, ap_b + i + 1)
                        i += 1
            @block.vector
            def _(v):
                i = 0
                for c in range(8):
                    for m in range(4):
                        v.wait_ge(bcm.h, bc_b + i + 1)
                        sl = chunkT(buf, c, m * 512, 512)
                        if gain is None:
                            apm.inc(v.tensor_tensor(sl, sl, psums[3][0:128, 0:512],
                                                    ALU.mult), 1)
                        else:
                            apm.inc(v.scalar_tensor_tensor(sl, sl, gain[:, :],
                                                           psums[3][0:128, 0:512],
                                                           ALU.mult, ALU.mult), 1)
                        if side == "k":
                            v.wait_ge(ksm.h, ks_b + i + 1)
                            if i >= 2:
                                v.wait_ge(kdm.h, kd_b + 16 * (i - 1))
                            kem.inc(v.tensor_copy(ksg[0:2, (i % 2) * 512:(i % 2 + 1) * 512],
                                                  psums[2][0:2, 0:512]), 1)
                        i += 1
            if side == "k":
                @block.sync
                def _(sync):
                    i = 0
                    for c in range(8):
                        for m in range(4):
                            sync.wait_ge(kem.h, ke_b + i + 1)
                            kdm.inc(sync.dma_start(
                                out=kss_out[2 * c:2 * c + 2, m * 512:(m + 1) * 512],
                                in_=ksg[0:2, (i % 2) * 512:(i % 2 + 1) * 512]), 16)
                            i += 1
                    sync.wait_ge(kdm.h, kdm.n)
    dma_block(p, [(qh_out[c * 128:(c + 1) * 128, :], qhT[:, c * TLOC:(c + 1) * TLOC])
                  for c in range(8)]
              + [(kh_out[c * 128:(c + 1) * 128, :], khT[:, c * TLOC:(c + 1) * TLOC])
                 for c in range(8)])
    return p


def prep_T1(inputs, st):
    maps = []
    ident = host_ident128()
    pwT = f32c(inputs["L1_attn_proj_w"])
    pwT = f32c(inputs["L1_attn_proj_w"].T)
    pb = f32c(inputs["L1_attn_proj_b"].reshape(8, 128).T)
    n1 = f32c(inputs["L1_ffn_n1_w"].reshape(8, 128).T)
    wqT = f32c(inputs["L1_ffn_wq"].T)
    wkT = f32c(inputs["L1_ffn_wk"].T)
    wvT = f32c(inputs["L1_ffn_wv"].T)
    qn2 = f32c(np.tile(inputs["L1_ffn_qn_w"], 2).reshape(128, 1))
    kn2 = f32c(np.tile(inputs["L1_ffn_kn_w"], 2).reshape(128, 1))
    zeros_tot = np.zeros((64, 16 * 65), np.float32)
    for c in range(NC):
        g0 = 4 * (c // 4)
        preds = [st["totals"][j] for j in range(g0, c)]
        while len(preds) < 3:
            preds.append(zeros_tot)
        m = dict(x=st["x2"][c], qt=st["qt"][c], ktm=st["ktm"][c], vtm=st["vtm"][c],
                 projwT=pwT, projb=pb, n1w=n1, wqT=wqT, wkT=wkT, wvT=wvT,
                 qn2=qn2, kn2=kn2, ident=ident)
        for i in range(3):
            m[f"ptot{i}"] = preds[i]
        maps.append(m)
    return maps




def build_F1():
    p = Prog("F1")
    q_in = p.dram_in("q", [2, 2, 64, S], BF16).ap()
    k_in = p.dram_in("k", [2, 2, 64, S], BF16).ap()
    v_in = p.dram_in("v", [2, 2, S, 64], BF16).ap()
    ks_in = p.dram_in("kss", [2, 2, S]).ap()
    m_in = p.dram_in("masks", [4, 128, 512], BF16).ap()
    ao_o = p.dram_out("ao", [2, 2, 64, S], BF16).ap()

    q_sb = [p.sbuf(f"q{i}", [96, S], BF16) for i in range(2)]
    k_sb = [p.sbuf(f"k{i}", [96, S], BF16) for i in range(2)]
    vh_sb = [p.sbuf(f"vh{i}", [128, 64 * 65], BF16) for i in range(4)]
    kssb = p.sbuf("kssb", [1, S], BF16)
    msk = p.sbuf("msk", [128, 4 * 512], BF16)
    ones64 = p.sbuf("ones64", [1, 64], BF16)
    ones128 = p.sbuf("ones128", [1, 128], BF16)
    csml = p.sbuf("csml", [1, 16], F32)
    cbf = p.sbuf("cbf", [1, 4], BF16)
    biasc = p.sbuf("biasc", [128, 4], F32)
    pt = [p.sbuf(f"pt{i}", [128, 512], BF16) for i in range(8)]
    stg = [dict(rd=p.sbuf(f"rd{t}", [1, 2 * 512], BF16),
                rf=p.sbuf(f"rf{t}", [1, 512], F32),
                num=p.sbuf(f"num{t}", [64, 2 * 512], BF16),
                ao=p.sbuf(f"aos{t}", [64, 2 * 512], BF16)) for t in range(2)]
    ps_s = [[p.psum(f"pss{t}{i}", [128, 512]) for i in range(2)] for t in range(2)]
    ps_o = [p.psum(f"pso{t}", [65, 512]) for t in range(2)]
    ps_b = [p.psum(f"psb{t}", [128, 512]) for t in range(2)]

    units = [(b, h) for b in range(2) for h in range(2)]
    lds = [p.sem("fldA"), p.sem("fldB")]
    ms = p.sem("fms")
    kl = p.sem("fkl")
    fc = p.sem("fcs")
    fs = p.sem("fsq")
    fv = p.sem("fvn")
    fm2 = p.sem("fm2")
    fv2 = p.sem("fv2")
    dma_block(p, [(msk[:, 512 * r:512 * (r + 1)], m_in[r]) for r in range(4)])

    with p.nc.Block() as block:
        @block.vector
        def _(v):
            v.memset(ones64[:, :], 1.0)
            v.memset(ones128[:, :], 1.0)
            for t in range(2):
                v.memset(q_sb[t][64:96, :], 0.0)
                v.memset(k_sb[t][64:96, :], 0.0)
            for u in range(4):
                ms.inc(v.memset(vh_sb[u][:, :], 1.0), 1)

    with p.nc.Block() as block:
        @block.gpsimd
        def _(g):
            for u, (b, h) in enumerate(units):
                if u >= 1:
                    g.wait_ge(fc.h, u)
                kl.inc(g.dma_start(out=kssb[:, :], in_=ks_in[b:b + 1, h, :]), 16)

        @block.vector
        def _(v):
            for u in range(4):
                v.wait_ge(kl.h, 16 * (u + 1))
                fc.inc(v.reduce_max(csml[:, u:u + 1], kssb[:, :], axis=AX), 1)

    with p.nc.Block() as block:
        @block.scalar
        def _(s):
            for u in range(4):
                fs.inc(s.activation(csml[:, 4 + u:5 + u], csml[:, u:u + 1],
                                    AF.Sqrt), 1)

        @block.vector
        def _(v):
            v.wait_ge(fs.h, 4)
            for u in range(4):
                fv.inc(v.tensor_scalar_mul(cbf[:, u:u + 1],
                                           csml[:, 4 + u:5 + u], -1.0), 1)

    with p.nc.Block() as block:
        @block.tensor
        def _(t):
            t.wait_ge(fv.h, 4)
            for u in range(4):
                if u >= 2:
                    t.wait_ge(fv2.h, u - 1)
                fm2.inc(t.matmul(ps_b[u % 2][0:128, 0:1], ones128[0:1, :],
                                 cbf[:, u:u + 1], start=True, stop=True), 1)

        @block.vector
        def _(v):
            for u in range(4):
                v.wait_ge(fm2.h, u + 1)
                fv2.inc(v.tensor_copy(biasc[:, u:u + 1], ps_b[u % 2][0:128, 0:1]), 1)

    with p.nc.Block() as block:
        @block.sync
        def _(sync):
            sync.wait_ge(ms.h, 4)
            for u in (0, 1):
                b, h = units[u]
                lds[0].inc(sync.dma_start(out=q_sb[u][0:64, :], in_=q_in[b, h]), 16)
                lds[0].inc(sync.dma_start(out=k_sb[u][0:64, :], in_=k_in[b, h]), 16)

        @block.gpsimd
        def _(g):
            g.wait_ge(ms.h, 4)
            for u in range(4):
                b, h = units[u]
                dst = vh_sb[u].ap().rearrange("p (k c) -> p k c", c=65)[:, :, 0:64]
                src = v_in[b, h].rearrange("(k p) c -> p k c", p=128)
                lds[1].inc(g.dma_start(out=dst, in_=src), 16)

    for pi in range(2):
        if pi == 1:
            with p.nc.Block() as block:
                @block.sync
                def _(sync):
                    for u in (2, 3):
                        b, h = units[u]
                        lds[0].inc(sync.dma_start(out=q_sb[u % 2][0:64, :],
                                                  in_=q_in[b, h]), 16)
                        lds[0].inc(sync.dma_start(out=k_sb[u % 2][0:64, :],
                                                  in_=k_in[b, h]), 16)
                    sync.wait_ge(lds[0].h, lds[0].n)
        US = []
        for t in range(2):
            u = 2 * pi + t
            b, h = units[u]
            US.append(dict(
                dq=96, q=q_sb[t], k=k_sb[t], vhat=vh_sb[u],
                pt=pt[4 * t:4 * t + 4], ps_s=ps_s[t], ps_o=ps_o[t],
                ps_b=ps_b[t], bias=biasc[:, u:u + 1],
                ao_out=lambda m, b=b, h=h: ao_o[b, h, :, 512 * m:512 * (m + 1)],
                sems=flash_sems(p, t), **stg[t]))
        emit_flash_pair(p, US, msk=msk, ones64=ones64, exp_scale=0.125,
                        clip=None,
                        wait_lds=[(lds[0], 64 * (pi + 1)), (lds[1], 64)])
    return p


def prep_F1(st):
    maps = []
    masks = host_band_masks()
    ident = host_ident128()
    qh, kh, vh, kss = st["qh"], st["kh"], st["vh"], st["kss"]
    for c in range(NC):
        q = np.zeros((2, 2, 64, S), BF)
        k = np.zeros((2, 2, 64, S), BF)
        vv = np.zeros((2, 2, S, 64), BF)
        ks = np.zeros((2, 2, S), np.float32)
        for bi in range(2):
            for hl in range(2):
                h = 2 * c + hl
                q[bi, hl] = qh[64 * h:64 * h + 64, bi * S:(bi + 1) * S]
                k[bi, hl] = kh[64 * h:64 * h + 64, bi * S:(bi + 1) * S]
                vv[bi, hl] = vh[64 * h:64 * h + 64, bi * S:(bi + 1) * S].T
                ks[bi, hl] = kss[h, bi * S:(bi + 1) * S]
        maps.append(dict(q=q, k=k, v=vv, kss=ks, masks=masks))
    return maps




def emit_fm_grid(p, wbuf, src_sb, evac_fn, psums, fchunks=8, nmac=4,
                 cchunks=8, wstride=None, engine="scalar", nstage=0, tag=None):
    if wstride is None:
        wstride = fchunks * 128
    jobs = []
    for f in range(fchunks):
        for m in range(nmac):
            jobs.append(dict(
                mms=[("m", wbuf[:, c * wstride + f * 128:c * wstride + (f + 1) * 128],
                      chunkT(src_sb, c, m * 512, 512)) for c in range(cchunks)],
                out_w=512, engine=engine, evac=evac_fn(f, m)))
    mm_grid(p, jobs, psums, tag=tag, nstage=nstage)


def ld_wT(p, wbuf, w_in, nchunk, ncols, col0=0):
    dma_block(p, [(wbuf[:, c * ncols:(c + 1) * ncols],
                   w_in[c * 128:(c + 1) * 128, col0:col0 + ncols])
                  for c in range(nchunk)], queue="gpsimd")



def wloads_fm(bufA_t, w_in, cchunks, fchunks, col0=0, reg0=0):
    G = fchunks // 2
    F = fchunks * 128
    gstride = cchunks * 256
    halves = []
    for h in range(2):
        g0 = h * (G // 2)
        ng = G // 2
        prs = []
        for c in range(cchunks):
            src = w_in[c * 128:(c + 1) * 128,
                       col0 + h * (F // 2):col0 + (h + 1) * (F // 2)]
            src = src.rearrange("p (g j) -> p g j", j=256)
            dst = bass.AP(bufA_t, reg0 + g0 * gstride + c * 256,
                          [[16384, 128], [gstride, ng], [1, 256]])
            prs.append((dst, src))
        halves.append(prs)
    return halves


def wloads_dn(bufA, w_in):
    return [[(bufA[:, h * 8192 + fc * 512:h * 8192 + (fc + 1) * 512],
              w_in[fc * 128:(fc + 1) * 128, h * 512:(h + 1) * 512])
             for fc in range(16)] for h in range(2)]


def jobs_h_pm(bufA, src, dstbuf, dstoff, m, act, fchunks=16, cchunks=8,
              mul_into=None):
    gstride = cchunks * 256
    jobs = []
    for f in range(fchunks):
        def mk(f=f):
            def evac(eng, pa):
                dst = dstbuf[:, dstoff + f * 512:dstoff + (f + 1) * 512]
                if mul_into is not None:
                    return eng.tensor_mul(dst, pa, dst)
                if act is None:
                    return eng.copy(dst, pa)
                return eng.activation(dst, pa, act)
            return evac
        col = (f // 2) * gstride + (f % 2) * 128
        jobs.append(dict(
            mms=[("m", bufA[:, col + c * 256:col + c * 256 + 128],
                  chunkT(src, c, m * 512, 512)) for c in range(cchunks)],
            out_w=512, engine="vector" if mul_into is not None else "scalar",
            evac=mk()))
    return jobs


def jobs_h2_pm(bufA, hbuf, srcoff, dstoff, act, fchunks=8, cchunks=16):
    gstride = cchunks * 256
    jobs = []
    for f in range(fchunks):
        def mk(f=f):
            def evac(eng, pa):
                dst = hbuf[:, dstoff + f * 512:dstoff + (f + 1) * 512]
                if act is None:
                    return eng.copy(dst, pa)
                return eng.activation(dst, pa, act)
            return evac
        col = (f // 2) * gstride + (f % 2) * 128
        jobs.append(dict(
            mms=[("m", bufA[:, col + c * 256:col + c * 256 + 128],
                  hbuf[:, srcoff + c * 512:srcoff + c * 512 + 512])
                 for c in range(cchunks)],
            out_w=512, engine="scalar", evac=mk()))
    return jobs


def jobs_dn_pm(bufA, bufB, x_sb, comb, m, e, hoff=0):
    jobs = []
    for half in range(2):
        for tt in range(4):
            def mk(m=m, tt=tt, half=half, e=e):
                def evac(eng, pa):
                    tg = 4 * m + tt
                    c0 = 256 if half == 0 else 0
                    xs = x_sb[:, tg * D + half * 512 + c0:
                              tg * D + (half + 1) * 512]
                    return eng.scalar_tensor_tensor(
                        xs, pa[:, c0:512], comb[:, tg * 4 + e:tg * 4 + e + 1],
                        xs, ALU.mult, ALU.add)
                return evac
            jobs.append(dict(
                mms=[("m", bufB[:, hoff + fc * 512 + 128 * tt:
                                hoff + fc * 512 + 128 * tt + 128],
                      bufA[:, half * 8192 + fc * 512:half * 8192 + (fc + 1) * 512])
                     for fc in range(16)],
                out_w=512, engine="vector", evac=mk()))
    return jobs


def run_wstages(p, stages, psums):
    dg = p.sem("dma_gp")
    fr = p.sem("fr_shared")
    dma_block(p, [pr for _, prs in stages[0]["loads"] for pr in prs],
              queue="gpsimd")
    region_free = [0, 0]
    for s, stg in enumerate(stages):
        jobs = stg["jobs"]
        for ji, cnt in stg.get("_jw", []):
            jobs[ji]["wait"] = (dg, cnt)
        for reg, after in stg["free"].items():
            region_free[reg] = fr.n + after
        gp_loads = []
        if s + 1 < len(stages):
            nxt = stages[s + 1]
            run = dg.n
            jw = []
            for reg, prs in nxt["loads"]:
                gp_loads.append((region_free[reg], prs))
                run += 16 * len(prs)
                jw.append((nxt["jstart"][reg], run))
            nxt["_jw"] = jw
        mm_grid(p, jobs, psums, tag=stg["tag"], gp_loads=gp_loads)


def build_T2():
    import os
    PH = int(os.environ.get("T2_PHASES", "9"))
    p = Prog("T2")
    x_in = p.dram_in("x", [TLOC, D]).ap()
    ao_in = p.dram_in("aoT", [D, TLOC], BF16).ap()
    woT_in = p.dram_in("woT", [D, D]).ap()
    n2_in = p.dram_in("n2w", [128, 8]).ap()
    gwT_in = p.dram_in("gwT", [D, 4]).ap()
    e0u_in = p.dram_in("e0upT", [D, 2048]).ap()
    e0d_in = p.dram_in("e0dnT", [2048, D]).ap()
    e1a_in = p.dram_in("e1w1T", [D, 2048]).ap()
    e1b_in = p.dram_in("e1w2T", [D, 2048]).ap()
    e1d_in = p.dram_in("e1dnT", [2048, D]).ap()
    e2a_in = p.dram_in("e2l1T", [D, 2048]).ap()
    e2b_in = p.dram_in("e2l2T", [2048, 2048]).ap()
    e2c_in = p.dram_in("e2l3T", [2048, 2048]).ap()
    e2d_in = p.dram_in("e2l4T", [2048, D]).ap()
    e3a_in = p.dram_in("e3dnT", [D, D]).ap()
    e3b_in = p.dram_in("e3upT", [D, 2048]).ap()
    e3c_in = p.dram_in("e3outT", [2048, D]).ap()
    wq_in = p.dram_in("wqT2", [D, D]).ap()
    wk_in = p.dram_in("wkT2", [D, D]).ap()
    wv_in = p.dram_in("wvT2", [D, D]).ap()
    bq_in = p.dram_in("bq", [128, 8]).ap()
    bk_in = p.dram_in("bk", [128, 8]).ap()
    bv_in = p.dram_in("bv", [128, 8]).ap()
    identi = p.dram_in("ident", [128, 128], BF16).ap()

    x_out = p.dram_out("x4", [TLOC, D]).ap()
    q_out = p.dram_out("q2", [D, TLOC], BF16).ap()
    k_out = p.dram_out("k2", [D, TLOC], BF16).ap()
    v_out = p.dram_out("v2", [TLOC, D], BF16).ap()
    qs_out = p.dram_out("qss2", [1, TLOC]).ap()
    ks_out = p.dram_out("kss2", [1, TLOC]).ap()

    p.make_arena(200704)
    x_sb = p.sbuf_at("xsb", [128, NT * D], F32, 0)
    OFF_C, OFF_A, OFF_B = 65536, 98304, 131072
    bufC = p.sbuf_at("bufC", [128, 8 * TLOC], BF16, OFF_C)
    bufA = p.sbuf_at("bufA", [128, 16 * 1024], BF16, OFF_A)
    bufB = p.sbuf_at("bufB", [128, 16 * 1024], BF16, OFF_B)
    OFF_M = 163840
    stg = p.sbuf_at("stg", [128, 4 * 512], BF16, OFF_M)
    comb = p.sbuf_at("comb", [128, NT * 4], F32, OFF_M + 4096)
    lgT = p.sbuf_at("lgT", [4, TLOC], BF16, OFF_M + 4352)
    rc16 = p.sbuf_at("rc16", [128, 16], F32, OFF_M + 8448)
    n2c = p.sbuf_at("n2c", [128, 8], F32, OFF_M + 8512)
    bqc = p.sbuf_at("bqc", [128, 8], F32, OFF_M + 8544)
    bkc = p.sbuf_at("bkc", [128, 8], F32, OFF_M + 8576)
    bvc = p.sbuf_at("bvc", [128, 8], F32, OFF_M + 8608)
    onesb = p.sbuf_at("onesb", [128, 1], BF16, OFF_M + 8640)
    sqs = p.sbuf_at("sqs", [128, 2 * 512], BF16, OFF_M + 8672)
    ssg = p.sbuf_at("ssg", [1, 4 * 512], F32, OFF_M + 12768)
    xrl = p.sbuf_at("xrl", [128, 2 * 1024], F32, OFF_M + 20960)
    ident = p.sbuf_at("id", [128, 128], BF16, OFF_M + 29152)

    psums = [p.psum(f"ps{i}", [128, 512]) for i in range(4)]
    psb16 = [p.psum(f"psb{i}", [128, 512], BF16) for i in range(2)]

    emit_load_x(p, x_in, x_sb)
    dma_block(p, [(ident[:, :], identi)])
    dma_block(p, [(bufA[:, c * 2048:(c + 1) * 2048],
                   ao_in[c * 128:(c + 1) * 128, :]) for c in range(8)], queue="gpsimd")
    dma_block(p, [(n2c[:, :], n2_in), (bqc[:, :], bq_in), (bkc[:, :], bk_in),
                  (bvc[:, :], bv_in)], queue="gpsimd")
    with p.nc.Block() as block:
        @block.vector
        def _(v):
            v.memset(onesb[:, :], 1.0)

    wW = p.sbuf_at("wW", [128, 8 * 1024], BF16, OFF_C)
    ld_wT(p, wW, woT_in, 8, 1024)
    jobs = []
    for f in range(2, 8):
        for m in range(4):
            def mk(f=f, m=m):
                def evac(eng, pa):
                    return eng.copy(chunkT(bufB, f, m * 512, 512), pa)
                return evac
            jobs.append(dict(
                mms=[("m", wW[:, c * 1024 + f * 128:c * 1024 + (f + 1) * 128],
                      bufA[:, c * 2048 + m * 512:c * 2048 + m * 512 + 512])
                     for c in range(8)],
                out_w=512, engine="scalar", evac=mk()))
    mm_grid(p, jobs, psums, tag="wo")
    emit_addT_into_x(p, bufB, x_sb, ident, psb16, skip_chunks=(0, 1))

    if PH < 2:
        emit_store_x(p, x_sb, x_out)
        return p
    emit_rms_cols(p, x_sb, rc16, EPS_RMS)
    with p.nc.Block() as block:
        @block.scalar
        def _(scb):
            for t in range(NT):
                scb.activation(bufB[:, t * D:(t + 1) * D], x_sb[:, t * D:(t + 1) * D],
                               AF.Copy, scale=rc16[:, t:t + 1])
    emit_toT_scaled(p, bufB, bufC, ident, psb16[0], psb16[1],
                    gain_cols=lambda c: n2c[:, c:c + 1])
    gw = p.sbuf_at("gw", [128, 8 * 4], BF16, OFF_M + 29152 + 256)
    ld_wT(p, gw, gwT_in, 8, 4)
    jobs = []
    for m in range(4):
        def mkg(m=m):
            def evac(eng, pa):
                return eng.copy(lgT[0:4, m * 512:(m + 1) * 512], pa)
            return evac
        jobs.append(dict(mms=[("m", gw[:, c * 4:(c + 1) * 4],
                               chunkT(bufC, c, m * 512, 512)) for c in range(8)],
                         out_w=512, rows=4, engine="scalar", evac=mkg()))
    mm_grid(p, jobs, psums[:2], tag="gate2")
    emit_gate_topk(p, lgT, comb, 4, 1e-8, psb16, ident)

    if PH < 3:
        emit_store_x(p, x_sb, x_out)
        return p
    bufA_t = bufA
    stages = []

    def add_stage(loads, jobs, tag, regions=2):
        n = len(jobs)
        if regions == 2:
            stages.append(dict(loads=[(0, loads[0]), (1, loads[1])], jobs=jobs,
                               free={0: n // 2, 1: n},
                               jstart={0: 0, 1: n // 2}, tag=tag))
        else:
            stages.append(dict(loads=[(0, loads[0]), (0, loads[1])], jobs=jobs,
                               free={0: n}, jstart={0: 0}, tag=tag))

    def add_stage2(loads, jA, jB, tag):
        n = len(jA)
        jobs = jA[:n // 2] + jB[:n // 2] + jA[n // 2:] + jB[n // 2:]
        stages.append(dict(loads=[(0, loads[0]), (1, loads[1])], jobs=jobs,
                           free={0: n, 1: 2 * n}, jstart={0: 0, 1: n},
                           tag=tag))

    for e in range(4):
        for m in range(4):
            if e == 0:
                add_stage(wloads_fm(bufA_t, e0u_in, 8, 16),
                          jobs_h_pm(bufA, bufC, bufB, 0, m, AF.Gelu), f"e0h{m}")
            elif e == 1:
                add_stage(wloads_fm(bufA_t, e1a_in, 8, 16),
                          jobs_h_pm(bufA, bufC, bufB, 0, m, AF.Silu), f"e1a{m}")
                add_stage(wloads_fm(bufA_t, e1b_in, 8, 16),
                          jobs_h_pm(bufA, bufC, bufB, 0, m, None, mul_into=True),
                          f"e1b{m}")
            elif e == 2:
                add_stage(wloads_fm(bufA_t, e2a_in, 8, 16),
                          jobs_h_pm(bufA, bufC, bufB, 0, m, AF.Gelu), f"e2a{m}")
                for half2 in range(2):
                    add_stage(wloads_fm(bufA_t, e2b_in, 16, 8, col0=half2 * 1024),
                              jobs_h2_pm(bufA, bufB, 0, 8192 + half2 * 4096,
                                         AF.Gelu), f"e2b{m}_{half2}")
                for half2 in range(2):
                    add_stage(wloads_fm(bufA_t, e2c_in, 16, 8, col0=half2 * 1024),
                              jobs_h2_pm(bufA, bufB, 8192, half2 * 4096,
                                         AF.Gelu), f"e2c{m}_{half2}")
            else:
                add_stage(wloads_fm(bufA_t, e3a_in, 8, 8),
                          jobs_h_pm(bufA, bufC, bufB, 8192, m, AF.Gelu,
                                    fchunks=8), f"e3a{m}", regions=1)
                add_stage(wloads_fm(bufA_t, e3b_in, 8, 16),
                          jobs_h2_pm(bufA, bufB, 8192, 0, AF.Gelu,
                                     fchunks=16, cchunks=8), f"e3b{m}")
            dn = (e0d_in, e1d_in, e2d_in, e3c_in)[e]
            add_stage(wloads_dn(bufA, dn),
                      jobs_dn_pm(bufA, bufB, x_sb, comb, m, e), f"dn{e}_{m}")
    run_wstages(p, stages, psums)

    if PH < 4:
        emit_store_x(p, x_sb, x_out)
        return p
    rl = p.sem("t2rl")
    ld2 = p.sem("t2ld")
    with p.nc.Block() as block:
        @block.sync
        def _(sync):
            for t in range(NT):
                if t >= 2:
                    sync.wait_ge(rl.h, t - 1)
                sl = (t % 2) * 1024
                ld2.inc(sync.dma_start(out=xrl[:, sl:sl + 1024],
                                       in_=x_in[t * 128:(t + 1) * 128, :]), 16)
        @block.vector
        def _(v):
            for t in range(NT):
                v.wait_ge(ld2.h, 16 * (t + 1))
                sl = (t % 2) * 1024
                rl.inc(v.tensor_add(x_sb[:, t * D:(t + 1) * D],
                                    x_sb[:, t * D:(t + 1) * D], xrl[:, sl:sl + 1024]), 1)
    emit_ln(p, x_sb)
    emit_store_x(p, x_sb, x_out)

    if PH < 5:
        return p
    SUB = int(os.environ.get("T2_SUB", "9"))
    emit_bf16_tiles(p, x_sb, bufB)
    emit_to_xT(p, bufB, bufC, ident, psb16[0], psb16[1])
    for w_in, bias, out_ap, ss_ap in ((wq_in, bqc, q_out, qs_out),
                                      (wk_in, bkc, k_out, ks_out))[:max(1, SUB - 0) if SUB >= 2 else 1]:
        ld_wT(p, bufA, w_in, 8, 1024)
        jobs = []
        ji = 0
        for f in range(8):
            for m in range(4):
                sl = ji % 4
                ji += 1
                def mk(f=f, m=m, sl=sl, bias=bias):
                    def evac(eng, pa):
                        eng.tensor_single_scalar(chunkT(bufB, f, m * 512, 512), pa,
                                                 bias[:, f:f + 1], ALU.add)
                        return eng.tensor_copy(stg[:, sl * 512:(sl + 1) * 512],
                                               chunkT(bufB, f, m * 512, 512))
                    return evac
                jobs.append(dict(
                    mms=[("m", bufA[:, c * 1024 + f * 128:c * 1024 + (f + 1) * 128],
                          chunkT(bufC, c, m * 512, 512)) for c in range(8)],
                    out_w=512, engine="vector", evac=mk(),
                    dma=(out_ap[f * 128:(f + 1) * 128, m * 512:(m + 1) * 512],
                         stg[:, sl * 512:(sl + 1) * 512])))
        mm_grid(p, jobs, psums, tag="l2qk", nstage=4)
        if SUB >= 2:
            emit_ssq_1024(p, bufB, sqs, ssg, onesb, ss_ap, psums)
        if SUB < 3:
            return p
    if SUB < 4:
        return p
    ld_wT(p, bufA, wv_in, 8, 1024)
    jobs = []
    for f in range(8):
        for m in range(4):
            def mkv(f=f, m=m):
                def evac(eng, pa):
                    return eng.tensor_single_scalar(chunkT(bufB, f, m * 512, 512),
                                                    pa, bvc[:, f:f + 1], ALU.add)
                return evac
            jobs.append(dict(
                mms=[("m", bufA[:, c * 1024 + f * 128:c * 1024 + (f + 1) * 128],
                      chunkT(bufC, c, m * 512, 512)) for c in range(8)],
                out_w=512, engine="vector", evac=mkv()))
    mm_grid(p, jobs, psums, tag="l2v")
    jobs = []
    ji = 0
    for t in range(NT):
        for c in range(8):
            sl = ji % 4
            ji += 1
            def mkt(t=t, c=c, sl=sl):
                def evac(eng, pa):
                    return eng.tensor_copy(stg[:, sl * 512:sl * 512 + 128], pa)
                return evac
            jobs.append(dict(mms=[("t", chunkT(bufB, c, t * 128, 128), ident[:, :])],
                             out_w=128, engine="vector", evac=mkt(),
                             dma=(v_out[t * 128:(t + 1) * 128, c * 128:(c + 1) * 128],
                                  stg[:, sl * 512:sl * 512 + 128])))
    mm_grid(p, jobs, psb16, tag="l2vt", nstage=4)
    return p


def emit_h_grid(p, wbuf, src, dstbuf, dstoff, m, act, psums, fchunks=16,
                wcols=2048, cchunks=8):
    jobs = []
    for f in range(fchunks):
        def mk(f=f):
            def evac(eng, pa):
                dst = dstbuf[:, dstoff + f * 512:dstoff + (f + 1) * 512]
                if act is None:
                    return eng.copy(dst, pa)
                return eng.activation(dst, pa, act)
            return evac
        jobs.append(dict(
            mms=[("m", wbuf[:, c * wcols + f * 128:c * wcols + (f + 1) * 128],
                  chunkT(src, c, m * 512, 512)) for c in range(cchunks)],
            out_w=512, engine="scalar", evac=mk()))
    mm_grid(p, jobs, psums, tag=f"h{p.uid()}")


def emit_h2_grid(p, wbuf, hbuf, srcoff, dstoff, act, psums, fchunks=8,
                 cchunks=16):
    jobs = []
    for f in range(fchunks):
        def mk(f=f):
            def evac(eng, pa):
                dst = hbuf[:, dstoff + f * 512:dstoff + (f + 1) * 512]
                if act is None:
                    return eng.copy(dst, pa)
                return eng.activation(dst, pa, act)
            return evac
        jobs.append(dict(
            mms=[("m", wbuf[:, c * (fchunks * 128) + f * 128:
                            c * (fchunks * 128) + (f + 1) * 128],
                  hbuf[:, srcoff + c * 512:srcoff + c * 512 + 512])
                 for c in range(cchunks)],
            out_w=512, engine="scalar", evac=mk()))
    mm_grid(p, jobs, psums, tag=f"h2{p.uid()}")


def emit_ssq_1024(p, srcT, sqs, ssg, onesb, out_ap, psums):
    sq2 = p.sem("t2sq")
    sm2 = p.sem("t2sm")
    se2 = p.sem("t2se")
    sd2 = p.sem("t2sd")
    b_sq, b_sm, b_se, b_sd = sq2.n, sm2.n, se2.n, sd2.n
    with p.nc.Block() as block:
        @block.scalar
        def _(s):
            i = 0
            for m in range(4):
                for c in range(8):
                    if i >= 2:
                        s.wait_ge(sm2.h, b_sm + i - 1)
                    sq2.inc(s.activation(sqs[:, (i % 2) * 512:(i % 2 + 1) * 512],
                                         chunkT(srcT, c, m * 512, 512), AF.Square), 1)
                    i += 1
        @block.tensor
        def _(t):
            i = 0
            for m in range(4):
                for c in range(8):
                    t.wait_ge(sq2.h, b_sq + i + 1)
                    if c == 0 and m >= 1:
                        t.wait_ge(se2.h, b_se + m)
                    sm2.inc(t.matmul(psums[2][0:1, 0:512], onesb[:, :],
                                     sqs[:, (i % 2) * 512:(i % 2 + 1) * 512],
                                     start=(c == 0), stop=(c == 7)), 1)
                    i += 1
        @block.vector
        def _(v):
            for m in range(4):
                v.wait_ge(sm2.h, b_sm + 8 * (m + 1))
                if m >= 2:
                    v.wait_ge(sd2.h, b_sd + 16 * (m - 1))
                se2.inc(v.tensor_copy(ssg[:, (m % 2) * 512:(m % 2 + 1) * 512],
                                      psums[2][0:1, 0:512]), 1)
        @block.sync
        def _(sync):
            for m in range(4):
                sync.wait_ge(se2.h, b_se + m + 1)
                sd2.inc(sync.dma_start(out=out_ap[:, m * 512:(m + 1) * 512],
                                       in_=ssg[:, (m % 2) * 512:(m % 2 + 1) * 512]), 16)
            sync.wait_ge(sd2.h, sd2.n)


def prep_T2(inputs, st):
    ident = host_ident128()
    woT = f32c(inputs["L1_ffn_wo"].T)
    n2 = f32c(inputs["L1_ffn_n2_w"].reshape(8, 128).T)
    gwT = f32c(inputs["L1_ffn_gate_w"].T)
    e0u = f32c(inputs["L1_ffn_e0_up"].T)
    e0d = f32c(inputs["L1_ffn_e0_down"].T)
    e1a = f32c(inputs["L1_ffn_e1_w1"].T)
    e1b = f32c(inputs["L1_ffn_e1_w2"].T)
    e1d = f32c(inputs["L1_ffn_e1_down"].T)
    e2a = f32c(inputs["L1_ffn_e2_l1"].T)
    e2b = f32c(inputs["L1_ffn_e2_l2"].T)
    e2c = f32c(inputs["L1_ffn_e2_l3"].T)
    e2d = f32c(inputs["L1_ffn_e2_l4"].T)
    e3a = f32c(inputs["L1_ffn_e3_down"].T)
    e3b = f32c(inputs["L1_ffn_e3_up"].T)
    e3c = f32c(inputs["L1_ffn_e3_out"].T)
    wq = f32c(inputs["L2_attn_Wq"].T)
    wk = f32c(inputs["L2_attn_Wk"].T)
    wv = f32c(inputs["L2_attn_Wv"].T)
    bq = f32c(inputs["L2_attn_bq"].reshape(8, 128).T)
    bk = f32c(inputs["L2_attn_bk"].reshape(8, 128).T)
    bv = f32c(inputs["L2_attn_bv"].reshape(8, 128).T)
    ao = np.zeros((D, B * S), BF)
    for c in range(NC):
        a = st["ao1_units"][c]
        for bi in range(2):
            for hl in range(2):
                h = 2 * c + hl
                ao[64 * h:64 * h + 64, bi * S:(bi + 1) * S] = a[bi, hl]
    maps = []
    for c in range(NC):
        t0 = c * TLOC
        maps.append(dict(
            x=st["x3"][c], aoT=np.ascontiguousarray(ao[:, t0:t0 + TLOC]),
            woT=woT, n2w=n2, gwT=gwT, e0upT=e0u, e0dnT=e0d, e1w1T=e1a,
            e1w2T=e1b, e1dnT=e1d, e2l1T=e2a, e2l2T=e2b, e2l3T=e2c, e2l4T=e2d,
            e3dnT=e3a, e3upT=e3b, e3outT=e3c, wqT2=wq, wkT2=wk, wvT2=wv,
            bq=bq, bk=bk, bv=bv, ident=ident))
    return maps



RT512 = float(np.sqrt(np.float64(512.0)))


def build_T3():
    p = Prog("T3")
    x_in = p.dram_in("x", [TLOC, D]).ap()
    q_in = p.dram_in("q2", [D, TLOC], BF16).ap()
    kf_in = p.dram_in("k2f", [D, S], BF16).ap()
    vf_in = p.dram_in("v2f", [S, D], BF16).ap()
    qs_in = p.dram_in("qss", [1, TLOC]).ap()
    ks_in = p.dram_in("kssf", [1, S]).ap()
    woT_in = p.dram_in("woT2", [D, D]).ap()
    bo_in = p.dram_in("bo", [128, 8]).ap()
    identi = p.dram_in("ident", [128, 128], BF16).ap()
    x_out = p.dram_out("x5", [TLOC, D]).ap()

    p.make_arena(209920)
    x_sb = p.sbuf_at("xsb", [128, NT * D], F32, 0)
    qsb = p.sbuf_at("qsb", [128, 8 * TLOC], BF16, 65536)
    at_tm = p.sbuf_at("at_tm", [128, NT * D], BF16, 98304)
    atT = p.sbuf_at("atT", [128, 8 * TLOC], BF16, 131072)
    wW = p.sbuf_at("wW", [128, 8 * 1024], BF16, 163840)
    ktile = p.sbuf_at("ktile", [128, 4 * 1024], BF16, 180224)
    vtile = p.sbuf_at("vtile", [128, 4 * 1024], BF16, 188416)
    pt = p.sbuf_at("pt", [128, 4 * 256], BF16, 196608)
    rdp = p.sbuf_at("rdp", [1, 256], BF16, 198656)
    rfp = p.sbuf_at("rfp", [1, 256], F32, 199680)
    rcol = p.sbuf_at("rcol", [128, 2], F32, 200704)
    qss_sb = p.sbuf_at("qss", [1, TLOC], BF16, 0)
    kss_sb = p.sbuf_at("kss", [1, S], BF16, 4096)
    csml = p.sbuf_at("csml", [1, 8], F32, 200832)
    cbf = p.sbuf_at("cbf", [1, 1], BF16, 200864)
    biasc = p.sbuf_at("biasc", [128, 1], F32, 200896)
    ones128 = p.sbuf_at("ones128", [1, 128], BF16, 200928)
    onesb = p.sbuf_at("onesb", [128, 1], BF16, 201184)
    bo8 = p.sbuf_at("bo8", [128, 8], F32, 201216)
    ident = p.sbuf_at("id", [128, 128], BF16, 201248)

    ps_s = [p.psum(f"pss{i}", [128, 256]) for i in range(2)]
    ps_o = [p.psum(f"pso{i}", [128, 512]) for i in range(4)]
    ps_d = p.psum("psd", [128, 512])
    psT = p.psum("psT", [128, 512], BF16)

    dma_block(p, [(ident[:, :], identi), (bo8[:, :], bo_in)])
    dma_block(p, [(qss_sb[:, :], qs_in), (kss_sb[:, :], ks_in)], queue="gpsimd")
    dma_block(p, [(qsb[:, c * TLOC:(c + 1) * TLOC], q_in[c * 128:(c + 1) * 128, :])
                  for c in range(8)], queue="gpsimd")
    with p.nc.Block() as block:
        @block.vector
        def _(v):
            v.memset(ones128[:, :], 1.0)
            v.memset(onesb[:, :], 1.0)

    fc = p.sem("t3fc")
    fcb = fc.n
    with p.nc.Block() as block:
        @block.vector
        def _(v):
            fc.inc(v.reduce_max(csml[:, 0:1], kss_sb[:, :], axis=AX), 1)
        @block.scalar
        def _(s):
            s.wait_ge(fc.h, fcb + 1)
            fc.inc(s.activation(csml[:, 3:4], csml[:, 0:1], AF.Sqrt), 1)

    ld = p.sem("t3ld")
    sc = p.sem("t3sc")
    ex = p.sem("t3ex")
    pv = p.sem("t3pv")
    e1 = p.sem("t3e1")
    tr3 = p.sem("t3tr")
    fl = p.sem("t3fl")
    NK = S // 128

    for m in range(8):
        fcb2 = fc.n
        with p.nc.Block() as blk:
            @blk.vector
            def _(v):
                fc.inc(v.reduce_sum(csml[:, 1:2], qss_sb[:, m * 256:(m + 1) * 256],
                                    axis=AX), 1)
                v.wait_ge(fc.h, fcb2 + 2)
                fc.inc(v.tensor_mul(csml[:, 4:5], csml[:, 2:3], csml[:, 3:4]), 1)
                v.wait_ge(fc.h, fcb2 + 3)
                fc.inc(v.tensor_scalar_mul(csml[:, 5:6], csml[:, 4:5],
                                           -1.0 / RT512), 1)
                v.wait_ge(fc.h, fcb2 + 4)
                fc.inc(v.tensor_copy(cbf[:, :], csml[:, 5:6]), 1)
            @blk.scalar
            def _(s):
                s.wait_ge(fc.h, fcb2 + 1)
                fc.inc(s.activation(csml[:, 2:3], csml[:, 1:2], AF.Sqrt,
                                    scale=1.0 / 256.0), 1)
            @blk.tensor
            def _(t):
                t.wait_ge(fc.h, fcb2 + 5)
                fc.inc(t.matmul(ps_d[0:128, 256:257], ones128[0:1, :], cbf[:, :],
                                start=True, stop=True), 1)
        with p.nc.Block() as blk:
            @blk.vector
            def _(v):
                v.tensor_copy(biasc[:, :], ps_d[0:128, 256:257])

        ld_b, sc_b, ex_b, pv_b, e1_b, tr_b = ld.n, sc.n, ex.n, pv.n, e1.n, tr3.n
        ksrc = kf_in.rearrange("(c p) k -> p c k", p=128)
        with p.nc.Block() as blk:
            @blk.sync
            def _(sync):
                for K in range(NK):
                    if K >= 4:
                        sync.wait_ge(pv.h, pv_b + 5 * (K - 3))
                    sl = K % 4
                    ld.inc(sync.dma_start(
                        out=ktile.ap().rearrange("p (s c k) -> p s c k", s=4, c=8)
                        [:, sl], in_=ksrc[:, :, 128 * K:128 * (K + 1)]), 16)
            @blk.gpsimd
            def _(g):
                for K in range(NK):
                    if K >= 4:
                        g.wait_ge(pv.h, pv_b + 5 * (K - 3))
                    sl = K % 4
                    ld.inc(g.dma_start(
                        out=vtile[:, sl * 1024:(sl + 1) * 1024],
                        in_=vf_in[128 * K:(K + 1) * 128, :]), 16)

            @blk.tensor
            def _(tensor):
                def emit_pv(pK):
                    psl = pK % 4
                    tensor.wait_ge(ex.h, ex_b + pK + 1)
                    for half in range(2):
                        for qt in range(2):
                            pv.inc(tensor.matmul(
                                ps_o[2 * qt + half][:, 0:512],
                                pt[:, psl * 256 + qt * 128:psl * 256 + (qt + 1) * 128],
                                vtile[:, psl * 1024 + half * 512:psl * 1024 + (half + 1) * 512],
                                start=(pK == 0), stop=(pK == NK - 1)), 1)
                    pv.inc(tensor.matmul(
                        ps_d[0:1, 0:256], onesb[:, :],
                        pt[:, psl * 256:(psl + 1) * 256],
                        start=(pK == 0), stop=(pK == NK - 1)), 1)

                for K in range(NK):
                    sl = K % 2
                    tensor.wait_ge(ld.h, ld_b + 32 * (K + 1))
                    if K >= 2:
                        tensor.wait_ge(ex.h, ex_b + K - 1)
                    last = None
                    for c in range(8):
                        last = tensor.matmul(
                            ps_s[sl][:, 0:256],
                            ktile[:, (K % 4) * 1024 + c * 128:(K % 4) * 1024 + (c + 1) * 128],
                            chunkT(qsb, c, m * 256, 256),
                            start=(c == 0), stop=(c == 7))
                    sc.inc(last, 1)
                    if K >= 1:
                        emit_pv(K - 1)
                emit_pv(NK - 1)
                tensor.wait_ge(e1.h, e1_b + 1)
                tr3.inc(tensor.transpose(psT[0:128, 0:1], rdp[0:1, 0:128],
                                         ident[0:1, 0:1]), 1)
                tr3.inc(tensor.transpose(psT[0:128, 2:3], rdp[0:1, 128:256],
                                         ident[0:1, 0:1]), 1)

            @blk.scalar
            def _(scalar):
                for K in range(NK):
                    sl = K % 2
                    scalar.wait_ge(sc.h, sc_b + K + 1)
                    if K >= 4:
                        scalar.wait_ge(pv.h, pv_b + 5 * (K - 3))
                    ex.inc(scalar.activation(pt[:, (K % 4) * 256:(K % 4 + 1) * 256],
                                             ps_s[sl][:, 0:256], AF.Exp,
                                             bias=biasc[:, 0:1], scale=1.0 / RT512), 1)
                scalar.wait_ge(pv.h, pv_b + 5 * NK)
                scalar.activation(rfp[:, :], ps_d[0:1, 0:256], AF.Ln)
                e1.inc(scalar.activation(rdp[0:1, 0:256], rfp[:, :], AF.Exp,
                                         scale=-1.0), 1)

            @blk.vector
            def _(v):
                v.wait_ge(tr3.h, tr_b + 2)
                v.tensor_copy(rcol[:, 0:1], psT[0:128, 0:1])
                fl.inc(v.tensor_copy(rcol[:, 1:2], psT[0:128, 2:3]), 1)
                v.wait_ge(fl.h, fl.n)
                for qt in range(2):
                    tok = 2 * m + qt
                    for half in range(2):
                        for cc in range(4):
                            c = half * 4 + cc
                            v.tensor_single_scalar(
                                at_tm[:, tok * D + c * 128:tok * D + (c + 1) * 128],
                                ps_o[2 * qt + half][:, cc * 128:(cc + 1) * 128],
                                rcol[:, qt:qt + 1], ALU.mult)

    import os as _os
    if _os.environ.get("T3_DBG", "0") == "1":
        dbg = p.dram_out("dbg_at", [TLOC, D], BF16).ap()
        dma_block(p, [(dbg.rearrange("(t p) c -> p t c", p=128),
                       at_tm.ap().rearrange("p (t c) -> p t c", t=NT))])
    emit_load_x(p, x_in, x_sb)
    emit_to_xT(p, at_tm, atT, ident, psT, psT)
    ld_wT(p, wW, woT_in, 8, 1024)
    jobs = []
    for f in range(8):
        for mm_ in range(4):
            def mk(f=f, mm_=mm_):
                def evac(eng, pa):
                    return eng.tensor_single_scalar(chunkT(qsb, f, mm_ * 512, 512),
                                                    pa, bo8[:, f:f + 1], ALU.add)
                return evac
            jobs.append(dict(
                mms=[("m", wW[:, c * 1024 + f * 128:c * 1024 + (f + 1) * 128],
                      chunkT(atT, c, mm_ * 512, 512)) for c in range(8)],
                out_w=512, engine="vector", evac=mk()))
    mm_grid(p, jobs, [ps_o[0], ps_o[1]], tag="l2wo")
    emit_addT_into_x(p, qsb, x_sb, ident, [psT])
    emit_ln(p, x_sb)
    emit_store_x(p, x_sb, x_out)
    return p


def prep_T3(inputs, st):
    ident = host_ident128()
    woT = f32c(inputs["L2_attn_Wo"].T)
    bo = f32c(inputs["L2_attn_bo"].reshape(8, 128).T)
    k2f = [np.concatenate([st["k2"][c] for c in range(4 * bq, 4 * bq + 4)], axis=1)
           for bq in range(2)]
    v2f = [np.concatenate([st["v2"][c] for c in range(4 * bq, 4 * bq + 4)], axis=0)
           for bq in range(2)]
    ksf = [np.concatenate([st["kss2"][c] for c in range(4 * bq, 4 * bq + 4)], axis=1)
           for bq in range(2)]
    maps = []
    for c in range(NC):
        bq = c // 4
        maps.append(dict(x=st["x4"][c], q2=st["q2"][c], k2f=k2f[bq], v2f=v2f[bq],
                         qss=st["qss2"][c], kssf=ksf[bq], woT2=woT, bo=bo,
                         ident=ident))
    return maps




def build_T4():
    p = Prog("T4")
    x_in = p.dram_in("x", [TLOC, D]).ap()
    xb_in = p.dram_in("xbound", [128, 8]).ap()
    mur_in = p.dram_in("mur", [128, 8]).ap()
    muk_in = p.dram_in("muk", [128, 8]).ap()
    wrT_in = p.dram_in("wrT", [D, D]).ap()
    wkT_in = p.dram_in("wkT", [D, D]).ap()
    wvT_in = p.dram_in("wvT", [D, D]).ap()
    wq3_in = p.dram_in("wq3", [D, D]).ap()
    wk3_in = p.dram_in("wk3", [D, 256]).ap()
    wv3_in = p.dram_in("wv3", [D, 256]).ap()
    identi = p.dram_in("ident", [128, 128], BF16).ap()

    x_out = p.dram_out("x6", [TLOC, D]).ap()
    q3_out = p.dram_out("q3", [D, TLOC], BF16).ap()
    k3_out = p.dram_out("k3", [256, TLOC], BF16).ap()
    v3_out = p.dram_out("v3", [256, TLOC], BF16).ap()
    qs_out = p.dram_out("qss3", [16, TLOC]).ap()
    ks_out = p.dram_out("kss3", [4, TLOC]).ap()

    p.make_arena(212480)
    x_sb = p.sbuf_at("xsb", [128, NT * D], F32, 0)
    xT = p.sbuf_at("xT", [128, 8 * TLOC], BF16, 65536)
    xsh = p.sbuf_at("xsh", [128, 8 * TLOC], BF16, 98304)
    mx = p.sbuf_at("mx", [128, 8 * TLOC], BF16, 131072)
    wW = p.sbuf_at("wW", [128, 8 * 1024], BF16, 163840)
    rT = p.sbuf_at("rT", [128, 8 * TLOC], BF16, 65536)
    stg = p.sbuf_at("stg", [128, 4 * 512], BF16, 180224)
    mrc = p.sbuf_at("mrc", [128, 8], F32, 184320)
    mrc1 = p.sbuf_at("mrc1", [128, 8], F32, 184352)
    mkc = p.sbuf_at("mkc", [128, 8], F32, 184384)
    mkc1 = p.sbuf_at("mkc1", [128, 8], F32, 184416)
    bcol = p.sbuf_at("bcol", [128, 8], BF16, 184448)
    bd = p.sbuf_at("bd", [128, 2], BF16, 184480)
    sq = p.sbuf_at("sq", [128, 2 * 512], BF16, 184512)
    ssg = p.sbuf_at("ssg", [4, 2 * 512], F32, 186560)
    ident = p.sbuf_at("id", [128, 128], BF16, 190656)
    psums = [p.psum(f"ps{i}", [128, 512]) for i in range(4)]
    psb16 = [p.psum(f"psb{i}", [128, 512], BF16) for i in range(2)]

    emit_load_x(p, x_in, x_sb)
    dma_block(p, [(ident[:, :], identi), (mrc[:, :], mur_in), (mkc[:, :], muk_in)])
    dma_block(p, [(bcol[:, :], xb_in)], queue="gpsimd")
    with p.nc.Block() as block:
        @block.vector
        def _(v):
            v.memset(bd[:, :], 0.0)
            v.memset(bd[0:64, 0:1], 1.0)
            v.memset(bd[64:128, 1:2], 1.0)
            v.tensor_scalar(mrc1[:, :], mrc[:, :], -1.0, 1.0, ALU.mult, ALU.add)
            v.tensor_scalar(mkc1[:, :], mkc[:, :], -1.0, 1.0, ALU.mult, ALU.add)
    emit_bf16_tiles(p, x_sb, xsh)
    emit_to_xT(p, xsh, xT, ident, psb16[0], psb16[1])
    with p.nc.Block() as block:
        @block.vector
        def _(v):
            for c in range(8):
                v.tensor_copy(xsh[:, c * TLOC + 1:(c + 1) * TLOC],
                              xT[:, c * TLOC:(c + 1) * TLOC - 1])
                v.tensor_copy(xsh[:, c * TLOC:c * TLOC + 1], bcol[:, c:c + 1])
    def mix(dst, mc, mc1):
        with p.nc.Block() as block:
            @block.vector
            def _(v):
                for c in range(8):
                    for mm_ in range(4):
                        sl_x = xT[:, c * TLOC + mm_ * 512:c * TLOC + (mm_ + 1) * 512]
                        sl_p = xsh[:, c * TLOC + mm_ * 512:c * TLOC + (mm_ + 1) * 512]
                        dd = dst[:, c * TLOC + mm_ * 512:c * TLOC + (mm_ + 1) * 512]
                        v.tensor_single_scalar(dd, sl_p, mc1[:, c:c + 1], ALU.mult)
                        v.scalar_tensor_tensor(dd, sl_x, mc[:, c:c + 1], dd,
                                               ALU.mult, ALU.add)
    mix(mx, mrc, mrc1)
    mix(xsh, mkc, mkc1)
    ld_wT(p, wW, wrT_in, 8, 1024)
    jobs = []
    for f in range(8):
        for m in range(4):
            def mk(f=f, m=m):
                def evac(eng, pa):
                    return eng.activation(chunkT(rT, f, m * 512, 512), pa, AF.Sigmoid)
                return evac
            jobs.append(dict(
                mms=[("m", wW[:, c * 1024 + f * 128:c * 1024 + (f + 1) * 128],
                      chunkT(mx, c, m * 512, 512)) for c in range(8)],
                out_w=512, engine="scalar", evac=mk()))
    mm_grid(p, jobs, psums, tag="l2fr")
    ld_wT(p, wW, wkT_in, 8, 1024)
    kaT = mx
    jobs = []
    for f in range(8):
        for m in range(4):
            def mk(f=f, m=m):
                def evac(eng, pa):
                    return eng.activation(chunkT(kaT, f, m * 512, 512), pa, AF.Relu)
                return evac
            jobs.append(dict(
                mms=[("m", wW[:, c * 1024 + f * 128:c * 1024 + (f + 1) * 128],
                      chunkT(xsh, c, m * 512, 512)) for c in range(8)],
                out_w=512, engine="scalar", evac=mk()))
    mm_grid(p, jobs, psums, tag="l2fk")
    with p.nc.Block() as block:
        @block.scalar
        def _(s):
            for c in range(8):
                for m in range(4):
                    sl = kaT[:, c * TLOC + m * 512:c * TLOC + (m + 1) * 512]
                    s.activation(sl, sl, AF.Square)
    ld_wT(p, wW, wvT_in, 8, 1024)
    jobs = []
    for f in range(8):
        for m in range(4):
            def mk(f=f, m=m):
                def evac(eng, pa):
                    return eng.tensor_tensor(chunkT(xsh, f, m * 512, 512),
                                             chunkT(rT, f, m * 512, 512), pa,
                                             ALU.mult)
                return evac
            jobs.append(dict(
                mms=[("m", wW[:, c * 1024 + f * 128:c * 1024 + (f + 1) * 128],
                      chunkT(kaT, c, m * 512, 512)) for c in range(8)],
                out_w=512, engine="vector", evac=mk()))
    mm_grid(p, jobs, psums, tag="l2fv")
    emit_addT_into_x(p, xsh, x_sb, ident, psb16)
    emit_ln(p, x_sb)
    emit_store_x(p, x_sb, x_out)

    emit_bf16_tiles(p, x_sb, xsh)
    emit_to_xT(p, xsh, xT, ident, psb16[0], psb16[1])

    def ld_w(wap, nchunk, ncols):
        dma_block(p, [(wW[:, c * ncols:(c + 1) * ncols],
                       wap[c * 128:(c + 1) * 128, :]) for c in range(nchunk)],
                  queue="gpsimd")

    for wap, fch, out_ap in ((wq3_in, 8, q3_out), (wk3_in, 2, k3_out),
                             (wv3_in, 2, v3_out)):
        ld_w(wap, 8, fch * 128)
        jobs = []
        ji = 0
        for f in range(fch):
            for m in range(4):
                sl = ji % 4
                ji += 1
                def mk(f=f, m=m, sl=sl, out_ap=out_ap, fch=fch):
                    def evac(eng, pa):
                        eng.copy(chunkT(mx, f, m * 512, 512), pa)
                        return eng.copy(stg[:, sl * 512:(sl + 1) * 512],
                                        chunkT(mx, f, m * 512, 512))
                    return evac
                jobs.append(dict(
                    mms=[("m", wW[:, c * fch * 128 + f * 128:
                                  c * fch * 128 + (f + 1) * 128],
                          chunkT(xT, c, m * 512, 512)) for c in range(8)],
                    out_w=512, engine="scalar", evac=mk(),
                    dma=(out_ap[f * 128:(f + 1) * 128, m * 512:(m + 1) * 512],
                         stg[:, sl * 512:(sl + 1) * 512])))
        mm_grid(p, jobs, psums, tag=f"l3p{fch}{p.uid()}", nstage=4)
        nheads2 = fch
        ss_ap = qs_out if out_ap is q3_out else (ks_out if out_ap is k3_out else None)
        if ss_ap is not None:
            emit_ssq_heads(p, mx, sq, ssg, bd, ss_ap, psums, fch)
    return p


def emit_ssq_heads(p, srcT, sq, ssg, bd, out_ap, psums, fch):
    a = p.sem("t4a")
    b = p.sem("t4b")
    c_ = p.sem("t4c")
    d_ = p.sem("t4d")
    ab, bb, cb, db = a.n, b.n, c_.n, d_.n
    with p.nc.Block() as block:
        @block.scalar
        def _(s):
            i = 0
            for ch in range(fch):
                for m in range(4):
                    if i >= 2:
                        s.wait_ge(b.h, bb + i - 1)
                    a.inc(s.activation(sq[:, (i % 2) * 512:(i % 2 + 1) * 512],
                                       chunkT(srcT, ch, m * 512, 512), AF.Square), 1)
                    i += 1
        @block.tensor
        def _(t):
            i = 0
            for ch in range(fch):
                for m in range(4):
                    t.wait_ge(a.h, ab + i + 1)
                    if i >= 1:
                        t.wait_ge(c_.h, cb + i)
                    b.inc(t.matmul(psums[2][0:2, 0:512], bd[:, 0:2],
                                   sq[:, (i % 2) * 512:(i % 2 + 1) * 512],
                                   start=True, stop=True), 1)
                    i += 1
        @block.vector
        def _(v):
            i = 0
            for ch in range(fch):
                for m in range(4):
                    v.wait_ge(b.h, bb + i + 1)
                    if i >= 2:
                        v.wait_ge(d_.h, db + 16 * (i - 1))
                    c_.inc(v.tensor_copy(ssg[0:2, (i % 2) * 512:(i % 2 + 1) * 512],
                                         psums[2][0:2, 0:512]), 1)
                    i += 1
        @block.sync
        def _(sync):
            i = 0
            for ch in range(fch):
                for m in range(4):
                    sync.wait_ge(c_.h, cb + i + 1)
                    d_.inc(sync.dma_start(
                        out=out_ap[2 * ch:2 * ch + 2, m * 512:(m + 1) * 512],
                        in_=ssg[0:2, (i % 2) * 512:(i % 2 + 1) * 512]), 16)
                    i += 1
            sync.wait_ge(d_.h, d_.n)


def prep_T4(inputs, st):
    ident = host_ident128()
    maps = []
    wrT = f32c(inputs["L2_ffn_W_r"].T)
    wkT = f32c(inputs["L2_ffn_W_k"].T)
    wvT = f32c(inputs["L2_ffn_W_v"].T)
    mur = f32c(inputs["L2_ffn_mu_r"].reshape(8, 128).T)
    muk = f32c(inputs["L2_ffn_mu_k"].reshape(8, 128).T)
    wq3 = f32c(inputs["L3_attn_Wq"])
    wk3 = f32c(inputs["L3_attn_Wk"])
    wv3 = f32c(inputs["L3_attn_Wv"])
    for c in range(NC):
        if c % 4 == 0:
            xb = np.zeros((128, 8), np.float32)
        else:
            xb = f32c(st["x5"][c - 1][TLOC - 1].reshape(8, 128).T)
        maps.append(dict(x=st["x5"][c], xbound=xb, mur=mur, muk=muk, wrT=wrT,
                         wkT=wkT, wvT=wvT, wq3=wq3, wk3=wk3, wv3=wv3,
                         ident=ident))
    return maps




def build_F3():
    p = Prog("F3")
    q_in = p.dram_in("q", [2, 2, 64, S], BF16).ap()
    k_in = p.dram_in("k", [2, 64, S], BF16).ap()
    v_in = p.dram_in("v", [2, S, 64], BF16).ap()
    qs_in = p.dram_in("qss", [2, 2, S]).ap()
    ks_in = p.dram_in("kss", [2, S]).ap()
    m_in = p.dram_in("masks", [4, 128, 512], BF16).ap()
    ao_o = p.dram_out("ao", [2, 2, 64, S], BF16).ap()

    q_sb = [p.sbuf(f"q{i}", [96, S], BF16) for i in range(2)]
    k_sb = [p.sbuf(f"k{i}", [96, S], BF16) for i in range(2)]
    vh_sb = [p.sbuf(f"vh{i}", [128, 64 * 65], BF16) for i in range(2)]
    kssb = p.sbuf("kssb", [1, S], BF16)
    qssb = p.sbuf("qssb", [1, S], BF16)
    msk = p.sbuf("msk", [128, 4 * 512], BF16)
    ones64 = p.sbuf("ones64", [1, 64], BF16)
    ones128 = p.sbuf("ones128", [1, 128], BF16)
    csml = p.sbuf("csml", [1, 144], F32)
    crowb = p.sbuf("crowb", [1, 64], BF16)
    biasm = p.sbuf("biasm", [128, 64], F32)
    pt = [p.sbuf(f"pt{i}", [128, 512], BF16) for i in range(8)]
    stg = [dict(rd=p.sbuf(f"rd{t}", [1, 2 * 512], BF16),
                rf=p.sbuf(f"rf{t}", [1, 512], F32),
                num=p.sbuf(f"num{t}", [64, 2 * 512], BF16),
                ao=p.sbuf(f"aos{t}", [64, 2 * 512], BF16)) for t in range(2)]
    ps_s = [[p.psum(f"pss{t}{i}", [128, 512]) for i in range(2)] for t in range(2)]
    ps_o = [p.psum(f"pso{t}", [65, 512]) for t in range(2)]
    ps_b = [p.psum(f"psb{t}", [128, 512]) for t in range(2)]

    units = [(b, h) for b in range(2) for h in range(2)]
    lds = [p.sem("fldA"), p.sem("fldB")]
    ms = p.sem("fms")
    kl = p.sem("fkl")
    fcc = p.sem("fcs")
    fs2 = p.sem("fsq")
    fq = p.sem("fqr")
    fsq = p.sem("fq2")
    fvx = p.sem("fvx")
    fm2 = p.sem("fm2")
    fv2 = p.sem("fv2")
    dma_block(p, [(msk[:, 512 * r:512 * (r + 1)], m_in[r]) for r in range(4)])

    with p.nc.Block() as block:
        @block.vector
        def _(v):
            v.memset(ones64[:, :], 1.0)
            v.memset(ones128[:, :], 1.0)
            for t in range(2):
                v.memset(q_sb[t][64:96, :], 0.0)
                v.memset(k_sb[t][64:96, :], 0.0)
            for t in range(2):
                ms.inc(v.memset(vh_sb[t][:, :], 1.0), 1)

    kb = kl.n
    fb = fcc.n
    with p.nc.Block() as block:
        @block.gpsimd
        def _(g):
            for b in range(2):
                if b == 1:
                    g.wait_ge(fcc.h, fb + 1)
                kl.inc(g.dma_start(out=kssb[:, :], in_=ks_in[b:b + 1, :]), 16)

        @block.vector
        def _(v):
            for b in range(2):
                v.wait_ge(kl.h, kb + 16 * (b + 1))
                fcc.inc(v.reduce_max(csml[:, b:b + 1], kssb[:, :], axis=AX), 1)

        @block.scalar
        def _(s):
            for b in range(2):
                s.wait_ge(fcc.h, fb + b + 1)
                fs2.inc(s.activation(csml[:, 2 + b:3 + b], csml[:, b:b + 1],
                                     AF.Sqrt, scale=1.0 / 64.0), 1)

    with p.nc.Block() as block:
        @block.vector
        def _(v):
            v.wait_ge(fs2.h, 2)
            for b in range(2):
                v.tensor_scalar_mul(csml[:, 4 + b:5 + b], csml[:, 2 + b:3 + b],
                                    -1.0)

    qb = kl.n
    with p.nc.Block() as block:
        @block.gpsimd
        def _(g):
            for u, (b, h) in enumerate(units):
                if u >= 1:
                    g.wait_ge(fq.h, u)
                kl.inc(g.dma_start(out=qssb[:, :], in_=qs_in[b:b + 1, h, :]), 16)

        @block.vector
        def _(v):
            for u in range(4):
                v.wait_ge(kl.h, qb + 16 * (u + 1))
                qv = bass.AP(qssb, 0, [[S, 1], [512, 16], [1, 512]])
                fq.inc(v.reduce_sum(csml[:, 8 + 16 * u:24 + 16 * u], qv, axis=AX), 1)

        @block.scalar
        def _(s):
            for u in range(4):
                s.wait_ge(fq.h, u + 1)
                fsq.inc(s.activation(csml[:, 72 + 16 * u:88 + 16 * u],
                                     csml[:, 8 + 16 * u:24 + 16 * u],
                                     AF.Sqrt, scale=1.0 / 512.0), 1)

    with p.nc.Block() as block:
        @block.vector
        def _(v):
            v.wait_ge(fsq.h, 4)
            for u, (b, h) in enumerate(units):
                fvx.inc(v.tensor_single_scalar(
                    crowb[:, 16 * u:16 * u + 16],
                    csml[:, 72 + 16 * u:88 + 16 * u],
                    csml[:, 4 + b:5 + b], ALU.mult), 1)

    with p.nc.Block() as block:
        @block.tensor
        def _(t):
            for u in range(4):
                if u >= 2:
                    t.wait_ge(fv2.h, u - 1)
                fm2.inc(t.matmul(ps_b[u % 2][0:128, 0:16], ones128[0:1, :],
                                 crowb[:, 16 * u:16 * u + 16],
                                 start=True, stop=True), 1)

        @block.vector
        def _(v):
            for u in range(4):
                v.wait_ge(fm2.h, u + 1)
                fv2.inc(v.tensor_copy(biasm[:, 16 * u:16 * u + 16],
                                      ps_b[u % 2][0:128, 0:16]), 1)

    with p.nc.Block() as block:
        @block.sync
        def _(sync):
            for t in range(2):
                lds[0].inc(sync.dma_start(out=k_sb[t][0:64, :], in_=k_in[t]), 16)
            for u in (0, 1):
                b, h = units[u]
                lds[0].inc(sync.dma_start(out=q_sb[u][0:64, :], in_=q_in[b, h]), 16)

        @block.gpsimd
        def _(g):
            g.wait_ge(ms.h, 2)
            for t in range(2):
                dst = vh_sb[t].ap().rearrange("p (k c) -> p k c", c=65)[:, :, 0:64]
                src = v_in[t].rearrange("(k p) c -> p k c", p=128)
                lds[1].inc(g.dma_start(out=dst, in_=src), 16)

    for pi in range(2):
        if pi == 1:
            with p.nc.Block() as block:
                @block.sync
                def _(sync):
                    for u in (2, 3):
                        b, h = units[u]
                        lds[0].inc(sync.dma_start(out=q_sb[u % 2][0:64, :],
                                                  in_=q_in[b, h]), 16)
                    sync.wait_ge(lds[0].h, lds[0].n)
        US = []
        for t in range(2):
            u = 2 * pi + t
            b, h = units[u]
            US.append(dict(
                dq=96, q=q_sb[t], k=k_sb[b], vhat=vh_sb[b],
                pt=pt[4 * t:4 * t + 4], ps_s=ps_s[t], ps_o=ps_o[t],
                ps_b=ps_b[t],
                bias=(lambda m, u=u: biasm[:, 16 * u + m:16 * u + m + 1]),
                ao_out=lambda m, b=b, h=h: ao_o[b, h, :, 512 * m:512 * (m + 1)],
                sems=flash_sems(p, t), **stg[t]))
        emit_flash_pair(p, US, msk=msk, ones64=ones64, exp_scale=0.125,
                        clip=None,
                        wait_lds=[(lds[0], lds[0].n), (lds[1], 32)])
    return p


def prep_F3(st):
    maps = []
    masks = host_band_masks()
    q3, k3, v3, qss, kss = st["q3"], st["k3"], st["v3"], st["qss3"], st["kss3"]
    for c in range(NC):
        kv = c // 2
        q = np.zeros((2, 2, 64, S), BF)
        k = np.zeros((2, 64, S), BF)
        vv = np.zeros((2, S, 64), BF)
        qs = np.zeros((2, 2, S), np.float32)
        ks = np.zeros((2, S), np.float32)
        for bi in range(2):
            k[bi] = k3[64 * kv:64 * kv + 64, bi * S:(bi + 1) * S]
            vv[bi] = v3[64 * kv:64 * kv + 64, bi * S:(bi + 1) * S].T
            ks[bi] = kss[kv, bi * S:(bi + 1) * S]
            for hl in range(2):
                h = 2 * c + hl
                q[bi, hl] = q3[64 * h:64 * h + 64, bi * S:(bi + 1) * S]
                qs[bi, hl] = qss[h, bi * S:(bi + 1) * S]
        maps.append(dict(q=q, k=k, v=vv, qss=qs, kss=ks, masks=masks))
    return maps




def build_T5():
    import os
    PH5 = int(os.environ.get("T5_PHASES", "9"))
    p = Prog("T5")
    x_in = p.dram_in("x", [TLOC, D]).ap()
    ao_in = p.dram_in("aoT", [D, TLOC], BF16).ap()
    wo3_in = p.dram_in("wo3", [D, D]).ap()
    kern_in = p.dram_in("kern", [D, D]).ap()
    b3_in = p.dram_in("bias3", [128, 8]).ap()
    a3_in = p.dram_in("a3", [D, 16]).ap()
    bl_in = p.dram_in("b3lora", [16, D]).ap()
    identi = p.dram_in("ident", [128, 128], BF16).ap()
    identfi = p.dram_in("identf", [128, 128]).ap()
    x_out = p.dram_out("x8", [TLOC, D]).ap()

    p.make_arena(204800)
    x_sb = p.sbuf_at("xsb", [128, NT * D], F32, 0)
    bufA = p.sbuf_at("bufA", [128, 8 * TLOC], BF16, 65536)
    bufB = p.sbuf_at("bufB", [128, 8 * TLOC], BF16, 98304)
    bufC = p.sbuf_at("bufC", [128, 8 * TLOC], BF16, 131072)
    wW = p.sbuf_at("wW", [128, 8 * 1024], BF16, 163840)
    qf = p.sbuf_at("qf", [128, 512], F32, 180224)
    qi = p.sbuf_at("qi", [128, 512], I32, 182272)
    mc8 = p.sbuf_at("mc8", [128, 16], F32, 184320)
    row1 = p.sbuf_at("row1", [1, 128], F32, 184416)
    g11 = p.sbuf_at("g11", [1, 8], F32, 184928)
    onesf = p.sbuf_at("onesf", [1, 128], F32, 184960)
    bsc = p.sbuf_at("bsc", [128, 2], F32, 185536)

    la = p.sbuf_at("la", [16, TLOC], BF16, 185600)
    b3l = p.sbuf_at("b3l", [16, D], BF16, 189696)
    bi8 = p.sbuf_at("bi8", [128, 8], F32, 191744)
    ones128 = p.sbuf_at("ones128", [1, 128], BF16, 191776)
    ident = p.sbuf_at("id", [128, 128], BF16, 192032)
    identf = p.sbuf_at("identf", [128, 128], F32, 192288)
    psums = [p.psum(f"ps{i}", [128, 512]) for i in range(4)]
    psb16 = [p.psum(f"psb{i}", [128, 512], BF16) for i in range(2)]

    emit_load_x(p, x_in, x_sb)
    dma_block(p, [(ident[:, :], identi), (bi8[:, :], b3_in)])
    dma_block(p, [(identf[:, :], identfi)])
    dma_block(p, [(bufA[:, c * 2048:(c + 1) * 2048],
                   ao_in[c * 128:(c + 1) * 128, :]) for c in range(8)]
              + [(b3l[:, :], bl_in)], queue="gpsimd")
    with p.nc.Block() as block:
        @block.vector
        def _(v):
            v.memset(ones128[:, :], 1.0)
            v.memset(onesf[:, :], 1.0)
    ld_wT(p, wW, wo3_in, 8, 1024)
    jobs = []
    for f in range(8):
        for m in range(4):
            def mk(f=f, m=m):
                def evac(eng, pa):
                    return eng.copy(chunkT(bufB, f, m * 512, 512), pa)
                return evac
            jobs.append(dict(
                mms=[("m", wW[:, c * 1024 + f * 128:c * 1024 + (f + 1) * 128],
                      bufA[:, c * 2048 + m * 512:c * 2048 + (m + 1) * 512])
                     for c in range(8)],
                out_w=512, engine="scalar", evac=mk()))
    mm_grid(p, jobs, psums, tag="wo3")
    emit_addT_into_x(p, bufB, x_sb, ident, psb16)
    emit_ln(p, x_sb)
    if PH5 < 2:
        emit_store_x(p, x_sb, x_out)
        return p
    emit_bf16_tiles(p, x_sb, bufB)
    emit_to_xT(p, bufB, bufC, ident, psb16[0], psb16[1])

    kernf = p.sbuf_at("kernf", [128, 8 * 1024], F32, 98304)
    dma_block(p, [(kernf[:, c * 1024:(c + 1) * 1024],
                   kern_in[c * 128:(c + 1) * 128, :]) for c in range(8)])
    fq = p.sem("t5fq")
    fqb = fq.n
    with p.nc.Block() as block:
        @block.vector
        def _(v):
            def flush(ins):
                fq.inc(ins, 1)
                v.wait_ge(fq.h, fq.n)
            for c in range(7):
                v.reduce_max(mc8[:, c:c + 1], kernf[:, c * 1024:(c + 1) * 1024],
                             axis=AX, apply_absolute_value=True)
            flush(v.reduce_max(mc8[:, 7:8], kernf[:, 7 * 1024:8 * 1024],
                               axis=AX, apply_absolute_value=True))
            flush(v.reduce_max(mc8[:, 8:9], mc8[:, 0:8], axis=AX))
            fq.inc(v.engine_nop(), 1)
        @block.tensor
        def _(t):
            t.wait_ge(fq.h, fqb + 3)
            fq.inc(t.transpose(psums[0][0:1, 0:128], mc8[:, 8:9], identf[:, :]), 1)
        @block.scalar
        def _(s):
            pass
    with p.nc.Block() as block:
        @block.vector
        def _(v):
            def flush(ins):
                fq.inc(ins, 1)
                v.wait_ge(fq.h, fq.n)
            flush(v.tensor_copy(row1[:, :], psums[0][0:1, 0:128]))
            flush(v.reduce_max(g11[:, 0:1], row1[:, :], axis=AX))
            v.wait_ge(fq.h, fq.n + 2)
            flush(v.tensor_single_scalar(g11[:, 3:4], g11[:, 0:1], 1.0 / 7.0,
                                         ALU.mult))
        @block.scalar
        def _(s):
            s.wait_ge(fq.h, fqb + 6)
            fq.inc(s.activation(g11[:, 1:2], g11[:, 0:1], AF.Ln, scale=1.0 / 7.0), 1)
            fq.inc(s.activation(g11[:, 2:3], g11[:, 1:2], AF.Exp, scale=-1.0), 1)
    with p.nc.Block() as block:
        @block.tensor
        def _(t):
            t.matmul(psums[0][0:128, 0:2], onesf[0:1, :], g11[:, 2:4],
                     start=True, stop=True)
        @block.vector
        def _(v):
            pass
    with p.nc.Block() as block:
        @block.vector
        def _(v):
            v.tensor_copy(bsc[:, 1:2], psums[0][0:128, 0:1])
            v.tensor_copy(bsc[:, 0:1], psums[0][0:128, 1:2])
    with p.nc.Block() as block:
        @block.vector
        def _(v):
            def flush(ins):
                fq.inc(ins, 1)
                v.wait_ge(fq.h, fq.n)
            for i in range(16):
                srcf = kernf[:, i * 512:(i + 1) * 512]
                dst = bufA[:, i * 512:(i + 1) * 512]
                flush(v.tensor_single_scalar(qf[:, :], srcf, bsc[:, 1:2], ALU.mult))
                flush(v.tensor_copy(qi[:, :], qf[:, :]))
                flush(v.tensor_copy(qf[:, :], qi[:, :]))
                flush(v.tensor_scalar(qf[:, :], qf[:, :], 7.0, -8.0, ALU.min,
                                      ALU.max))
                flush(v.tensor_single_scalar(dst, qf[:, :], bsc[:, 0:1], ALU.mult))

    if os.environ.get("T5_DBG", "0") == "1":
        dbgw = p.dram_out("dbg_wq", [D, D], BF16).ap()
        dma_block(p, [(dbgw[c * 128:(c + 1) * 128, :], bufA[:, c * 1024:(c + 1) * 1024])
                      for c in range(8)])
    ld_wT(p, wW, a3_in, 8, 16)
    jobs = []
    for m in range(4):
        def mka(m=m):
            def evac(eng, pa):
                return eng.copy(la[0:16, m * 512:(m + 1) * 512], pa)
            return evac
        jobs.append(dict(mms=[("m", wW[:, c * 16:(c + 1) * 16],
                               chunkT(bufC, c, m * 512, 512)) for c in range(8)],
                         out_w=512, rows=16, engine="scalar", evac=mka()))
    mm_grid(p, jobs, psums[:2], tag="lora")
    jobs = []
    for f in range(8):
        for m in range(4):
            def mk(f=f, m=m):
                def evac(eng, pa):
                    return eng.tensor_single_scalar(chunkT(bufB, f, m * 512, 512),
                                                    pa, bi8[:, f:f + 1], ALU.add)
                return evac
            mms = [("m", bufA[:, c * 1024 + f * 128:c * 1024 + (f + 1) * 128],
                    chunkT(bufC, c, m * 512, 512)) for c in range(8)]
            mms.append(("m", b3l[0:16, f * 128:(f + 1) * 128],
                        la[0:16, m * 512:(m + 1) * 512]))
            jobs.append(dict(mms=mms, out_w=512, engine="vector", evac=mk()))
    mm_grid(p, jobs, psums, tag="l3f")
    emit_addT_into_x(p, bufB, x_sb, ident, psb16)
    emit_ln(p, x_sb)
    emit_store_x(p, x_sb, x_out)
    return p


def prep_T5(inputs, st):
    ident = host_ident128()
    wo3 = f32c(inputs["L3_attn_Wo"])
    kern = f32c(inputs["L3_ffn_kernel"])
    b3 = f32c(inputs["L3_ffn_bias"].reshape(8, 128).T)
    a3 = f32c(inputs["L3_ffn_a"])
    bl = f32c(inputs["L3_ffn_b"])
    ao = np.zeros((D, B * S), BF)
    for c in range(NC):
        a = st["ao3_units"][c]
        for bi in range(2):
            for hl in range(2):
                h = 2 * c + hl
                ao[64 * h:64 * h + 64, bi * S:(bi + 1) * S] = a[bi, hl]
    maps = []
    for c in range(NC):
        t0 = c * TLOC
        maps.append(dict(x=st["x6"][c], aoT=np.ascontiguousarray(ao[:, t0:t0 + TLOC]),
                         wo3=wo3, kern=kern, bias3=b3, a3=a3, b3lora=bl,
                         ident=ident, identf=np.eye(128, dtype=np.float32)))
    return maps
